# version 01
# baseline (speedup 1.0000x reference)
import numpy as np
import ml_dtypes
import concourse.bass as bass
import concourse.mybir as mybir
from concourse.alu_op_type import AluOpType as Op
from concourse.bass_utils import run_bass_kernel_spmd
from concourse import library_config
from concourse.library_overlay import lower_extended_insts

F32 = mybir.dt.float32
BF16 = mybir.dt.bfloat16
I16 = mybir.dt.int16
U16 = mybir.dt.uint16
AF = mybir.ActivationFunctionType
nbf16 = ml_dtypes.bfloat16

T = 2048
DM = 1024
NCORE = 8
NH = 16
DH = 64
HID0 = 4096
WIN = 256
SD = 128
EXP = 16
CAP = 256
FSH = 2048
MLPH = 4096

ENG = {'te': 'tensor', 've': 'vector', 'se': 'scalar', 'gp': 'gpsimd', 'sy': 'sync'}


def _ap_span(ap):
    try:
        t = ap.tensor
        name = t.name
        off = ap.offset
        ext = 1
        for step, cnt in ap.ap:
            ext += abs(step) * (cnt - 1)
        sz = mybir.dt.size(ap.dtype)
        return (name, off * sz, (off + ext) * sz)
    except Exception:
        return None


class _EngProxy:
    _OPS = {
        'tensor_tensor', 'tensor_scalar', 'scalar_tensor_tensor', 'tensor_copy',
        'tensor_reduce', 'reduce_max', 'reduce_sum', 'tensor_add', 'tensor_sub',
        'tensor_mul', 'tensor_max', 'reciprocal', 'tensor_tensor_scan', 'select',
        'copy_predicated', 'memset', 'max', 'max_index', 'match_replace',
        'activation', 'add', 'mul', 'copy', 'sign', 'sqrt', 'square', 'iota',
        'transpose', 'affine_select', 'tensor_scalar_add', 'tensor_scalar_mul',
        'tensor_scalar_max', 'tensor_scalar_min', 'bn_stats', 'bn_aggr',
    }

    def __init__(self, B, eng, e):
        self._B = B
        self._eng = eng
        self._e = e

    def __getattr__(self, name):
        e = self._e
        f = getattr(e, name)
        if name not in self._OPS:
            return f
        B = self._B
        eng = self._eng

        def wrap(*args, **kw):
            spans = []
            for a in list(args) + list(kw.values()):
                if isinstance(a, bass.AP):
                    sp = _ap_span(a)
                    if sp:
                        spans.append(sp)
            lw = B.last_writes.get(eng, [])
            hit = False
            for (n1, l1, h1) in spans[1:] or spans:
                for (n2, l2, h2) in lw:
                    if n1 == n2 and l1 < h2 and l2 < h1:
                        hit = True
            if hit:
                if B.c[eng] > B.waited.get((eng, eng + '#self'), 0):
                    e.wait_ge(B.S[eng], B.c[eng])
                    B.waited[(eng, eng + '#self')] = B.c[eng]
                B.last_writes[eng] = []
            ins = f(*args, **kw)
            out = None
            for a in list(args) + list(kw.values()):
                if isinstance(a, bass.AP):
                    out = a
                    break
            if out is not None:
                sp = _ap_span(out)
                if sp:
                    lw = B.last_writes.setdefault(eng, [])
                    lw.append(sp)
                    if len(lw) > 2:
                        lw.pop(0)
            try:
                ins.then_inc(B.S[eng], 1)
                B.c[eng] += 1
            except Exception:
                pass
            return _NoInc(ins)
        return wrap


class _NoInc:
    def __init__(self, ins):
        self.ins = ins

    def then_inc(self, sem, n):
        self.ins.then_inc(sem, n)
        return self

POOL_N = 46720
O_A = 0
O_B = 16384
O_C = 33024
O_D = 41216
N_A = 16384
N_B = 16640
N_C = 8192
N_D = 5504


class Builder:
    def __init__(self):
        self.nc = bass.Bass("TRN2", target_bir_lowering=False)
        self.c = {}
        self.waited = {}
        self.S = {}
        self.ctx = []
        self.t = {}
        self.last_writes = {}

    def sem(self, name):
        cm = self.nc.semaphore(name)
        h = cm.__enter__()
        self.ctx.append(cm)
        self.S[name] = h
        self.c[name] = 0
        return h

    def inc(self, instr, s, n=1):
        if isinstance(instr, _NoInc):
            if s in ('ve', 'se', 'gp') and self.S[s] is not None:
                return (s, self.c[s])
            instr.ins.then_inc(self.S[s], n)
            self.c[s] += n
            return (s, self.c[s])
        instr.then_inc(self.S[s], n)
        self.c[s] += n
        return (s, self.c[s])

    def tok(self, s):
        return (s, self.c[s])

    def wait(self, e, eng, *tokens):
        for tkn in tokens:
            if tkn is None:
                continue
            s, v = tkn
            if s == eng:
                continue
            k = (eng, s)
            if v > self.waited.get(k, 0):
                e.wait_ge(self.S[s], v)
                self.waited[k] = v

    def on(self, eng):
        dec = getattr(self.block, ENG[eng])
        if eng in ('ve', 'se', 'gp'):
            B = self

            def wrapped(fn):
                return dec(lambda e: fn(_EngProxy(B, eng, e)))
            return wrapped
        return dec

    def bar(self, e, eng):
        v = self.c[eng]
        if v > self.waited.get((eng, eng + '#self'), 0):
            e.wait_ge(self.S[eng], v)
            self.waited[(eng, eng + '#self')] = v

    def dram(self, name, shape, dtype, kind="Internal"):
        h = self.nc.dram_tensor(name, list(shape), dtype, kind=kind)
        self.t[name] = h
        return h.ap()

    def sbuf(self, name, shape, dtype):
        cm = self.nc.sbuf_tensor(name, list(shape), dtype)
        h = cm.__enter__()
        self.ctx.append(cm)
        self.t[name] = h
        return h

    def psum(self, name, shape, dtype):
        cm = self.nc.psum_tensor(name, list(shape), dtype)
        h = cm.__enter__()
        self.ctx.append(cm)
        self.t[name] = h
        return h



def _bf(x):
    return np.ascontiguousarray(np.asarray(x).astype(nbf16))


def _f32(x):
    return np.ascontiguousarray(np.asarray(x, dtype=np.float32))


def _vec_pc(v, chunks):
    return _f32(np.asarray(v, np.float32).reshape(chunks, 128).T)


def host_prepare(w):
    d = {}
    d['l0_wqT'] = _bf(w['L0_attn_Wq'].T)
    d['l0_wkT'] = _bf(w['L0_attn_Wk'].T)
    d['l0_wvT'] = _bf(w['L0_attn_Wv'].T)
    d['l0_woT'] = _bf(w['L0_attn_Wo'].T)
    d['w1T_f32'] = _f32(np.asarray(w['L0_ffn_w1']).T)
    d['w2T_f32'] = _f32(np.asarray(w['L0_ffn_w2']).T)
    cw = np.asarray(w['L0_ffn_conv_w'], np.float32).reshape(HID0, 3)
    d['convw'] = _f32(cw.reshape(32, 128, 3).transpose(1, 0, 2).reshape(128, 96))
    d['convb'] = _vec_pc(w['L0_ffn_conv_b'], 32)
    d['ln1g'] = _vec_pc(w['L1_attn_ln1_g'], 8)
    d['ln1b'] = _vec_pc(w['L1_attn_ln1_b'], 8)
    d['ln2g'] = _vec_pc(w['L1_attn_ln2_g'], 8)
    d['ln2b'] = _vec_pc(w['L1_attn_ln2_b'], 8)
    d['wqkvT'] = _bf(w['L1_attn_W_qkv'])
    d['l1_woT'] = _bf(w['L1_attn_W_O'])
    d['bO'] = _vec_pc(w['L1_attn_b_O'], 8)
    d['wugT'] = _bf(w['L1_attn_W_ug'])
    d['bug'] = _f32(np.asarray(w['L1_attn_b_ug'], np.float32).reshape(9, 128).T)
    d['bwT'] = _bf(w['L1_attn_B_w'])
    d['Avec'] = _f32(np.asarray(w['L1_attn_A'], np.float32).reshape(128, 1))
    d['cwT'] = _bf(w['L1_attn_C_w'])
    d['mw1T'] = _bf(w['L1_attn_mlp_W1'])
    d['mb1'] = _vec_pc(w['L1_attn_mlp_b1'], 32)
    d['mw2T'] = _bf(w['L1_attn_mlp_W2'])
    d['mb2'] = _vec_pc(w['L1_attn_mlp_b2'], 8)
    d['gateT'] = _f32(np.asarray(w['L1_ffn_gate_w']).T.reshape(8, 128, EXP).transpose(1, 0, 2))
    d['wgT'] = _bf(np.transpose(np.asarray(w['L1_ffn_Wg']), (0, 2, 1)))
    d['wuT'] = _bf(np.transpose(np.asarray(w['L1_ffn_Wu']), (0, 2, 1)))
    d['wdT'] = _bf(np.transpose(np.asarray(w['L1_ffn_Wd']), (0, 2, 1)))
    d['wsgT'] = _bf(np.asarray(w['L1_ffn_Ws_g']).T)
    d['wsuT'] = _bf(np.asarray(w['L1_ffn_Ws_u']).T)
    d['wsdT'] = _bf(np.asarray(w['L1_ffn_Ws_d']).T)

    d['identb'] = _bf(np.eye(128))
    d['identf'] = _f32(np.eye(128))
    inv = 1.0 / (10000.0 ** (np.arange(0, DH, 2, dtype=np.float32) / DH))
    ang = np.outer(inv, np.arange(T, dtype=np.float32))
    rows = np.arange(128)
    fidx = (rows % 64) // 2
    sgn = np.where(rows % 2 == 0, -1.0, 1.0).astype(np.float32)
    d['ropeC'] = _bf(np.cos(ang)[fidx])
    d['ropeS'] = _bf(np.sin(ang)[fidx] * sgn[:, None])
    sw = np.zeros((128, 128), np.float32)
    sw[np.arange(128), np.arange(128) ^ 1] = 1.0
    d['swap'] = _bf(sw)
    qi = np.arange(128)[:, None]
    ki = np.arange(128)[None, :]
    mB = np.where(ki > qi, -1e9, 0.0).astype(np.float32)
    mA = np.where(ki <= qi, -1e9, 0.0).astype(np.float32)
    d['maskB'] = _f32(mB)
    d['maskL1'] = _f32(np.concatenate([mA, np.zeros((128, 128), np.float32), mB], 1))
    d['ones1'] = _f32(np.ones((1, 128)))
    d['ones128'] = _f32(np.ones((128, 1)))
    ind2 = np.zeros((2, 128), np.float32)
    ind2[0, :64] = 1.0
    ind2[1, 64:] = 1.0
    d['ind2'] = _f32(ind2)
    return d


SHARED_SPECS = {
    'l0_wqT': ([DM, DM], BF16), 'l0_wkT': ([DM, DM], BF16),
    'l0_wvT': ([DM, DM], BF16), 'l0_woT': ([DM, DM], BF16),
    'w1T_f32': ([DM, HID0], F32), 'w2T_f32': ([HID0, DM], F32),
    'convw': ([128, 96], F32), 'convb': ([128, 32], F32),
    'ln1g': ([128, 8], F32), 'ln1b': ([128, 8], F32),
    'ln2g': ([128, 8], F32), 'ln2b': ([128, 8], F32),
    'wqkvT': ([DM, 3 * DM], BF16), 'l1_woT': ([DM, DM], BF16),
    'bO': ([128, 8], F32), 'wugT': ([DM, 1152], BF16), 'bug': ([128, 9], F32),
    'bwT': ([DM, SD], BF16), 'Avec': ([128, 1], F32), 'cwT': ([SD, DM], BF16),
    'mw1T': ([DM, MLPH], BF16), 'mb1': ([128, 32], F32),
    'mw2T': ([MLPH, DM], BF16), 'mb2': ([128, 8], F32),
    'gateT': ([128, 8, EXP], F32),
    'wgT': ([EXP, DM, FSH], BF16), 'wuT': ([EXP, DM, FSH], BF16),
    'wdT': ([EXP, FSH, DM], BF16),
    'wsgT': ([DM, FSH], BF16), 'wsuT': ([DM, FSH], BF16), 'wsdT': ([FSH, DM], BF16),
    'identb': ([128, 128], BF16), 'identf': ([128, 128], F32),
    'ropeC': ([128, T], BF16), 'ropeS': ([128, T], BF16),
    'swap': ([128, 128], BF16),
    'maskB': ([128, 128], F32), 'maskL1': ([128, 384], F32),
    'ones1': ([1, 128], F32), 'ones128': ([128, 1], F32), 'ind2': ([2, 128], F32),
}

CONST_BF = ['identb', 'swap', 'ropeC', 'ropeS']
CONST_F32 = ['identf', 'maskB', 'maskL1', 'ones1', 'ones128', 'ind2', 'convw', 'convb',
             'ln1g', 'ln1b', 'ln2g', 'ln2b', 'bO', 'bug', 'Avec', 'mb1', 'mb2']



class St:
    pass


def build_program(upto=99, dbg=None):
    B = Builder()
    nc = B.nc
    st = St()
    st.B = B
    st.dbg = dbg or []

    st.din = {}
    st.din['x'] = B.dram('x', [T, DM], F32, kind="ExternalInput")
    for n, (shp, dt) in SHARED_SPECS.items():
        st.din[n] = B.dram(n, shp, dt, kind="ExternalInput")
    st.dout = B.dram('out', [T, DM], F32, kind="ExternalOutput")

    st.scr = {}
    st.scr['X'] = B.dram('scr_X', [128, 8, T], F32)
    st.scr['s0'] = B.dram('scr_s0', [128, 8, T], F32)
    st.scr['g0'] = B.dram('scr_g0', [128, 8, T], BF16)
    st.scr['xn32'] = B.dram('scr_xn32', [128, 8, T], F32)
    st.scr['xn16'] = B.dram('scr_xn16', [T, DM], BF16)
    st.scr['xn1'] = B.dram('scr_xn1', [128, 8, T], BF16)
    st.scr['w1q'] = B.dram('scr_w1q', [DM, HID0], BF16)
    st.scr['w2q'] = B.dram('scr_w2q', [HID0, DM], BF16)
    st.scr['acc'] = B.dram('scr_acc', [T, DM], F32)
    st.scr['h2'] = B.dram('scr_h2', [128, 8, T], BF16)
    st.scr['idx'] = B.dram('scr_idx', [EXP, CAP], U16)
    st.scr['val'] = B.dram('scr_val', [EXP, CAP], F32)

    st.pool = B.sbuf('pool', [128, POOL_N], F32)

    def R(off, n, c=None):
        assert off + n <= POOL_N, (off, n)
        ap = st.pool[:, off:off + n]
        if c is not None:
            ap = ap.rearrange("p (c n) -> p c n", c=c)
        return ap

    def Rb(off, nbf, c=None):
        assert off + nbf // 2 <= POOL_N, (off, nbf)
        ap = st.pool[:, off:off + nbf // 2].bitcast(BF16)
        if c is not None:
            ap = ap.rearrange("p (c n) -> p c n", c=c)
        return ap

    def Ri(off, ni):
        return st.pool[:, off:off + ni // 2].bitcast(I16)

    st.R, st.Rb, st.Ri = R, Rb, Ri

    st.cst = {}
    for n in CONST_BF:
        st.cst[n] = B.sbuf('c_' + n, SHARED_SPECS[n][0], BF16)
    for n in CONST_F32:
        st.cst[n] = B.sbuf('c_' + n, SHARED_SPECS[n][0], F32)
    st.sm = B.sbuf('small', [128, 3072], F32)

    st.psS = B.psum('psS', [128, 2048], F32)
    st.psA = B.psum('psA', [128, 512], F32)
    st.psB = B.psum('psB', [128, 512], F32)
    st.psC = B.psum('psC', [128, 512], F32)
    st.psT = B.psum('psT', [128, 1024], BF16)

    for s in ['te', 've', 'se', 'gp', 'dW', 'dG', 'dO']:
        B.sem(s)

    blk = nc.Block()
    B.block = blk.__enter__()
    B.ctx.append(blk)

    ph_consts(st)
    ph_p1_load_ln0(st)
    if upto >= 2:
        ph_p2_l0_attn(st)
    if upto >= 3:
        ph_p3_l0_ffn(st)
    if upto >= 4:
        ph_p4_combine0(st)
    if upto >= 5:
        ph_p5_ln1(st)
    if upto >= 6:
        ph_p6_moe(st)
    if upto >= 7:
        ph_p7_l1_attn(st)
    if upto >= 8:
        ph_p8_final(st)

    if st.dbg:
        ph_dbg(st)

    for c in reversed(B.ctx):
        c.__exit__(None, None, None)
    lower_extended_insts(nc)
    return B


def ph_dbg(st):
    B = st.B
    outs = []
    for name in st.dbg:
        if name.startswith('scr_'):
            ap = st.scr[name[4:]]
        else:
            ap = getattr(st, name)
            if not isinstance(ap, bass.AP):
                ap = ap[:, :]
        shp = list(ap.shape)
        d = B.dram('dbg_' + name, shp, ap.dtype, kind="ExternalOutput")
        outs.append((d, ap))

    @B.on('sy')
    def _(e):
        for s in ['te', 've', 'se', 'gp', 'dG', 'dW', 'dO']:
            if B.c[s]:
                e.wait_ge(B.S[s], B.c[s])
        n = 0
        for d, ap in outs:
            B.inc(e.dma_start(d, ap), 'dO', 16)
            n += 16
        e.wait_ge(B.S['dO'], B.c['dO'])


def ph_consts(st):
    B = st.B

    @B.on('sy')
    def _(e):
        for n, h in st.cst.items():
            B.inc(e.dma_start(h[:, :], st.din[n]), 'dW', 16)
    st.tk_consts = B.tok('dW')

    st.cv = {}
    vals = {'eps': 1e-5, 'neg8': -8.0, 'neg20': -20.0, 'none': 0.0}
    @B.on('ve')
    def _(e):
        for i, (n, v) in enumerate(vals.items()):
            slot = st.sm[:, 2960 + i:2961 + i]
            e.memset(slot, v)
            st.cv[n] = slot


def ph_p1_load_ln0(st):
    import os
    DIS = os.environ.get('P1DIS', '').split(',')
    """Load x token-major; transpose raw x -> X fm, spill to scr_X.
    LN0 in TM; quantize xq0 (TM) and transpose to xq0 fm bf16 (bC);
    xn0 fm bf16 -> bA[0:8192]. alpha0 row + sxq row in sm."""
    B = st.B
    R, Rb, Ri = st.R, st.Rb, st.Ri
    cst, sm = st.cst, st.sm
    psS, psT = st.psS, st.psT

    x_tm = R(O_B, 16384, c=16)
    st.x_tm = x_tm
    st.sm_ap = sm[:, 0:256]
    st.sm_hi = sm[:, 2944:3072]
    stage = R(O_A + 8192, 8192)
    xn16 = Rb(O_A, 16384, c=8)
    xq0 = Rb(O_C, 16384, c=8)
    st.xn16_0, st.xq0 = xn16, xq0

    flagf = sm[:, 3000:3001]
    flagi = Ri(0, 0)
    cvec = sm[:, 3002:3003]
    st.cvec = cvec

    @B.on('sy')
    def _(e):
        for tt in range(16):
            B.inc(e.dma_start(x_tm[:, tt, :], st.din['x'][128 * tt:128 * (tt + 1), :]),
                  'dW', 16)
    tk_x = B.tok('dW')

    @B.on('ve')
    def _(e):
        B.inc(e.memset(flagf, 0.7), 've')
        B.bar(e, 've')
        B.inc(e.tensor_scalar(sm[:, 3001:3002].bitcast(I16)[:, 0:1], flagf, 0.0, None, Op.add), 've')
        B.bar(e, 've')
        B.inc(e.tensor_scalar(cvec, sm[:, 3001:3002].bitcast(I16)[:, 0:1], -0.5, 0.5,
                              Op.mult, Op.add), 've')

    @B.on('ve')
    def _(e):
        B.wait(e, 've', tk_x)
        for tt in range(16):
            ins = e.tensor_reduce(sm[:, tt:tt + 1], x_tm[:, tt, :], mybir.AxisListType.X, Op.add)
        B.inc(ins, 've')
        B.bar(e, 've')
        B.inc(e.tensor_scalar(sm[:, 16:32], sm[:, 0:16], -1.0 / DM, None, Op.mult), 've')
    tk_negmu = B.tok('ve')

    @B.on('se')
    def _(e):
        B.wait(e, 'se', tk_x)
        for tt in range(16):
            B.inc(e.activation(stage[:, 7168:8192], x_tm[:, tt, :], AF.Square,
                               accum_out=sm[:, 32 + tt:33 + tt]), 'se')
    tk_sq = B.tok('se')

    tk_sp = []
    tk_ev = []
    for c in (() if 'xspill' in DIS else range(8)):
        @B.on('te')
        def _(e, c=c):
            B.wait(e, 'te', tk_x)
            if c >= 1:
                B.wait(e, 'te', tk_ev[c - 1])
            for tt in range(16):
                ins = e.transpose(psS[:, 128 * tt:128 * (tt + 1)],
                                  x_tm[:, tt, 128 * c:128 * (c + 1)], cst['identf'][:, :])
            B.inc(ins, 'te')
        tk_t = B.tok('te')

        @B.on('ve')
        def _(e, c=c, tk_t=tk_t):
            B.wait(e, 've', tk_t)
            if c >= 2:
                B.wait(e, 've', tk_sp[c - 2])
            B.inc(e.tensor_copy(stage[:, 0:2048] if c % 2 == 0 else stage[:, 2048:4096],
                                psS[:, :]), 've')
        tk_ev.append(B.tok('ve'))

        @B.on('sy')
        def _(e, c=c):
            B.wait(e, 'sy', tk_ev[c])
            B.inc(e.dma_start(st.scr['X'][:, c, :],
                              stage[:, 0:2048] if c % 2 == 0 else stage[:, 2048:4096]),
                  'dW', 16)
        tk_sp.append(B.tok('dW'))
    st.tk_Xspill = tk_sp[-1] if tk_sp else None
    tk_xtrans = ('te', B.c['te'])

    @B.on('ve')
    def _(e):
        B.wait(e, 've', tk_sq)
        B.bar(e, 've')
        e.tensor_tensor(sm[:, 48:64], sm[:, 16:32], sm[:, 16:32], Op.mult)
        B.inc(e.engine_nop(), 've')
        B.bar(e, 've')
        e.tensor_scalar(sm[:, 112:128], sm[:, 32:48], 1.0 / DM, None, Op.mult)
        B.inc(e.tensor_tensor(sm[:, 48:64], sm[:, 112:128], sm[:, 48:64], Op.subtract), 've')
    tk_var = B.tok('ve')

    @B.on('se')
    def _(e):
        B.wait(e, 'se', tk_var)
        B.inc(e.activation(sm[:, 48:64], sm[:, 48:64], AF.Sqrt, bias=st.cv['eps'], scale=1.0), 'se')
    tk_sqv = B.tok('se')

    @B.on('ve')
    def _(e):
        B.wait(e, 've', tk_sqv, tk_xtrans)
        B.inc(e.reciprocal(sm[:, 48:64], sm[:, 48:64]), 've')
        B.bar(e, 've')
        for tt in range(16):
            B.inc(e.tensor_scalar(x_tm[:, tt, :], x_tm[:, tt, :],
                                  sm[:, 16 + tt:17 + tt], sm[:, 48 + tt:49 + tt],
                                  Op.add, Op.mult), 've')
    tk_xn = B.tok('ve')

    @B.on('ve')
    def _(e):
        for tt in range(16):
            ins = e.tensor_reduce(sm[:, 64 + tt:65 + tt], x_tm[:, tt, :], mybir.AxisListType.X,
                                  Op.max, apply_absolute_value=True)
        B.inc(ins, 've')
        B.bar(e, 've')
        B.inc(e.tensor_scalar(sm[:, 64:80], sm[:, 64:80], 1e-5, None, Op.max), 've')
        B.bar(e, 've')
        B.inc(e.reciprocal(sm[:, 80:96], sm[:, 64:80]), 've')
        B.bar(e, 've')
        e.tensor_scalar(sm[:, 80:96], sm[:, 80:96], 127.0, None, Op.mult)
        B.inc(e.tensor_scalar(sm[:, 96:112], sm[:, 64:80], 1.0 / 127.0, None, Op.mult), 've')
        B.bar(e, 've')
    tk_scales = B.tok('ve')

    sgn_st = stage[:, 4096:5120]
    u_st = stage[:, 5120:6144]
    u16_st = stage[:, 6144:6656].bitcast(I16)
    xqtm = stage[:, 6656:7168].bitcast(BF16)
    tk_q = []
    for tt in (() if 'quant' in DIS else range(16)):
        @B.on('se')
        def _(e, tt=tt):
            B.wait(e, 'se', tk_xn)
            if tt >= 1:
                B.wait(e, 'se', ('ve', B.c['ve']))
            B.inc(e.activation(sgn_st, x_tm[:, tt, :], AF.Sign), 'se')
        tk_sg = B.tok('se')

        @B.on('ve')
        def _(e, tt=tt, tk_sg=tk_sg):
            B.wait(e, 've', tk_sg, tk_scales)
            B.inc(e.tensor_scalar(u_st, x_tm[:, tt, :], sm[:, 80 + tt:81 + tt], None, Op.mult), 've')
            B.bar(e, 've')
            B.inc(e.scalar_tensor_tensor(u16_st, sgn_st, st.cvec[:, 0:1], u_st, Op.mult, Op.add), 've')
            B.bar(e, 've')
            B.inc(e.tensor_copy(xqtm, u16_st), 've')
        tk_xq = B.tok('ve')

        @B.on('te')
        def _(e, tt=tt, tk_xq=tk_xq):
            B.wait(e, 'te', tk_xq)
            B.wait(e, 'te', ('ve', B.c['ve']))
            for c in range(8):
                ins = e.transpose(psT[:, 128 * (c % 8):128 * (c % 8 + 1)],
                                  xqtm.rearrange("p (c n) -> p c n", c=8)[:, c, :],
                                  cst['identb'][:, :])
            B.inc(ins, 'te')
        tk_tp = B.tok('te')

        @B.on('ve')
        def _(e, tt=tt, tk_tp=tk_tp):
            B.wait(e, 've', tk_tp)
            for c in range(8):
                ins = e.tensor_copy(xq0[:, c, 128 * tt:128 * (tt + 1)],
                                    psT[:, 128 * c:128 * (c + 1)])
            B.inc(ins, 've')
        tk_q.append(B.tok('ve'))

    for c in (() if 'xn16' in DIS else range(8)):
        @B.on('te')
        def _(e, c=c):
            B.wait(e, 'te', tk_xn)
            B.wait(e, 'te', ('ve', B.c['ve']))
            for tt in range(16):
                ins = e.transpose(psS[:, 128 * tt:128 * (tt + 1)],
                                  x_tm[:, tt, 128 * c:128 * (c + 1)], cst['identf'][:, :])
            B.inc(ins, 'te')
        tk_t = B.tok('te')

        @B.on('ve')
        def _(e, c=c, tk_t=tk_t):
            B.wait(e, 've', tk_t)
            B.inc(e.tensor_copy(xn16[:, c, :], psS[:, :]), 've')
    st.tk_p1_done = B.tok('ve')
    st.tk_xq0 = tk_q[-1] if tk_q else None



def _wq_slices(st):
    B = st.B
    R, Rb, Ri = st.R, st.Rb, st.Ri
    sm = st.sm
    cst = st.cst
    slab = [R(O_A + 12288, 2048), R(O_A + 14336, 2048)]
    sgn_w = R(O_A + 10240, 1024)
    u_w = R(O_A + 11264, 1024)

    units = []
    for i in range(16):
        units.append(('w1', bass.AP(st.din['w1T_f32'].tensor,
                                    (128 * (i // 2)) * HID0 + (i % 2) * 2048,
                                    [[HID0, 128], [1, 2048]])))
    for i in range(16):
        units.append(('w2', bass.AP(st.din['w2T_f32'].tensor, (i * 256) * DM,
                                    [[DM, 128], [DM * 128, 2], [1, 1024]])))

    def wq_dst(i, h):
        kind = units[i][0]
        if kind == 'w1':
            off = (128 * (i // 2)) * HID0 + (i % 2) * 2048 + 1024 * h
            return bass.AP(st.scr['w1q'].tensor, off, [[HID0, 128], [1, 1024]])
        j = i - 16
        off = (256 * j + 128 * h) * DM
        return bass.AP(st.scr['w2q'].tensor, off, [[DM, 128], [1, 1024]])

    mtok = {'dma': [None] * 32, 've': [None] * 32}
    qtok = {'dma': [None] * 32, 've': [None] * 32, 'out': [None] * 32}

    def mk_mean(lo, hi):
        def emit():
            for i in range(lo, hi):
                @B.on('gp')
                def _(e, i=i):
                    B.wait(e, 'gp', st.tk_p1_done, st.tk_p1_se)
                    if i >= 2:
                        B.wait(e, 'gp', mtok['ve'][i - 2])
                    mtok['dma'][i] = B.inc(e.dma_start(slab[i % 2], units[i][1]), 'dG', 16)

                @B.on('ve')
                def _(e, i=i):
                    B.wait(e, 've', mtok['dma'][i])
                    j = 0 if units[i][0] == 'w1' else 1
                    e.tensor_reduce(sm[:, 162:163], slab[i % 2], mybir.AxisListType.X,
                                    Op.add, apply_absolute_value=True)
                    if i % 16 == 0:
                        mtok['ve'][i] = B.inc(e.tensor_copy(sm[:, 160 + j:161 + j],
                                                            sm[:, 162:163]), 've')
                    else:
                        mtok['ve'][i] = B.inc(e.tensor_tensor(sm[:, 160 + j:161 + j],
                                                              sm[:, 160 + j:161 + j],
                                                              sm[:, 162:163], Op.add), 've')
        return emit

    def mk_final():
        def emit():
            tkv = ('ve', B.c['ve'])

            @B.on('te')
            def _(e):
                B.wait(e, 'te', tkv)
                B.inc(e.matmul(st.psA[0:1, 0:2], cst['ones128'][:, :], sm[:, 160:162],
                               start=True, stop=True), 'te')
            tkt = B.tok('te')

            @B.on('ve')
            def _(e):
                B.wait(e, 've', tkt)
                B.inc(e.tensor_scalar(sm[0:1, 163:165], st.psA[0:1, 0:2],
                                      1.0 / (4096.0 * 1024.0), 1e-5, Op.mult, Op.max), 've')
            tkv2 = ('ve', B.c['ve'])

            @B.on('te')
            def _(e):
                B.wait(e, 'te', tkv2)
                B.inc(e.matmul(st.psB[:, 0:2], cst['ones1'][0:1, :], sm[0:1, 163:165],
                               start=True, stop=True), 'te')
            tkt2 = B.tok('te')

            @B.on('ve')
            def _(e):
                B.wait(e, 've', tkt2)
                e.tensor_copy(sm[:, 176:178], st.psB[:, 0:2])
                e.reciprocal(sm[:, 178:180], sm[:, 176:178])
        return emit

    def mk_quant(lo, hi):
        def emit():
            for i in range(lo, hi):
                @B.on('gp')
                def _(e, i=i):
                    if i >= 2:
                        B.wait(e, 'gp', qtok['ve'][i - 2], qtok['out'][i - 2])
                    qtok['dma'][i] = B.inc(e.dma_start(slab[i % 2], units[i][1]), 'dG', 16)
                for h in range(2):
                    src = slab[i % 2][:, 1024 * h:1024 * (h + 1)]
                    u16 = st.pool[:, (O_A + 12288 + 2048 * (i % 2) + 1024 * h):
                                  (O_A + 12288 + 2048 * (i % 2) + 1024 * h + 512)].bitcast(I16)
                    wqv = st.pool[:, (O_A + 12288 + 2048 * (i % 2) + 1024 * h + 512):
                                  (O_A + 12288 + 2048 * (i % 2) + 1024 * h + 1024)].bitcast(BF16)

                    @B.on('se')
                    def _(e, i=i, h=h, src=src):
                        B.wait(e, 'se', qtok['dma'][i])
                        if not (i == lo and h == 0):
                            B.wait(e, 'se', ('ve', B.c['ve']))
                        B.inc(e.activation(sgn_w, src, AF.Sign), 'se')
                    tk_sg = B.tok('se')

                    @B.on('ve')
                    def _(e, i=i, h=h, src=src, u16=u16, wqv=wqv, tk_sg=tk_sg):
                        j = 0 if units[i][0] == 'w1' else 1
                        B.wait(e, 've', qtok['dma'][i], tk_sg)
                        e.tensor_scalar(u_w, src, sm[:, 178 + j:179 + j], None, Op.mult)
                        e.scalar_tensor_tensor(u16, sgn_w, st.cvec[:, 0:1], u_w,
                                               Op.mult, Op.add)
                        tk = B.inc(e.tensor_scalar(wqv, u16, 1.0, -1.0, Op.min, Op.max), 've')
                        if h == 1:
                            qtok['ve'][i] = tk

                    @B.on('gp')
                    def _(e, i=i, h=h, wqv=wqv):
                        B.wait(e, 'gp', ('ve', B.c['ve']))
                        qtok['out'][i] = B.inc(e.dma_start(wq_dst(i, h), wqv), 'dG', 16)
        return emit

    return ([mk_mean(0, 8), mk_mean(8, 16), mk_mean(16, 24), mk_mean(24, 32), mk_final()],
            [mk_quant(0, 8), mk_quant(8, 16), mk_quant(16, 24), mk_quant(24, 32)])


def ph_p2_l0_attn(st):
    B = st.B
    R, Rb, Ri = st.R, st.Rb, st.Ri
    cst, sm = st.cst, st.sm
    psS, psA, psB, psC, psT = st.psS, st.psA, st.psB, st.psC, st.psT
    xn16 = st.xn16_0

    st.tk_p1_se = ('se', B.c['se'])
    attn_full = Rb(O_B, 16384, c=8)
    qraw = Rb(O_B + 8192, 2048)
    kraw = Rb(O_B + 9216, 2048)
    qrot = Rb(O_B + 10240, 2048)
    krot = Rb(O_B + 11264, 2048)
    v2 = Rb(O_B + 12288, 2048)
    vtm = Rb(O_B + 13312, 2048, c=16)
    t1b = Rb(O_B + 14336, 2048)
    t2b = Rb(O_B + 15360, 2048)
    aqe = Rb(O_A + 8192, 2048)
    attnT = Rb(O_A + 9216, 2048, c=16)
    s0st = R(O_A + 10240, 2048)
    st.attn_full0 = attn_full

    wr = [Rb(O_D + 512 * i, 1024, c=8) for i in range(6)]
    mean_sl, quant_sl = _wq_slices(st)

    tk_wv = [None] * 8
    tk_dma = [None] * 8

    def load_pair(p):
        @B.on('sy')
        def _(e):
            if p >= 2 and tk_wv[p - 2] is not None:
                B.wait(e, 'sy', tk_wv[p - 2])
            base = (p % 2) * 3
            for wi, wn in enumerate(['l0_wqT', 'l0_wkT', 'l0_wvT']):
                src = bass.AP(st.din[wn].tensor, 128 * p, [[DM, 128], [DM * 128, 8], [1, 128]])
                B.inc(e.dma_start(wr[base + wi][:, :, :], src), 'dW', 16)
        tk_dma[p] = B.tok('dW')

    load_pair(0)
    load_pair(1)

    for p in range(8):
        base = (p % 2) * 3
        for wi, dst in enumerate([qraw, kraw, v2]):
            @B.on('te')
            def _(e, p=p, wi=wi):
                B.wait(e, 'te', tk_dma[p], st.tk_p1_done)
                B.wait(e, 'te', ('ve', B.c['ve']))
                for n in range(4):
                    for f in range(8):
                        ins = e.matmul(psS[:, 512 * n:512 * (n + 1)],
                                       wr[base + wi][:, f, :],
                                       xn16[:, f, 512 * n:512 * (n + 1)],
                                       start=(f == 0), stop=(f == 7))
                B.inc(ins, 'te')
            tk_mm = B.tok('te')

            @B.on('ve')
            def _(e, dst=dst, tk_mm=tk_mm):
                B.wait(e, 've', tk_mm)
                e.tensor_copy(dst, psS[:, :])
        tk_wv[p] = B.tok('ve')
        tk_wv[p] = ('te', B.c['te'])
        if p + 2 < 8:
            load_pair(p + 2)

        for srcb, rotb in [(qraw, qrot), (kraw, krot)]:
            @B.on('te')
            def _(e, srcb=srcb):
                B.wait(e, 'te', ('ve', B.c['ve']))
                for n in range(4):
                    ins = e.matmul(psS[:, 512 * n:512 * (n + 1)], cst['swap'][:, :],
                                   srcb[:, 512 * n:512 * (n + 1)], start=True, stop=True)
                B.inc(ins, 'te')
            tk_sw = B.tok('te')

            @B.on('ve')
            def _(e, srcb=srcb, rotb=rotb, tk_sw=tk_sw):
                B.wait(e, 've', tk_sw)
                e.tensor_tensor(t1b, srcb, cst['ropeC'][:, :], Op.mult)
                e.tensor_tensor(t2b, psS[:, :], cst['ropeS'][:, :], Op.mult)
                e.tensor_tensor(rotb, t1b, t2b, Op.add)

        for g in range(2):
            @B.on('te')
            def _(e, g=g):
                B.wait(e, 'te', ('ve', B.c['ve']))
                for kt in range(8):
                    ins = e.transpose(psT[:, 128 * kt:128 * (kt + 1)],
                                      v2[:, 128 * (8 * g + kt):128 * (8 * g + kt + 1)],
                                      cst['identb'][:, :])
                B.inc(ins, 'te')
            tk = B.tok('te')

            @B.on('ve')
            def _(e2, g=g, tk=tk):
                B.wait(e2, 've', tk)
                for kt in range(8):
                    e2.tensor_copy(vtm[:, 8 * g + kt, :], psT[:, 128 * kt:128 * (kt + 1)])
        tk_vtm = ('ve', B.c['ve'])

        import os as _os
        if _os.environ.get('P2NOINT'):
            pass
        elif p < 5 and mean_sl:
            mean_sl.pop(0)()
        elif quant_sl and p >= 5:
            quant_sl.pop(0)()

        for hl in range(2):
            hb = 64 * hl
            for qt in range(16):
                nk = (qt + 1) * 128
                @B.on('te')
                def _(e, p=p, hl=hl, qt=qt, hb=hb, nk=nk):
                    B.wait(e, 'te', ('se', B.c['se']))
                    B.wait(e, 'te', ('ve', B.c['ve']))
                    for kn in range((nk + 511) // 512):
                        N = min(512, nk - 512 * kn)
                        ins = e.matmul(psS[:, 512 * kn:512 * kn + N],
                                       qrot[hb:hb + 64, 128 * qt:128 * (qt + 1)],
                                       krot[hb:hb + 64, 512 * kn:512 * kn + N],
                                       start=True, stop=True)
                    B.inc(ins, 'te')
                tk_sc = B.tok('te')

                @B.on('ve')
                def _(e, qt=qt, tk_sc=tk_sc):
                    B.wait(e, 've', tk_sc)
                    B.inc(e.tensor_tensor(psS[:, 128 * qt:128 * (qt + 1)],
                                          psS[:, 128 * qt:128 * (qt + 1)],
                                          cst['maskB'][:, :], Op.add), 've')
                tk_msk = B.tok('ve')

                @B.on('se')
                def _(e, hl=hl, qt=qt, nk=nk, tk_msk=tk_msk):
                    B.wait(e, 'se', tk_msk)
                    B.inc(e.activation(aqe[:, 0:nk], psS[:, 0:nk], AF.Exp,
                                       bias=st.cv['neg8'], scale=0.125,
                                       accum_out=sm[:, 128 + qt:129 + qt]), 'se')
                tk_exp = B.tok('se')

                @B.on('ve')
                def _(e, qt=qt, nk=nk, tk_exp=tk_exp):
                    B.wait(e, 've', tk_exp)
                    e.reciprocal(sm[:, 144 + qt:145 + qt], sm[:, 128 + qt:129 + qt])
                    B.inc(e.tensor_scalar(aqe[:, 0:nk], aqe[:, 0:nk],
                                          sm[:, 144 + qt:145 + qt], None, Op.mult), 've')
                tk_nrm = B.tok('ve')

                nbat = (qt + 1 + 7) // 8
                for g in range(nbat):
                    kts = list(range(8 * g, min(8 * g + 8, qt + 1)))
                    @B.on('te')
                    def _(e, kts=kts, tk_nrm=tk_nrm):
                        B.wait(e, 'te', tk_nrm)
                        B.wait(e, 'te', ('ve', B.c['ve']))
                        for kt in kts:
                            ins = e.transpose(psT[:, 128 * (kt % 8):128 * (kt % 8 + 1)],
                                              aqe[:, 128 * kt:128 * (kt + 1)],
                                              cst['identb'][:, :])
                        B.inc(ins, 'te')
                    tk_tp = B.tok('te')

                    @B.on('ve')
                    def _(e, kts=kts, tk_tp=tk_tp):
                        B.wait(e, 've', tk_tp)
                        for kt in kts:
                            ins = e.tensor_copy(attnT[:, kt, :],
                                                psT[:, 128 * (kt % 8):128 * (kt % 8 + 1)])
                        B.inc(ins, 've')
                tk_aT = B.tok('ve')

                bank = [psA, psB, psC][(hl * 16 + qt) // 4 % 3]
                @B.on('te')
                def _(e, hl=hl, qt=qt, hb=hb, bank=bank, tk_aT=tk_aT):
                    B.wait(e, 'te', tk_aT)
                    if qt % 4 == 0:
                        B.wait(e, 'te', ('ve', B.c['ve']))
                    for kt in range(qt + 1):
                        ins = e.matmul(bank[hb:hb + 64, 128 * (qt % 4):128 * (qt % 4 + 1)],
                                       vtm[:, kt, hb:hb + 64], attnT[:, kt, :],
                                       start=(kt == 0), stop=(kt == qt),
                                       tile_position=(0, hb))
                    B.inc(ins, 'te')
                tk_av = B.tok('te')

                if qt % 4 == 3:
                    @B.on('ve')
                    def _(e, p=p, hl=hl, qt=qt, hb=hb, bank=bank, tk_av=tk_av):
                        B.wait(e, 've', tk_av)
                        B.inc(e.tensor_copy(
                            attn_full[hb:hb + 64, p, 512 * (qt // 4):512 * (qt // 4 + 1)],
                            bank[hb:hb + 64, :]), 've')

    for f in mean_sl + quant_sl:
        f()
    st.tk_wq_done = B.tok('dG')

    tk_wo = [None, None]
    for d in range(8):
        @B.on('sy')
        def _(e, d=d):
            B.wait(e, 'sy', tk_wv[6])
            if d >= 2:
                B.wait(e, 'sy', tk_wo[d % 2])
            src = bass.AP(st.din['l0_woT'].tensor, 128 * d, [[DM, 128], [DM * 128, 8], [1, 128]])
            B.inc(e.dma_start(wr[d % 2][:, :, :], src), 'dW', 16)
        tk_d = B.tok('dW')

        @B.on('te')
        def _(e, d=d, tk_d=tk_d):
            B.wait(e, 'te', tk_d)
            B.wait(e, 'te', ('ve', B.c['ve']))
            for n in range(4):
                for f in range(8):
                    ins = e.matmul(psS[:, 512 * n:512 * (n + 1)], wr[d % 2][:, f, :],
                                   attn_full[:, f, 512 * n:512 * (n + 1)],
                                   start=(f == 0), stop=(f == 7))
            B.inc(ins, 'te')
        tk_wo[d % 2] = B.tok('te')
        tk_mm = B.tok('te')

        @B.on('ve')
        def _(e, d=d, tk_mm=tk_mm):
            B.wait(e, 've', tk_mm)
            B.wait(e, 've', ('dG', B.c['dG']))
            if d >= 2:
                B.wait(e, 've', st.tk_s0sp[d - 2])
            B.inc(e.tensor_copy(s0st if d % 2 == 0 else st.R(O_A + 12288, 2048), psS[:, :]), 've')
        tk_ev = B.tok('ve')

        @B.on('sy')
        def _(e, d=d, tk_ev=tk_ev):
            B.wait(e, 'sy', tk_ev)
            B.inc(e.dma_start(st.scr['s0'][:, d, :],
                              s0st if d % 2 == 0 else st.R(O_A + 12288, 2048)), 'dW', 16)
        if not hasattr(st, 'tk_s0sp'):
            st.tk_s0sp = []
        st.tk_s0sp.append(B.tok('dW'))
    st.tk_p2_done = ('ve', B.c['ve'])


def ph_p3_l0_ffn(st):
    B = st.B
    R, Rb, Ri = st.R, st.Rb, st.Ri
    cst, sm = st.cst, st.sm
    psS, psA, psB, psC, psT = st.psS, st.psA, st.psB, st.psC, st.psT
    psTf = psT[:, :].bitcast(F32)
    xq0 = st.xq0
    pool = st.pool

    h1c = R(O_B, 16640).rearrange("p (c n) -> p c n", c=32)
    xqh = Rb(O_A, 16384, c=32)
    ring = [Rb(O_A + 8192 + 512 * i, 1024, c=8) for i in range(4)]
    prowb = R(O_A + 10240, 514)
    o1st = R(O_A + 10754, 512)
    o3st = R(O_A + 11266, 512)
    sgn3 = o1st
    u3 = o3st
    u16_3 = Ri(O_A + 11778, 512)
    shqb = R(O_A + 12290, 512)
    prow2b = R(O_A + 12802, 512)
    evstg = R(O_A + 13314, 512)
    g0st = Rb(O_A + 13826, 1024)
    arow = pool[0:1, O_D:O_D + 2050]
    hrow = pool[0:1, O_D + 2056:O_D + 2568]
    hrow2 = pool[0:1, O_D + 2568:O_D + 3080]
    arow_t = pool[0:16, O_D + 3084:O_D + 3212]
    rmax_t = pool[0:4, O_D + 3216:O_D + 3344]

    tk_p2te = ('te', B.c['te'])
    tk_p2dw = ('dW', B.c['dW'])

    @B.on('te')
    def _(e):
        B.wait(e, 'te', tk_p2dw)
        B.wait(e, 'te', ('ve', B.c['ve']))
        B.inc(e.transpose(psA[0:16, 0:128], sm[:, 96:112], cst['identf'][:, :]), 'te')
    tk1 = B.tok('te')

    @B.on('ve')
    def _(e):
        B.wait(e, 've', tk1)
        e.memset(arow, 0.0)
        B.inc(e.tensor_copy(arow_t, psA[0:16, 0:128]), 've')
    tk2 = B.tok('ve')

    @B.on('gp')
    def _(e):
        B.wait(e, 'gp', tk2)
        src = bass.AP(B.t['pool'], O_D + 3084, [[POOL_N, 16], [1, 128]])
        dst = bass.AP(B.t['pool'], O_D + 1, [[POOL_N, 1], [1, 2048]])
        B.inc(e.dma_start(dst, src), 'dG', 16)
    tk3 = B.tok('dG')

    @B.on('ve')
    def _(e):
        B.wait(e, 've', tk3)
        B.inc(e.tensor_scalar(arow[0:1, 1:2049], arow[0:1, 1:2049],
                              sm[0:1, 176:177], None, Op.mult), 've')
        B.inc(e.tensor_scalar(sm[0:1, 180:181], sm[0:1, 177:178], 1.0 / 127.0,
                              None, Op.mult), 've')
    tk_arow = B.tok('ve')

    tk_ld = [None] * 32
    tk_use = [None] * 32

    for tc in range(4):
        t0 = 512 * tc
        P0 = 1 if tc == 0 else 0
        P1_ = 513 if tc == 3 else 514

        @B.on('ve')
        def _(e, tc=tc):
            if tc == 0:
                B.wait(e, 've', tk_arow)
                e.memset(bass.AP(B.t['pool'], O_B + 0, [[POOL_N, 128], [520, 32], [1, 1]]), 0.0)
            if tc == 3:
                e.memset(bass.AP(B.t['pool'], O_B + 513, [[POOL_N, 128], [520, 32], [1, 1]]), 0.0)
        tkz = ('ve', B.c['ve'])

        @B.on('te')
        def _(e, tc=tc, t0=t0):
            B.wait(e, 'te', tk_arow)
            B.wait(e, 'te', ('ve', B.c['ve']))
            e.matmul(psC[:, 0:512], cst['ones1'][0:1, :], arow[0:1, t0:t0 + 512],
                     start=True, stop=True)
            B.inc(e.matmul(psA[:, 0:2], cst['ones1'][0:1, :], arow[0:1, t0 + 512:t0 + 514],
                           start=True, stop=True), 'te')
        tkp = B.tok('te')

        @B.on('ve')
        def _(e, tkp=tkp):
            B.wait(e, 've', tkp)
            e.tensor_copy(prowb[:, 0:512], psC[:, 0:512])
            B.inc(e.tensor_copy(prowb[:, 512:514], psA[:, 0:2]), 've')
        tk_prow = B.tok('ve')

        tk_evac = [None] * 32
        for hc in range(33):
            if hc < 32:
                @B.on('sy')
                def _(e, hc=hc, tc=tc):
                    i = hc
                    if tc == 0 and hc < 4:
                        B.wait(e, 'sy', st.tk_wq_done, tk_p2te)
                        B.wait(e, 'sy', ('ve', B.c['ve']))
                        B.wait(e, 'sy', ('se', B.c['se']))
                    if tk_use[i % 4] is not None:
                        B.wait(e, 'sy', tk_use[i % 4])
                    src = bass.AP(st.scr['w1q'].tensor, 128 * hc,
                                  [[HID0, 128], [HID0 * 128, 8], [1, 128]])
                    tk_ld[i % 4] = B.inc(e.dma_start(ring[i % 4][:, :, :], src), 'dW', 16)

                @B.on('te')
                def _(e, hc=hc, tc=tc, t0=t0, P0=P0, P1_=P1_):
                    B.wait(e, 'te', tk_ld[hc % 4], st.tk_xq0, tk_prow)
                    if hc >= 2:
                        B.wait(e, 'te', tk_evac[hc - 2])
                    half = 1024 * (hc % 2)
                    n0 = min(512, P1_) - P0
                    ins = None
                    for f in range(8):
                        ins = e.matmul(psS[:, half + P0:half + P0 + n0],
                                       ring[hc % 4][:, f, :],
                                       xq0[:, f, t0 - 1 + P0:t0 - 1 + P0 + n0],
                                       start=(f == 0), stop=(f == 7))
                    if P1_ > 512:
                        n1 = P1_ - 512
                        for f in range(8):
                            ins = e.matmul(psS[:, half + 512:half + 512 + n1],
                                           ring[hc % 4][:, f, :],
                                           xq0[:, f, t0 + 511:t0 + 511 + n1],
                                           start=(f == 0), stop=(f == 7))
                    B.inc(ins, 'te')
                tk_use[hc % 4] = B.tok('te')
                tk_mm = B.tok('te')

                @B.on('ve')
                def _(e, hc=hc, P0=P0, P1_=P1_, tk_mm=tk_mm):
                    B.wait(e, 've', tk_mm)
                    half = 1024 * (hc % 2)
                    tk_evac[hc] = B.inc(
                        e.tensor_tensor(h1c[:, hc, P0:P1_], psS[:, half + P0:half + P1_],
                                        prowb[:, P0:P1_], Op.mult), 've')
            if hc >= 1:
                c = hc - 1
                @B.on('se')
                def _(e, c=c):
                    B.wait(e, 'se', tk_evac[c])
                    B.inc(e.activation(o1st, h1c[:, c, 1:513], AF.Identity,
                                       bias=cst['convb'][:, c:c + 1],
                                       scale=cst['convw'][:, 3 * c + 1:3 * c + 2]), 'se')
                tko1 = B.tok('se')

                @B.on('ve')
                def _(e, c=c, tko1=tko1):
                    B.wait(e, 've', tko1)
                    e.scalar_tensor_tensor(o3st, h1c[:, c, 0:512],
                                           cst['convw'][:, 3 * c:3 * c + 1], o1st,
                                           Op.mult, Op.add)
                    B.inc(e.scalar_tensor_tensor(o1st, h1c[:, c, 2:514],
                                                 cst['convw'][:, 3 * c + 2:3 * c + 3],
                                                 o3st, Op.mult, Op.add), 've')
                tko3 = B.tok('ve')

                @B.on('se')
                def _(e, c=c, tko3=tko3):
                    B.wait(e, 'se', tko3)
                    B.inc(e.activation(h1c[:, c, 0:512], o1st, AF.Silu), 'se')
        tk_hs = B.tok('se')

        for g in range(4):
            for rr in range(2):
                @B.on('te')
                def _(e, g=g, rr=rr):
                    B.wait(e, 'te', tk_hs)
                    B.wait(e, 'te', ('ve', B.c['ve']))
                    for k in range(16):
                        hc = 16 * rr + k
                        ins = e.transpose(psS[:, 128 * k:128 * (k + 1)],
                                          h1c[:, hc, 128 * g:128 * (g + 1)],
                                          cst['identf'][:, :])
                    B.inc(ins, 'te')
                tkt = B.tok('te')

                @B.on('ve')
                def _(e, g=g, rr=rr, tkt=tkt):
                    B.wait(e, 've', tkt)
                    e.tensor_reduce(sm[:, 196:197], psS[:, :], mybir.AxisListType.X,
                                    Op.max, apply_absolute_value=True)
                    if rr == 0:
                        B.inc(e.tensor_copy(sm[:, 192 + g:193 + g], sm[:, 196:197]), 've')
                    else:
                        B.inc(e.tensor_tensor(sm[:, 192 + g:193 + g],
                                              sm[:, 192 + g:193 + g],
                                              sm[:, 196:197], Op.max), 've')
        tk_rm = B.tok('ve')

        @B.on('te')
        def _(e):
            B.wait(e, 'te', tk_rm)
            B.inc(e.transpose(psA[0:4, 0:128], sm[:, 192:196], cst['identf'][:, :]), 'te')
        tkt2 = B.tok('te')

        @B.on('ve')
        def _(e, tkt2=tkt2):
            B.wait(e, 've', tkt2)
            B.inc(e.tensor_copy(rmax_t, psA[0:4, 0:128]), 've')
        tkrt = B.tok('ve')

        @B.on('gp')
        def _(e, tkrt=tkrt):
            B.wait(e, 'gp', tkrt)
            src = bass.AP(B.t['pool'], O_D + 3216, [[POOL_N, 4], [1, 128]])
            dst = bass.AP(B.t['pool'], O_D + 2056, [[POOL_N, 1], [1, 512]])
            B.inc(e.dma_start(dst, src), 'dG', 16)
        tkhr = B.tok('dG')

        @B.on('ve')
        def _(e, tkhr=tkhr):
            B.wait(e, 've', tkhr)
            e.tensor_scalar(hrow, hrow, 1e-5, None, Op.max)
            e.reciprocal(hrow2, hrow)
            e.tensor_scalar(hrow2, hrow2, 127.0, None, Op.mult)
            B.inc(e.tensor_scalar(hrow, hrow, sm[0:1, 180:181], None, Op.mult), 've')
        tk_rows = B.tok('ve')

        @B.on('te')
        def _(e, tk_rows=tk_rows):
            B.wait(e, 'te', tk_rows)
            B.wait(e, 'te', ('ve', B.c['ve']))
            e.matmul(psC[:, 0:512], cst['ones1'][0:1, :], hrow2, start=True, stop=True)
            B.inc(e.matmul(psB[:, 0:512], cst['ones1'][0:1, :], hrow, start=True, stop=True), 'te')
        tkb = B.tok('te')

        @B.on('ve')
        def _(e, tkb=tkb):
            B.wait(e, 've', tkb)
            e.tensor_copy(shqb, psC[:, 0:512])
            B.inc(e.tensor_copy(prow2b, psB[:, 0:512]), 've')
        tk_bc = B.tok('ve')

        bank = [psS[:, 0:512], psS[:, 512:1024], psS[:, 1024:1536], psS[:, 1536:2048],
                psA[:, :], psB[:, :], psC[:, :], psTf]
        tk_q = [None] * 32
        for hc in range(32):
            @B.on('se')
            def _(e, hc=hc):
                B.wait(e, 'se', tk_bc)
                B.wait(e, 'se', ('ve', B.c['ve']))
                B.inc(e.activation(sgn3, h1c[:, hc, 0:512], AF.Sign), 'se')
            tksg = B.tok('se')

            @B.on('ve')
            def _(e, hc=hc, tksg=tksg):
                B.wait(e, 've', tksg, tk_bc)
                e.tensor_tensor(u3, h1c[:, hc, 0:512], shqb, Op.mult)
                e.scalar_tensor_tensor(u16_3, sgn3, st.cvec[:, 0:1], u3, Op.mult, Op.add)
                tk_q[hc] = B.inc(e.tensor_copy(xqh[:, hc, :], u16_3), 've')

            @B.on('sy')
            def _(e, hc=hc, tc=tc):
                B.wait(e, 'sy', tk_use[hc % 4])
                src = bass.AP(st.scr['w2q'].tensor, (128 * hc) * DM,
                              [[DM, 128], [128, 8], [1, 128]])
                tk_ld[hc % 4] = B.inc(e.dma_start(ring[hc % 4][:, :, :], src), 'dW', 16)

            @B.on('te')
            def _(e, hc=hc):
                B.wait(e, 'te', tk_ld[hc % 4], tk_q[hc])
                if hc == 0:
                    B.wait(e, 'te', ('ve', B.c['ve']))
                ins = None
                for d in range(8):
                    ins = e.matmul(bank[d], ring[hc % 4][:, d, :], xqh[:, hc, :],
                                   start=(hc == 0), stop=(hc == 31))
                B.inc(ins, 'te')
            tk_use[hc % 4] = B.tok('te')
        tk_bl2 = B.tok('te')

        for d in range(8):
            @B.on('ve')
            def _(e, d=d):
                B.wait(e, 've', tk_bl2)
                if d >= 1:
                    B.wait(e, 've', ('se', B.c['se']))
                B.inc(e.tensor_tensor(evstg, bank[d], prow2b, Op.mult), 've')
            tkev = B.tok('ve')

            if not hasattr(st, 'tk_g0sp'):
                st.tk_g0sp = []

            @B.on('se')
            def _(e, d=d, tkev=tkev):
                B.wait(e, 'se', tkev)
                if len(st.tk_g0sp) >= 2:
                    B.wait(e, 'se', st.tk_g0sp[-2])
                B.inc(e.activation(g0st[:, 0:512] if d % 2 == 0 else g0st[:, 512:1024],
                                   evstg, AF.Silu), 'se')
            tksl = B.tok('se')

            @B.on('sy')
            def _(e, d=d, tc=tc, tksl=tksl):
                B.wait(e, 'sy', tksl)
                B.inc(e.dma_start(st.scr['g0'][:, d, t0:t0 + 512],
                                  g0st[:, 0:512] if d % 2 == 0 else g0st[:, 512:1024]),
                      'dW', 16)
            st.tk_g0sp.append(B.tok('dW'))
    st.tk_p3_done = ('dW', B.c['dW'])


def ph_p4_combine0(st):
    B = st.B
    R, Rb = st.R, st.Rb
    x1 = R(O_A, N_A, c=8)
    st.X1 = x1
    tk_in = [None] * 8
    for d in range(8):
        o = (d % 2) * 7168
        stX = R(O_B + o, 2048)
        stS = R(O_B + o + 2048, 2048)
        stG = Rb(O_B + o + 4096, 2048)
        tmp = R(O_B + o + 5120, 2048)

        @B.on('sy')
        def _(e, d=d):
            if d == 0:
                B.wait(e, 'sy', st.tk_p3_done)
                B.wait(e, 'sy', ('te', B.c['te']))
                B.wait(e, 'sy', ('ve', B.c['ve']))
                B.wait(e, 'sy', ('se', B.c['se']))
            if d >= 2:
                B.wait(e, 'sy', tk_in[d - 2])
            B.inc(e.dma_start(stX if d % 2 == 0 else R(O_B + 7168, 2048),
                              st.scr['X'][:, d, :]), 'dW', 16) if False else None
            B.inc(e.dma_start(stX, st.scr['X'][:, d, :]), 'dW', 16)
            B.inc(e.dma_start(stS, st.scr['s0'][:, d, :]), 'dW', 16)
            B.inc(e.dma_start(stG, st.scr['g0'][:, d, :]), 'dW', 16)
        tkd = B.tok('dW')

        @B.on('ve')
        def _(e, d=d, stX=stX, stS=stS, stG=stG, tmp=tmp, tkd=tkd):
            B.wait(e, 've', tkd)
            e.tensor_tensor(tmp, stS, stG, Op.mult)
            tk_in[d] = B.inc(e.tensor_tensor(x1[:, d, :], tmp, stX, Op.add), 've')
    st.tk_p4_done = B.tok('ve')


def ph_p5_ln1(st):
    B = st.B
    R, Rb = st.R, st.Rb
    cst, sm = st.cst, st.sm
    psS, psA, psB, psC = st.psS, st.psA, st.psB, st.psC
    x1 = st.X1
    pool = st.pool

    x_tm = R(O_B, 16384, c=16)
    xn16 = Rb(O_C, 16384, c=8)
    st.xn16_1 = xn16
    gatesb = R(O_D + 3344, 128, None).rearrange("p (c n) -> p c n", c=8)
    lgbuf = R(O_D + 3472, 256)
    st.lg = lgbuf

    @B.on('sy')
    def _(e):
        B.wait(e, 'sy', st.tk_p4_done)
        B.inc(e.dma_start(gatesb[:, :, :], st.din['gateT']), 'dW', 16)
        for d in range(8):
            B.inc(e.dma_start(st.scr['X'][:, d, :], x1[:, d, :]), 'dW', 16)
    tk_g = B.tok('dW')

    tk_ev = [None] * 8
    for c in range(8):
        @B.on('te')
        def _(e, c=c):
            B.wait(e, 'te', st.tk_p4_done)
            B.wait(e, 'te', ('ve', B.c['ve']))
            for tt in range(16):
                ins = e.transpose(psS[:, 128 * tt:128 * (tt + 1)],
                                  x1[:, c, 128 * tt:128 * (tt + 1)], cst['identf'][:, :])
            B.inc(ins, 'te')
        tkt = B.tok('te')

        @B.on('ve')
        def _(e, c=c, tkt=tkt):
            B.wait(e, 've', tkt)
            for tt in range(16):
                ins = e.tensor_copy(x_tm[:, tt, 128 * c:128 * (c + 1)],
                                    psS[:, 128 * tt:128 * (tt + 1)])
            B.inc(ins, 've')
        tk_ev[c] = B.tok('ve')
    tk_xtm = B.tok('ve')

    @B.on('ve')
    def _(e):
        B.wait(e, 've', tk_xtm)
        for tt in range(16):
            ins = e.tensor_reduce(sm[:, tt:tt + 1], x_tm[:, tt, :], mybir.AxisListType.X, Op.add)
        B.inc(ins, 've')
        B.bar(e, 've')
        B.inc(e.tensor_scalar(sm[:, 16:32], sm[:, 0:16], -1.0 / DM, None, Op.mult), 've')
    tk_mu = B.tok('ve')

    @B.on('se')
    def _(e):
        B.wait(e, 'se', tk_xtm)
        for tt in range(16):
            B.inc(e.activation(pool[:, O_A + 8192:O_A + 9216], x_tm[:, tt, :], AF.Square,
                               accum_out=sm[:, 32 + tt:33 + tt]), 'se')
    tk_sq = B.tok('se')

    @B.on('ve')
    def _(e):
        B.wait(e, 've', tk_sq, tk_mu)
        e.tensor_tensor(sm[:, 48:64], sm[:, 16:32], sm[:, 16:32], Op.mult)
        e.tensor_scalar(sm[:, 112:128], sm[:, 32:48], 1.0 / DM, None, Op.mult)
        B.inc(e.tensor_tensor(sm[:, 48:64], sm[:, 112:128], sm[:, 48:64], Op.subtract), 've')
    tk_var = B.tok('ve')

    @B.on('se')
    def _(e):
        B.wait(e, 'se', tk_var)
        B.inc(e.activation(sm[:, 48:64], sm[:, 48:64], AF.Sqrt, bias=st.cv['eps'], scale=1.0), 'se')
    tk_sqv = B.tok('se')

    @B.on('ve')
    def _(e):
        B.wait(e, 've', tk_sqv)
        B.inc(e.reciprocal(sm[:, 48:64], sm[:, 48:64]), 've')
        B.bar(e, 've')
        for tt in range(16):
            B.inc(e.tensor_scalar(x_tm[:, tt, :], x_tm[:, tt, :],
                                  sm[:, 16 + tt:17 + tt], sm[:, 48 + tt:49 + tt],
                                  Op.add, Op.mult), 've')
    tk_xn = B.tok('ve')

    @B.on('gp')
    def _(e):
        B.wait(e, 'gp', tk_xn)
        for tt in range(16):
            B.inc(e.dma_start(st.scr['xn16'][128 * tt:128 * (tt + 1), :],
                              x_tm[:, tt, :]), 'dG', 16)
    tk_xnsp = B.tok('dG')

    tk_gacc = None
    for c in range(8):
        @B.on('te')
        def _(e, c=c):
            B.wait(e, 'te', tk_xn)
            B.wait(e, 'te', ('ve', B.c['ve']))
            for tt in range(16):
                ins = e.transpose(psS[:, 128 * tt:128 * (tt + 1)],
                                  x_tm[:, tt, 128 * c:128 * (c + 1)], cst['identf'][:, :])
            B.inc(ins, 'te')
        tkt = B.tok('te')

        @B.on('ve')
        def _(e, c=c, tkt=tkt):
            B.wait(e, 've', tkt)
            if c == 0:
                B.wait(e, 've', tk_g)
            e.tensor_copy(x1[:, c, :], psS[:, :])
            B.inc(e.tensor_copy(xn16[:, c, :], x1[:, c, :]), 've')
        tk_evc = B.tok('ve')

        @B.on('sy')
        def _(e, c=c, tk_evc=tk_evc):
            B.wait(e, 'sy', tk_evc)
            B.inc(e.dma_start(st.scr['xn32'][:, c, :], x1[:, c, :]), 'dW', 16)

    tk_evall = B.tok('ve')

    @B.on('te')
    def _(e):
        B.wait(e, 'te', tk_evall, tk_g)
        for tt in range(16):
            for c in range(8):
                ins = e.matmul(psA[:, 16 * tt:16 * (tt + 1)],
                               x1[:, c, 128 * tt:128 * (tt + 1)],
                               gatesb[:, c, :], start=(c == 0), stop=(c == 7))
        B.inc(ins, 'te')
    tk_gacc = B.tok('te')

    @B.on('ve')
    def _(e):
        B.wait(e, 've', tk_gacc)
        B.inc(e.tensor_copy(st.lg, psA[:, 0:256]), 've')
    st.tk_lg = B.tok('ve')

    @B.on('ve')
    def _(e):
        B.wait(e, 've', st.tk_lg)
        for tt in range(16):
            ins = e.tensor_reduce(sm[:, tt:tt + 1], x_tm[:, tt, :], mybir.AxisListType.X, Op.add)
        B.inc(ins, 've')
        B.bar(e, 've')
        B.inc(e.tensor_scalar(sm[:, 16:32], sm[:, 0:16], -1.0 / DM, None, Op.mult), 've')
    tk_mu2 = B.tok('ve')

    @B.on('se')
    def _(e):
        B.wait(e, 'se', tk_mu2)
        for tt in range(16):
            B.inc(e.activation(pool[:, O_A + 8192:O_A + 9216], x_tm[:, tt, :], AF.Square,
                               accum_out=sm[:, 32 + tt:33 + tt]), 'se')
    tk_sq2 = B.tok('se')

    @B.on('ve')
    def _(e):
        B.wait(e, 've', tk_sq2)
        e.tensor_tensor(sm[:, 48:64], sm[:, 16:32], sm[:, 16:32], Op.mult)
        e.tensor_scalar(sm[:, 112:128], sm[:, 32:48], 1.0 / DM, None, Op.mult)
        B.inc(e.tensor_tensor(sm[:, 48:64], sm[:, 112:128], sm[:, 48:64], Op.subtract), 've')
    tk_var2 = B.tok('ve')

    @B.on('se')
    def _(e):
        B.wait(e, 'se', tk_var2)
        B.inc(e.activation(sm[:, 48:64], sm[:, 48:64], AF.Sqrt, bias=st.cv['eps'], scale=1.0), 'se')
    tk_sqv2 = B.tok('se')

    @B.on('ve')
    def _(e):
        B.wait(e, 've', tk_sqv2, tk_xnsp)
        B.inc(e.reciprocal(sm[:, 48:64], sm[:, 48:64]), 've')
        B.bar(e, 've')
        for tt in range(16):
            B.inc(e.tensor_scalar(x_tm[:, tt, :], x_tm[:, tt, :],
                                  sm[:, 16 + tt:17 + tt], sm[:, 48 + tt:49 + tt],
                                  Op.add, Op.mult), 've')
    tk_xn1 = B.tok('ve')

    tk_sp1 = [None] * 8
    for c in range(8):
        xst = Rb(O_A + 8192 + (c % 2) * 1024, 2048)
        @B.on('te')
        def _(e, c=c):
            B.wait(e, 'te', tk_xn1)
            B.wait(e, 'te', ('ve', B.c['ve']))
            for tt in range(16):
                ins = e.transpose(psS[:, 128 * tt:128 * (tt + 1)],
                                  x_tm[:, tt, 128 * c:128 * (c + 1)], cst['identf'][:, :])
            B.inc(ins, 'te')
        tkt = B.tok('te')

        @B.on('ve')
        def _(e, c=c, xst=xst, tkt=tkt):
            B.wait(e, 've', tkt)
            if c >= 2:
                B.wait(e, 've', tk_sp1[c - 2])
            B.inc(e.tensor_scalar(xst, psS[:, :], cst['ln1g'][:, c:c + 1],
                                  cst['ln1b'][:, c:c + 1], Op.mult, Op.add), 've')
        tke = B.tok('ve')

        @B.on('sy')
        def _(e, c=c, xst=xst, tke=tke):
            B.wait(e, 'sy', tke)
            B.inc(e.dma_start(st.scr['xn1'][:, c, :], xst), 'dW', 16)
        tk_sp1[c] = B.tok('dW')
    st.tk_p5_done = B.tok('dW')


def ph_p6_moe(st):
    B = st.B
    R, Rb, Ri = st.R, st.Rb, st.Ri
    cst, sm = st.cst, st.sm
    psS, psA, psB, psC, psT = st.psS, st.psA, st.psB, st.psC, st.psT
    psTf = psT[:, :].bitcast(F32)
    xn16 = st.xn16_1
    pool = st.pool

    sc_tm = R(O_B, 256)
    sc_et = R(O_B + 256, 2048)
    work = R(O_B + 2304, 2048)
    valb = R(O_B + 4352, 256)
    idxb = pool[:, O_B + 4608:O_B + 4736].bitcast(U16)
    idx16 = Ri(O_B + 4736, 256)
    wv = R(O_B + 4864, 32)
    wsd_res = Rb(O_B + 5120, 16384, c=16)
    eo_fm = R(O_B + 13312, 2048, c=8)
    hsh = Rb(O_A, 16384, c=16)
    ring = [Rb(O_A + 8192 + 512 * i, 1024, c=8) for i in range(8)]
    gst = R(O_A + 12288, 1024)
    shst = [R(O_A + 13312 + 1024 * i, 1024) for i in range(2)]
    xsel = [Rb(O_A, 2048, c=8), Rb(O_A + 1024, 2048, c=8)]
    he = Rb(O_A + 4096, 4096, c=16)
    gst2 = R(O_A + 12288, 256)
    eo_tm = R(O_A + 12544, 2048).rearrange("p (c n) -> p c n", c=2)

    tk_p5ve = ('ve', B.c['ve'])

    @B.on('gp')
    def _(e):
        B.wait(e, 'gp', st.tk_p5_done)
        e.load_library(library_config.mlp)
    tk_lib = ('gp', B.c['gp'])

    @B.on('ve')
    def _(e):
        B.wait(e, 've', st.tk_lg)
        for tt in range(16):
            ins = e.tensor_reduce(sm[:, 128 + tt:129 + tt], st.lg[:, 16 * tt:16 * (tt + 1)],
                                  mybir.AxisListType.X, Op.max)
        B.inc(ins, 've')
        B.bar(e, 've')
        B.inc(e.tensor_scalar(sm[:, 144:160], sm[:, 128:144], -1.0, None, Op.mult), 've')
    tk_nm = B.tok('ve')

    for tt in range(16):
        @B.on('se')
        def _(e, tt=tt):
            B.wait(e, 'se', tk_nm)
            B.inc(e.activation(sc_tm[:, 16 * tt:16 * (tt + 1)], st.lg[:, 16 * tt:16 * (tt + 1)],
                               AF.Exp, bias=sm[:, 144 + tt:145 + tt], scale=1.0,
                               accum_out=sm[:, 160 + tt:161 + tt]), 'se')
    tk_exp = B.tok('se')

    @B.on('ve')
    def _(e):
        B.wait(e, 've', tk_exp)
        B.inc(e.reciprocal(sm[:, 176:192], sm[:, 160:176]), 've')
        B.bar(e, 've')
        for tt in range(16):
            B.inc(e.tensor_scalar(sc_tm[:, 16 * tt:16 * (tt + 1)], sc_tm[:, 16 * tt:16 * (tt + 1)],
                                  sm[:, 176 + tt:177 + tt], None, Op.mult), 've')
    tk_sc = B.tok('ve')

    for g4 in range(4):
        @B.on('te')
        def _(e, g4=g4):
            B.wait(e, 'te', tk_sc)
            B.wait(e, 'te', ('ve', B.c['ve']))
            for k in range(4):
                tt = 4 * g4 + k
                ins = e.transpose(psA[0:16, 128 * k:128 * (k + 1)],
                                  sc_tm[:, 16 * tt:16 * (tt + 1)], cst['identf'][:, :])
            B.inc(ins, 'te')
        tkt = B.tok('te')

        @B.on('ve')
        def _(e, g4=g4, tkt=tkt):
            B.wait(e, 've', tkt)
            B.inc(e.tensor_copy(sc_et[0:16, 512 * g4:512 * (g4 + 1)], psA[0:16, :]), 've')
    tk_et = B.tok('ve')

    @B.on('ve')
    def _(e):
        B.wait(e, 've', tk_et)
        e.tensor_copy(work[0:16, :], sc_et[0:16, :])
        for i in range(32):
            e.max(out=valb[0:16, 8 * i:8 * i + 8], in_=work[0:16, :])
            e.max_index(idxb[0:16, 8 * i:8 * i + 8], valb[0:16, 8 * i:8 * i + 8], work[0:16, :])
            e.match_replace(out=work[0:16, :], in_to_replace=valb[0:16, 8 * i:8 * i + 8],
                            in_values=work[0:16, :], imm_value=-1.0)
        B.inc(e.engine_nop(), 've')
    tk_topk = B.tok('ve')

    @B.on('gp')
    def _(e):
        B.wait(e, 'gp', tk_topk)
        B.inc(e.dma_start(st.scr['idx'], idxb[0:16, :]), 'dG', 16)
        B.inc(e.dma_start(st.scr['val'], valb[0:16, :]), 'dG', 16)
    tk_bnc = B.tok('dG')

    @B.on('gp')
    def _(e):
        B.wait(e, 'gp', tk_bnc)
        with B.nc.allow_non_contiguous_dma(reason="tiny routing relayouts"):
            for ex in range(EXP):
                for rep in range(2):
                    src = bass.AP(st.scr['idx'].tensor, 256 * ex, [[1, 16], [16, 16]])
                    B.inc(e.dma_start(idx16[16 * rep:16 * rep + 16, 16 * ex:16 * (ex + 1)],
                                      src), 'dG', 16)
                srcv = bass.AP(st.scr['val'].tensor, 256 * ex, [[1, 128], [128, 2]])
                B.inc(e.dma_start(wv[:, 2 * ex:2 * (ex + 1)], srcv), 'dG', 16)
    tk_rel = B.tok('dG')

    tk_ld = [None] * 8
    tk_use = [None] * 8

    @B.on('sy')
    def _(e):
        B.wait(e, 'sy', tk_p5ve)
        src = bass.AP(st.din['wsdT'].tensor, 0, [[DM, 128], [DM * 128, 16], [1, DM]])
        B.inc(e.dma_start(wsd_res[:, :, :], src), 'dW', 16)
    tk_wsd = B.tok('dW')

    tk_acc = []
    for th in range(2):
        for shc in range(16):
            u = (2 * (16 * th + shc)) % 8
            @B.on('sy')
            def _(e, th=th, shc=shc, u=u):
                if tk_use[u] is not None:
                    B.wait(e, 'sy', tk_use[u])
                if tk_use[u + 1 if u + 1 < 8 else 0] is not None:
                    B.wait(e, 'sy', tk_use[(u + 1) % 8])
                for wi, wn in enumerate(['wsgT', 'wsuT']):
                    src = bass.AP(st.din[wn].tensor, 128 * shc,
                                  [[FSH, 128], [FSH * 128, 8], [1, 128]])
                    tk_ld[(u + wi) % 8] = B.inc(e.dma_start(ring[(u + wi) % 8][:, :, :], src),
                                                'dW', 16)

            @B.on('te')
            def _(e, th=th, shc=shc, u=u):
                B.wait(e, 'te', tk_ld[u], tk_ld[(u + 1) % 8], st.tk_p5_done)
                B.wait(e, 'te', ('ve', B.c['ve']))
                B.wait(e, 'te', ('se', B.c['se']))
                ins = None
                for half in range(2):
                    for f in range(8):
                        ins = e.matmul(psS[:, 512 * half:512 * (half + 1)],
                                       ring[u][:, f, :],
                                       xn16[:, f, 1024 * th + 512 * half:1024 * th + 512 * (half + 1)],
                                       start=(f == 0), stop=(f == 7))
                for half in range(2):
                    for f in range(8):
                        ins = e.matmul(psS[:, 1024 + 512 * half:1024 + 512 * (half + 1)],
                                       ring[(u + 1) % 8][:, f, :],
                                       xn16[:, f, 1024 * th + 512 * half:1024 * th + 512 * (half + 1)],
                                       start=(f == 0), stop=(f == 7))
                B.inc(ins, 'te')
            tk_use[u] = tk_use[(u + 1) % 8] = B.tok('te')
            tk_mm = B.tok('te')

            @B.on('se')
            def _(e, tk_mm=tk_mm):
                B.wait(e, 'se', tk_mm)
                B.wait(e, 'se', ('ve', B.c['ve']))
                B.inc(e.activation(gst, psS[:, 0:1024], AF.Gelu), 'se')
            tk_g = B.tok('se')

            @B.on('ve')
            def _(e, th=th, shc=shc, tk_g=tk_g):
                B.wait(e, 've', tk_g)
                B.inc(e.tensor_tensor(hsh[:, shc, :], gst, psS[:, 1024:2048], Op.mult), 've')
        tk_hsh = B.tok('ve')

        for tb in range(8):
            half = 1024 * (tb % 2)
            @B.on('te')
            def _(e, th=th, tb=tb, half=half):
                B.wait(e, 'te', tk_hsh, tk_wsd)
                if len(tk_acc) >= 2:
                    B.wait(e, 'te', tk_acc[-2][0])
                ins = None
                for q in range(2):
                    for shc in range(16):
                        ins = e.matmul(psS[:, half + 512 * q:half + 512 * (q + 1)],
                                       hsh[:, shc, 128 * tb:128 * (tb + 1)],
                                       wsd_res[:, shc, 512 * q:512 * (q + 1)],
                                       start=(shc == 0), stop=(shc == 15))
                B.inc(ins, 'te')
            tk_mm = B.tok('te')

            @B.on('ve')
            def _(e, tb=tb, half=half, tk_mm=tk_mm):
                B.wait(e, 've', tk_mm)
                if len(tk_acc) >= 2:
                    B.wait(e, 've', tk_acc[-2][1])
                B.inc(e.tensor_copy(shst[tb % 2], psS[:, half:half + 1024]), 've')
            tk_ev = B.tok('ve')

            @B.on('sy')
            def _(e, th=th, tb=tb, tk_ev=tk_ev):
                B.wait(e, 'sy', tk_ev)
                r0 = 1024 * th + 128 * tb
                B.inc(e.dma_start(st.scr['acc'][r0:r0 + 128, :], shst[tb % 2]), 'dW', 16)
            tk_acc.append((tk_ev, B.tok('dW')))
    tk_shared_done = B.tok('dW')
    tk_hsh_free = ('te', B.c['te'])

    tk_gather = [None, None]
    tk_xuse = [None, None]
    tk_scat = None
    tk_eo = None
    st._tk_wd_prev = None

    def gather(ex):
        @B.on('gp')
        def _(e, ex=ex):
            B.wait(e, 'gp', tk_rel, tk_hsh_free)
            if tk_xuse[ex % 2] is not None:
                B.wait(e, 'gp', tk_xuse[ex % 2])
            tk_gather[ex % 2] = B.inc(
                e.dma_gather(xsel[ex % 2][:, :, :], st.scr['xn16'],
                             idx16[:, 16 * ex:16 * (ex + 1)], CAP, CAP, DM,
                             transpose=True), 'dG', 16)

    gather(0)
    for ex in range(EXP):
        if ex + 1 < EXP:
            gather(ex + 1)
        for oc in range(16):
            u = (2 * oc) % 8
            @B.on('sy')
            def _(e, ex=ex, oc=oc, u=u):
                for wi in range(2):
                    if tk_use[(u + wi) % 8] is not None:
                        B.wait(e, 'sy', tk_use[(u + wi) % 8])
                for wi, wn in enumerate(['wgT', 'wuT']):
                    src = bass.AP(st.din[wn].tensor, ex * DM * FSH + 128 * oc,
                                  [[FSH, 128], [FSH * 128, 8], [1, 128]])
                    tk_ld[(u + wi) % 8] = B.inc(e.dma_start(ring[(u + wi) % 8][:, :, :], src),
                                                'dW', 16)

            @B.on('te')
            def _(e, ex=ex, oc=oc, u=u):
                B.wait(e, 'te', tk_ld[u], tk_ld[(u + 1) % 8], tk_gather[ex % 2])
                B.wait(e, 'te', ('ve', B.c['ve']))
                B.wait(e, 'te', ('se', B.c['se']))
                ins = None
                for f in range(8):
                    ins = e.matmul(psA[:, 0:256], ring[u][:, f, :], xsel[ex % 2][:, f, :],
                                   start=(f == 0), stop=(f == 7))
                for f in range(8):
                    ins = e.matmul(psB[:, 0:256], ring[(u + 1) % 8][:, f, :], xsel[ex % 2][:, f, :],
                                   start=(f == 0), stop=(f == 7))
                B.inc(ins, 'te')
            tk_use[u] = tk_use[(u + 1) % 8] = B.tok('te')
            if oc == 15:
                tk_xuse[ex % 2] = B.tok('te')
            tk_mm = B.tok('te')

            @B.on('se')
            def _(e, tk_mm=tk_mm):
                B.wait(e, 'se', tk_mm)
                B.wait(e, 'se', ('ve', B.c['ve']))
                B.inc(e.activation(gst2, psA[:, 0:256], AF.Gelu), 'se')
            tk_g = B.tok('se')

            @B.on('ve')
            def _(e, ex=ex, oc=oc, tk_g=tk_g):
                B.wait(e, 've', tk_g)
                if oc == 0 and st._tk_wd_prev is not None:
                    B.wait(e, 've', st._tk_wd_prev)
                B.inc(e.tensor_tensor(he[:, oc, :], gst2, psB[:, 0:256], Op.mult), 've')
        tk_he = B.tok('ve')

        bank8 = [psS[:, 0:512], psS[:, 512:1024], psS[:, 1024:1536], psS[:, 1536:2048],
                 psA, psB, psC, psTf]
        for oc in range(16):
            u = oc % 8
            @B.on('sy')
            def _(e, ex=ex, oc=oc, u=u):
                if tk_use[u] is not None:
                    B.wait(e, 'sy', tk_use[u])
                src = bass.AP(st.din['wdT'].tensor, ex * FSH * DM + 128 * oc * DM,
                              [[DM, 128], [128, 8], [1, 128]])
                tk_ld[u] = B.inc(e.dma_start(ring[u][:, :, :], src), 'dW', 16)

            @B.on('te')
            def _(e, ex=ex, oc=oc, u=u):
                B.wait(e, 'te', tk_ld[u], tk_he)
                if oc == 0:
                    B.wait(e, 'te', ('ve', B.c['ve']))
                ins = None
                for d in range(8):
                    ins = e.matmul(bank8[d][:, 0:256], ring[u][:, d, :], he[:, oc, :],
                                   start=(oc == 0), stop=(oc == 15))
                B.inc(ins, 'te')
            tk_use[u] = B.tok('te')
        tk_wd = B.tok('te')
        st._tk_wd_prev = tk_wd

        @B.on('ve')
        def _(e, ex=ex):
            B.wait(e, 've', tk_wd)
            if tk_eo is not None:
                B.wait(e, 've', tk_eo)
            for d in range(8):
                ins = e.tensor_copy(eo_fm[:, d, :], bank8[d][:, 0:256])
            B.inc(ins, 've')
        tk_ef = B.tok('ve')

        @B.on('te')
        def _(e, ex=ex, tk_ef=tk_ef):
            B.wait(e, 'te', tk_ef)
            for d in range(8):
                for sc2 in range(2):
                    ins = e.transpose(psS[:, 128 * (2 * d + sc2):128 * (2 * d + sc2 + 1)],
                                      eo_fm[:, d, 128 * sc2:128 * (sc2 + 1)],
                                      cst['identf'][:, :])
            B.inc(ins, 'te')
        tk_tp = B.tok('te')

        @B.on('ve')
        def _(e, ex=ex, tk_tp=tk_tp):
            B.wait(e, 've', tk_tp, tk_rel, tk_shared_done)
            if tk_scat is not None:
                B.wait(e, 've', tk_scat)
            for d in range(8):
                for sc2 in range(2):
                    ins = e.tensor_scalar(eo_tm[:, sc2, 128 * d:128 * (d + 1)],
                                          psS[:, 128 * (2 * d + sc2):128 * (2 * d + sc2 + 1)],
                                          wv[:, 2 * ex + sc2:2 * ex + sc2 + 1], None, Op.mult)
            B.inc(ins, 've')
        tk_eo = B.tok('ve')

        @B.on('gp')
        def _(e, ex=ex, tk_eo=tk_eo):
            B.wait(e, 'gp', tk_eo, tk_shared_done)
            B.inc(e.dma_scatter_add(st.scr['acc'], eo_tm[:, :, :],
                                    idx16[:, 16 * ex:16 * (ex + 1)], CAP, CAP, DM), 'dG', 16)
        tk_scat = B.tok('dG')

    g1 = Rb(O_C, 16384, c=8)
    st.g1 = g1
    ast = [R(O_A + 8192 + 1024 * i, 1024) for i in range(2)]
    bst = [Rb(O_A + 10240 + 512 * i, 1024) for i in range(2)]
    tk_tt = [None, None]
    for tt in range(16):
        @B.on('sy')
        def _(e, tt=tt):
            B.wait(e, 'sy', tk_scat)
            if tk_tt[tt % 2] is not None:
                B.wait(e, 'sy', tk_tt[tt % 2])
            B.inc(e.dma_start(ast[tt % 2], st.scr['acc'][128 * tt:128 * (tt + 1), :]), 'dW', 16)
        tk_a = B.tok('dW')

        @B.on('se')
        def _(e, tt=tt, tk_a=tk_a):
            B.wait(e, 'se', tk_a)
            B.wait(e, 'se', ('te', B.c['te']))
            B.inc(e.activation(bst[tt % 2], ast[tt % 2], AF.Silu), 'se')
        tk_s = B.tok('se')

        @B.on('te')
        def _(e, tt=tt, tk_s=tk_s):
            B.wait(e, 'te', tk_s)
            B.wait(e, 'te', ('ve', B.c['ve']))
            for c in range(8):
                ins = e.transpose(psT[:, 128 * c:128 * (c + 1)],
                                  bst[tt % 2].rearrange("p (c n) -> p c n", c=8)[:, c, :],
                                  cst['identb'][:, :])
            B.inc(ins, 'te')
        tk_tt[tt % 2] = B.tok('te')
        tkt = B.tok('te')

        @B.on('ve')
        def _(e, tt=tt, tkt=tkt):
            B.wait(e, 've', tkt)
            for c in range(8):
                ins = e.tensor_copy(g1[:, c, 128 * tt:128 * (tt + 1)],
                                    psT[:, 128 * c:128 * (c + 1)])
            B.inc(ins, 've')
    st.tk_p6_done = B.tok('ve')


def ph_p7_l1_attn(st):
    import os
    DIS = os.environ.get('P7DIS', '').split(',')
    B = st.B
    R, Rb, Ri = st.R, st.Rb, st.Ri
    cst, sm = st.cst, st.sm
    psS, psA, psB, psC, psT = st.psS, st.psA, st.psB, st.psC, st.psT
    psTf = psT[:, :].bitcast(F32)
    pool = st.pool

    xn1 = Rb(O_A, 16384, c=8)
    gsig = Rb(O_A + 8192, 16384, c=8)
    xmid = R(O_A, N_A, c=8)
    attn_full = Rb(O_B, 16384, c=8)
    q2 = Rb(O_B + 8192, 2048)
    k2 = Rb(O_B + 9216, 2048)
    v2 = Rb(O_B + 10240, 2048)
    vtm = Rb(O_B + 11264, 2048, c=16)
    aqe = Rb(O_B + 12288, 2048)
    attnT = Rb(O_B + 13312, 2048, c=16)
    ubuf = R(O_B + 14336, 2048)
    cw_res = pool[:, O_D + 3072:O_D + 3584].bitcast(BF16)
    drive = R(O_D, 2048)
    states = pool[:, O_D + 2048:O_D + 3072].bitcast(BF16)
    ring = [Rb(O_D + 3584 + 512 * i, 1024, c=8) for i in range(3)]
    st.xmid = xmid

    tk_p6ve = ('ve', B.c['ve'])
    tk_p6te = ('te', B.c['te'])

    @B.on('sy')
    def _(e):
        B.wait(e, 'sy', st.tk_p6_done, tk_p6te)
        B.wait(e, 'sy', ('se', B.c['se']))
        for c in range(8):
            B.inc(e.dma_start(xn1[:, c, :], st.scr['xn1'][:, c, :]), 'dW', 16)
        B.inc(e.dma_start(cw_res, st.din['cwT']), 'dW', 16)
    tk_xn1 = B.tok('dW')

    tk_ld = [None] * 3
    tk_use = [None] * 3

    def load_pair(p, slot):
        @B.on('sy')
        def _(e, p=p, slot=slot):
            if tk_use[slot] is not None:
                B.wait(e, 'sy', tk_use[slot])
            pass

    for p in range(8):
        for wi in range(3):
            @B.on('sy')
            def _(e, p=p, wi=wi):
                if tk_use[wi] is not None:
                    B.wait(e, 'sy', tk_use[wi])
                if p == 0 and wi == 0:
                    B.wait(e, 'sy', st.tk_p6_done)
                src = bass.AP(st.din['wqkvT'].tensor, 1024 * wi + 128 * p,
                              [[3 * DM, 128], [3 * DM * 128, 8], [1, 128]])
                tk_ld[wi] = B.inc(e.dma_start(ring[wi][:, :, :], src), 'dW', 16)

        for wi, dst in enumerate([q2, k2, v2]):
            @B.on('te')
            def _(e, p=p, wi=wi):
                B.wait(e, 'te', tk_ld[wi], tk_xn1)
                B.wait(e, 'te', ('ve', B.c['ve']))
                ins = None
                for n in range(4):
                    for f in range(8):
                        ins = e.matmul(psS[:, 512 * n:512 * (n + 1)], ring[wi][:, f, :],
                                       xn1[:, f, 512 * n:512 * (n + 1)],
                                       start=(f == 0), stop=(f == 7))
                B.inc(ins, 'te')
            tk_use[wi] = B.tok('te')
            tk_mm = B.tok('te')

            @B.on('ve')
            def _(e, dst=dst, tk_mm=tk_mm):
                B.wait(e, 've', tk_mm)
                B.inc(e.tensor_copy(dst, psS[:, :]), 've')

        for g in range(2):
            @B.on('te')
            def _(e, g=g):
                B.wait(e, 'te', ('ve', B.c['ve']))
                for kt in range(8):
                    ins = e.transpose(psT[:, 128 * kt:128 * (kt + 1)],
                                      v2[:, 128 * (8 * g + kt):128 * (8 * g + kt + 1)],
                                      cst['identb'][:, :])
                B.inc(ins, 'te')
            tkv = B.tok('te')

            @B.on('ve')
            def _(e, g=g, tkv=tkv):
                B.wait(e, 've', tkv)
                for kt in range(8):
                    ins = e.tensor_copy(vtm[:, 8 * g + kt, :], psT[:, 128 * kt:128 * (kt + 1)])
                B.inc(ins, 've')

        for hl in range(2):
            hb = 64 * hl
            for c in range(8):
                for qt in range(2):
                    qg = 256 * c + 128 * qt
                    if c == 0:
                        k0 = 0
                        nk = 128 * (qt + 1)
                        moff = 384 - nk
                    else:
                        k0 = 256 * (c - 1) + 128 * qt
                        nk = 384
                        moff = 0
                    @B.on('te')
                    def _(e, hb=hb, qg=qg, k0=k0, nk=nk):
                        B.wait(e, 'te', ('se', B.c['se']))
                        B.wait(e, 'te', ('ve', B.c['ve']))
                        B.inc(e.matmul(psA[:, 0:nk], q2[hb:hb + 64, qg:qg + 128],
                                       k2[hb:hb + 64, k0:k0 + nk], start=True, stop=True),
                              'te')
                    tk_sc = B.tok('te')

                    @B.on('ve')
                    def _(e, nk=nk, moff=moff, tk_sc=tk_sc):
                        B.wait(e, 've', tk_sc)
                        B.inc(e.tensor_tensor(psA[:, 0:nk], psA[:, 0:nk],
                                              cst['maskL1'][:, moff:moff + nk], Op.add), 've')
                    tk_msk = B.tok('ve')

                    @B.on('se')
                    def _(e, nk=nk, tk_msk=tk_msk):
                        B.wait(e, 'se', tk_msk)
                        B.inc(e.activation(aqe[:, 0:nk], psA[:, 0:nk], AF.Exp,
                                           bias=st.cv['neg8'], scale=0.125,
                                           accum_out=sm[:, 128:129]), 'se')
                    tk_exp = B.tok('se')

                    @B.on('ve')
                    def _(e, nk=nk, tk_exp=tk_exp):
                        B.wait(e, 've', tk_exp)
                        e.reciprocal(sm[:, 129:130], sm[:, 128:129])
                        B.inc(e.tensor_scalar(aqe[:, 0:nk], aqe[:, 0:nk],
                                              sm[:, 129:130], None, Op.mult), 've')
                    tk_nrm = B.tok('ve')

                    nkt = nk // 128
                    @B.on('te')
                    def _(e, nkt=nkt, tk_nrm=tk_nrm):
                        B.wait(e, 'te', tk_nrm)
                        B.wait(e, 'te', ('ve', B.c['ve']))
                        for kt in range(nkt):
                            ins = e.transpose(psT[:, 128 * kt:128 * (kt + 1)],
                                              aqe[:, 128 * kt:128 * (kt + 1)],
                                              cst['identb'][:, :])
                        B.inc(ins, 'te')
                    tk_tp = B.tok('te')

                    @B.on('ve')
                    def _(e, nkt=nkt, tk_tp=tk_tp):
                        B.wait(e, 've', tk_tp)
                        for kt in range(nkt):
                            ins = e.tensor_copy(attnT[:, kt, :], psT[:, 128 * kt:128 * (kt + 1)])
                        B.inc(ins, 've')
                    tk_aT = B.tok('ve')

                    @B.on('te')
                    def _(e, hb=hb, c=c, qt=qt, k0=k0, nkt=nkt, tk_aT=tk_aT):
                        B.wait(e, 'te', tk_aT)
                        if qt == 0:
                            B.wait(e, 'te', ('ve', B.c['ve']))
                        for kt in range(nkt):
                            vi = (k0 // 128) + kt
                            ins = e.matmul(psB[hb:hb + 64, 128 * qt:128 * (qt + 1)],
                                           vtm[:, vi, hb:hb + 64], attnT[:, kt, :],
                                           start=(kt == 0), stop=(kt == nkt - 1),
                                           tile_position=(0, hb))
                        B.inc(ins, 'te')
                    tk_av = B.tok('te')

                @B.on('ve')
                def _(e, p=p, hb=hb, c=c, tk_av=tk_av):
                    B.wait(e, 've', tk_av)
                    B.inc(e.tensor_copy(attn_full[hb:hb + 64, p, 256 * c:256 * (c + 1)],
                                        psB[hb:hb + 64, 0:256]), 've')
    tk_attn = B.tok('ve')

    for j in range(9):
        wi = j % 3
        @B.on('sy')
        def _(e, j=j, wi=wi):
            B.wait(e, 'sy', tk_use[wi])
            src = bass.AP(st.din['wugT'].tensor, 128 * j,
                          [[1152, 128], [1152 * 128, 8], [1, 128]])
            tk_ld[wi] = B.inc(e.dma_start(ring[wi][:, :, :], src), 'dW', 16)

        @B.on('te')
        def _(e, j=j, wi=wi):
            B.wait(e, 'te', tk_ld[wi])
            B.wait(e, 'te', ('ve', B.c['ve']))
            B.wait(e, 'te', ('se', B.c['se']))
            ins = None
            for n in range(4):
                for f in range(8):
                    ins = e.matmul(psS[:, 512 * n:512 * (n + 1)], ring[wi][:, f, :],
                                   xn1[:, f, 512 * n:512 * (n + 1)],
                                   start=(f == 0), stop=(f == 7))
            B.inc(ins, 'te')
        tk_use[wi] = B.tok('te')
        tk_mm = B.tok('te')

        if j == 0:
            @B.on('ve')
            def _(e, tk_mm=tk_mm):
                B.wait(e, 've', tk_mm)
                B.inc(e.tensor_scalar(ubuf, psS[:, :], cst['bug'][:, 0:1], None, Op.add), 've')
        else:
            @B.on('se')
            def _(e, j=j, tk_mm=tk_mm):
                B.wait(e, 'se', tk_mm)
                B.inc(e.activation(gsig[:, j - 1, :], psS[:, :], AF.Sigmoid,
                                   bias=cst['bug'][:, j:j + 1], scale=1.0), 'se')
    tk_ug = B.tok('se')

    @B.on('sy')
    def _(e):
        B.wait(e, 'sy', tk_use[0])
        src = bass.AP(st.din['bwT'].tensor, 0, [[SD, 128], [SD * 128, 8], [1, 128]])
        tk_ld[0] = B.inc(e.dma_start(ring[0][:, :, :], src), 'dW', 16)

    @B.on('te')
    def _(e):
        B.wait(e, 'te', tk_ld[0])
        B.wait(e, 'te', ('ve', B.c['ve']))
        B.wait(e, 'te', ('se', B.c['se']))
        ins = None
        for n in range(4):
            for f in range(8):
                ins = e.matmul(psS[:, 512 * n:512 * (n + 1)], ring[0][:, f, :],
                               xn1[:, f, 512 * n:512 * (n + 1)],
                               start=(f == 0), stop=(f == 7))
        B.inc(ins, 'te')
    tk_use[0] = B.tok('te')
    tk_bw = B.tok('te')

    @B.on('ve')
    def _(e):
        B.wait(e, 've', tk_bw)
        B.inc(e.tensor_tensor(drive, psS[:, :], ubuf, Op.add), 've')
        B.bar(e, 've')
        B.inc(e.tensor_tensor_scan(states, cst['Avec'][:, 0:1].to_broadcast([128, T]),
                                   drive, 0.0, Op.mult, Op.add), 've')
    tk_scan = B.tok('ve')

    st.attn_full1 = attn_full
    st.gsig = gsig
    st.states_ap = states
    st.drive_ap = drive
    if 'combine' in DIS:
        st.tk_p7_done = B.tok('ve')
        return
    st1 = R(O_B + 12288, 2048)
    st2 = R(O_B + 8192, 2048)
    xnst = [R(O_B + 10240, 2048), R(O_B + 14336, 2048)]
    tk_xnld = [None, None]
    tk_cmb = [None, None]
    for d in range(8):
        @B.on('sy')
        def _(e, d=d):
            B.wait(e, 'sy', tk_attn, tk_scan)
            if tk_cmb[d % 2] is not None:
                B.wait(e, 'sy', tk_cmb[d % 2])
            tk_xnld[d % 2] = B.inc(e.dma_start(xnst[d % 2], st.scr['xn32'][:, d, :]), 'dW', 16)

        wi = d % 3
        @B.on('sy')
        def _(e, d=d, wi=wi):
            B.wait(e, 'sy', tk_use[wi])
            src = bass.AP(st.din['l1_woT'].tensor, 128 * d, [[DM, 128], [DM * 128, 8], [1, 128]])
            tk_ld[wi] = B.inc(e.dma_start(ring[wi][:, :, :], src), 'dW', 16)

        @B.on('te')
        def _(e, d=d, wi=wi):
            B.wait(e, 'te', tk_ld[wi], tk_attn, tk_scan)
            B.wait(e, 'te', ('ve', B.c['ve']))
            ins = None
            for n in range(4):
                for f in range(8):
                    ins = e.matmul(psS[:, 512 * n:512 * (n + 1)], ring[wi][:, f, :],
                                   attn_full[:, f, 512 * n:512 * (n + 1)],
                                   start=(f == 0), stop=(f == 7))
            ybank = [psA, psB, psC, psTf]
            for n in range(4):
                ins = e.matmul(ybank[n][:, 0:512], cw_res[:, 128 * d:128 * (d + 1)],
                               states[:, 512 * n:512 * (n + 1)], start=True, stop=True)
            B.inc(ins, 'te')
        tk_use[wi] = B.tok('te')
        tk_mm = B.tok('te')

        @B.on('ve')
        def _(e, d=d, tk_mm=tk_mm):
            B.wait(e, 've', tk_mm, tk_ug, tk_xnld[d % 2])
            ybank = [psA, psB, psC, psTf]
            e.tensor_scalar(st1, psS[:, :], cst['bO'][:, d:d + 1], None, Op.add)
            for n in range(4):
                e.tensor_tensor(st2[:, 512 * n:512 * (n + 1)], st1[:, 512 * n:512 * (n + 1)],
                                ybank[n][:, 0:512], Op.subtract)
            e.tensor_tensor(st2, st2, gsig[:, d, :], Op.mult)
            for n in range(4):
                e.tensor_tensor(st2[:, 512 * n:512 * (n + 1)], st2[:, 512 * n:512 * (n + 1)],
                                ybank[n][:, 0:512], Op.add)
            tk_cmb[d % 2] = B.inc(e.tensor_tensor(xmid[:, d, :], st2, xnst[d % 2], Op.add), 've')
    tk_xmid = B.tok('ve')
    if 'mlp' in DIS:
        st.tk_p7_done = tk_xmid
        return

    x_tm = R(O_B, 16384, c=16)
    for c in range(8):
        @B.on('te')
        def _(e, c=c):
            B.wait(e, 'te', tk_xmid)
            B.wait(e, 'te', ('ve', B.c['ve']))
            for tt in range(16):
                ins = e.transpose(psS[:, 128 * tt:128 * (tt + 1)],
                                  xmid[:, c, 128 * tt:128 * (tt + 1)], cst['identf'][:, :])
            B.inc(ins, 'te')
        tkt = B.tok('te')

        @B.on('ve')
        def _(e, c=c, tkt=tkt):
            B.wait(e, 've', tkt)
            for tt in range(16):
                ins = e.tensor_copy(x_tm[:, tt, 128 * c:128 * (c + 1)],
                                    psS[:, 128 * tt:128 * (tt + 1)])
            B.inc(ins, 've')
    tk_tm = B.tok('ve')

    @B.on('ve')
    def _(e):
        B.wait(e, 've', tk_tm)
        for tt in range(16):
            ins = e.tensor_reduce(sm[:, tt:tt + 1], x_tm[:, tt, :], mybir.AxisListType.X, Op.add)
        B.inc(ins, 've')
        B.bar(e, 've')
        B.inc(e.tensor_scalar(sm[:, 16:32], sm[:, 0:16], -1.0 / DM, None, Op.mult), 've')
    tk_mu = B.tok('ve')

    @B.on('se')
    def _(e):
        B.wait(e, 'se', tk_mu)
        for tt in range(16):
            B.inc(e.activation(pool[:, O_D:O_D + 1024], x_tm[:, tt, :],
                               AF.Square, accum_out=sm[:, 32 + tt:33 + tt]), 'se')
    tk_sq = B.tok('se')

    @B.on('ve')
    def _(e):
        B.wait(e, 've', tk_sq)
        e.tensor_tensor(sm[:, 48:64], sm[:, 16:32], sm[:, 16:32], Op.mult)
        e.tensor_scalar(sm[:, 112:128], sm[:, 32:48], 1.0 / DM, None, Op.mult)
        B.inc(e.tensor_tensor(sm[:, 48:64], sm[:, 112:128], sm[:, 48:64], Op.subtract), 've')
    tk_var = B.tok('ve')

    @B.on('se')
    def _(e):
        B.wait(e, 'se', tk_var)
        B.inc(e.activation(sm[:, 48:64], sm[:, 48:64], AF.Sqrt, bias=st.cv['eps'], scale=1.0), 'se')
    tk_sqv = B.tok('se')

    @B.on('ve')
    def _(e):
        B.wait(e, 've', tk_sqv)
        B.inc(e.reciprocal(sm[:, 48:64], sm[:, 48:64]), 've')
        B.bar(e, 've')
        for tt in range(16):
            B.inc(e.tensor_scalar(x_tm[:, tt, :], x_tm[:, tt, :],
                                  sm[:, 16 + tt:17 + tt], sm[:, 48 + tt:49 + tt],
                                  Op.add, Op.mult), 've')
    tk_h2n = B.tok('ve')

    h2stg = [pool[:, O_D + i * 1024:O_D + i * 1024 + 1024].bitcast(BF16) for i in range(2)]
    tk_h2sp = [None, None]
    for c in range(8):
        @B.on('te')
        def _(e, c=c):
            B.wait(e, 'te', tk_h2n)
            B.wait(e, 'te', ('ve', B.c['ve']))
            for tt in range(16):
                ins = e.transpose(psS[:, 128 * tt:128 * (tt + 1)],
                                  x_tm[:, tt, 128 * c:128 * (c + 1)], cst['identf'][:, :])
            B.inc(ins, 'te')
        tkt = B.tok('te')

        @B.on('ve')
        def _(e, c=c, tkt=tkt):
            B.wait(e, 've', tkt)
            if tk_h2sp[c % 2] is not None:
                B.wait(e, 've', tk_h2sp[c % 2])
            B.inc(e.tensor_scalar(h2stg[c % 2], psS[:, :], cst['ln2g'][:, c:c + 1],
                                  cst['ln2b'][:, c:c + 1], Op.mult, Op.add), 've')
        tke = B.tok('ve')

        @B.on('sy')
        def _(e, c=c, tke=tke):
            B.wait(e, 'sy', tke)
            B.inc(e.dma_start(st.scr['h2'][:, c, :], h2stg[c % 2]), 'dW', 16)
        tk_h2sp[c % 2] = B.tok('dW')
    tk_h2 = B.tok('dW')

    h3q = Rb(O_B, 16384, c=32)
    h2buf = [Rb(O_D + 0, 4096, c=8), Rb(O_D + 2048, 4096, c=8)]
    st.h3q = h3q
    st.h2buf1 = h2buf[1]
    bank8 = [psS[:, 0:512], psS[:, 512:1024], psS[:, 1024:1536], psS[:, 1536:2048],
             psA, psB, psC, psTf]
    mring = [Rb(O_B + 8192 + 512 * i, 1024, c=8) for i in range(4)]
    tk_mld = [None] * 4
    tk_muse = [None] * 4
    tk_h2ld = [None, None]
    for q in range(4):
        @B.on('sy')
        def _(e, q=q):
            B.wait(e, 'sy', tk_h2)
            if tk_h2ld[q % 2] is not None:
                B.wait(e, 'sy', ('te', B.c['te']))
            for f in range(8):
                ins = e.dma_start(h2buf[q % 2][:, f, :],
                                  st.scr['h2'][:, f, 512 * q:512 * (q + 1)])
                B.inc(ins, 'dW', 16)
        tk_h2ld[q % 2] = B.tok('dW')

        for hc in range(32):
            u = hc % 4
            @B.on('sy')
            def _(e, q=q, hc=hc, u=u):
                if tk_muse[u] is not None:
                    B.wait(e, 'sy', tk_muse[u])
                if q == 0 and hc == 0:
                    B.wait(e, 'sy', ('te', B.c['te']))
                src = bass.AP(st.din['mw1T'].tensor, 128 * hc,
                              [[MLPH, 128], [MLPH * 128, 8], [1, 128]])
                tk_mld[u] = B.inc(e.dma_start(mring[u][:, :, :], src), 'dW', 16)

            pb = [psA, psB][hc % 2]
            @B.on('te')
            def _(e, q=q, hc=hc, u=u, pb=pb):
                B.wait(e, 'te', tk_mld[u], tk_h2ld[q % 2])
                if hc == 0:
                    B.wait(e, 'te', ('ve', B.c['ve']))
                if hc >= 2:
                    B.wait(e, 'te', ('se', B.c['se']))
                ins = None
                for f in range(8):
                    ins = e.matmul(pb[:, 0:512], mring[u][:, f, :], h2buf[q % 2][:, f, :],
                                   start=(f == 0), stop=(f == 7))
                B.inc(ins, 'te')
            tk_muse[u] = B.tok('te')
            tk_mm = B.tok('te')

            @B.on('se')
            def _(e, q=q, hc=hc, pb=pb, tk_mm=tk_mm):
                B.wait(e, 'se', tk_mm)
                B.inc(e.activation(h3q[:, hc, :], pb[:, 0:512], AF.Gelu,
                                   bias=cst['mb1'][:, hc:hc + 1], scale=1.0), 'se')
        tk_h3 = B.tok('se')

        for hc in range(32):
            u = hc % 4
            @B.on('sy')
            def _(e, q=q, hc=hc, u=u):
                B.wait(e, 'sy', tk_muse[u])
                src = bass.AP(st.din['mw2T'].tensor, (128 * hc) * DM,
                              [[DM, 128], [128, 8], [1, 128]])
                tk_mld[u] = B.inc(e.dma_start(mring[u][:, :, :], src), 'dW', 16)

            @B.on('te')
            def _(e, q=q, hc=hc, u=u):
                B.wait(e, 'te', tk_mld[u], tk_h3)
                if hc == 0:
                    B.wait(e, 'te', ('ve', B.c['ve']))
                ins = None
                for d in range(8):
                    ins = e.matmul(bank8[d][:, 0:512], mring[u][:, d, :], h3q[:, hc, :],
                                   start=(hc == 0), stop=(hc == 31))
                B.inc(ins, 'te')
            tk_muse[u] = B.tok('te')
        tk_w2 = B.tok('te')

        @B.on('ve')
        def _(e, q=q, tk_w2=tk_w2):
            B.wait(e, 've', tk_w2)
            for d in range(8):
                ins = e.scalar_tensor_tensor(xmid[:, d, 512 * q:512 * (q + 1)],
                                             bank8[d][:, 0:512], cst['mb2'][:, d:d + 1],
                                             xmid[:, d, 512 * q:512 * (q + 1)],
                                             Op.add, Op.add)
            B.inc(ins, 've')
    st.tk_p7_done = B.tok('ve')


def ph_p8_final(st):
    B = st.B
    R, Rb = st.R, st.Rb
    cst = st.cst
    psS = st.psS
    pool = st.pool
    xmid = st.xmid
    g1 = st.g1

    Xst = [R(O_B + 2048 * i, 2048) for i in range(2)]
    tst = R(O_B + 4096, 2048)
    ost = [R(O_B + 6144 + 1024 * i, 1024) for i in range(2)]
    tk_x = [None, None]
    tk_c = [None, None]
    for d in range(8):
        @B.on('sy')
        def _(e, d=d):
            B.wait(e, 'sy', st.tk_p7_done)
            if tk_c[d % 2] is not None:
                B.wait(e, 'sy', tk_c[d % 2])
            tk_x[d % 2] = B.inc(e.dma_start(Xst[d % 2], st.scr['X'][:, d, :]), 'dW', 16)

        @B.on('ve')
        def _(e, d=d):
            B.wait(e, 've', st.tk_p7_done, tk_x[d % 2], st.tk_p6_done)
            e.tensor_tensor(tst, xmid[:, d, :], g1[:, d, :], Op.mult)
            tk_c[d % 2] = B.inc(e.tensor_tensor(xmid[:, d, :], tst, Xst[d % 2], Op.add), 've')
    tk_x2 = B.tok('ve')

    tk_out = [None, None]
    for tt in range(16):
        @B.on('te')
        def _(e, tt=tt):
            B.wait(e, 'te', tk_x2)
            B.wait(e, 'te', ('ve', B.c['ve']))
            for d in range(8):
                ins = e.transpose(psS[:, 128 * d:128 * (d + 1)],
                                  xmid[:, d, 128 * tt:128 * (tt + 1)], cst['identf'][:, :])
            B.inc(ins, 'te')
        tkt = B.tok('te')

        @B.on('ve')
        def _(e, tt=tt, tkt=tkt):
            B.wait(e, 've', tkt)
            if tk_out[tt % 2] is not None:
                B.wait(e, 've', tk_out[tt % 2])
            B.inc(e.tensor_copy(ost[tt % 2], psS[:, 0:1024]), 've')
        tke = B.tok('ve')

        @B.on('sy')
        def _(e, tt=tt, tke=tke):
            B.wait(e, 'sy', tke)
            B.inc(e.dma_start(st.dout[128 * tt:128 * (tt + 1), :], ost[tt % 2]), 'dO', 16)
        tk_out[tt % 2] = B.tok('dO')

    @B.on('sy')
    def _(e):
        e.wait_ge(B.S['dO'], B.c['dO'])



_BUILD_CACHE = {}


def kernel(**inputs):
    x = np.ascontiguousarray(np.asarray(inputs['L0_attn_x'], np.float32))
    shared = host_prepare({k: v for k, v in inputs.items() if k != 'L0_attn_x'})
    B = build_program(upto=99)
    in_maps = []
    for c in range(NCORE):
        m = {'x': np.ascontiguousarray(x[c])}
        m.update(shared)
        in_maps.append(m)
    res = run_bass_kernel_spmd(B.nc, in_maps, core_ids=list(range(NCORE)))
    out = np.stack([np.asarray(r['out'], np.float32) for r in res.results], axis=0)
    return out


# version 04
# speedup vs baseline: 1.3505x; 1.3505x over previous
import numpy as np
import ml_dtypes
import concourse.bass as bass
import concourse.mybir as mybir
from concourse.alu_op_type import AluOpType as Op
from concourse.bass_utils import run_bass_kernel_spmd
from concourse import library_config
from concourse.library_overlay import lower_extended_insts

F32 = mybir.dt.float32
BF16 = mybir.dt.bfloat16
I16 = mybir.dt.int16
U16 = mybir.dt.uint16
AF = mybir.ActivationFunctionType
nbf16 = ml_dtypes.bfloat16

T = 2048
DM = 1024
NCORE = 8
NH = 16
DH = 64
HID0 = 4096
WIN = 256
SD = 128
EXP = 16
CAP = 256
FSH = 2048
MLPH = 4096

ENG = {'te': 'tensor', 've': 'vector', 'se': 'scalar', 'gp': 'gpsimd', 'sy': 'sync'}


def _ap_span(ap):
    try:
        t = ap.tensor
        name = t.name
        off = ap.offset
        ext = 1
        for step, cnt in ap.ap:
            ext += abs(step) * (cnt - 1)
        sz = mybir.dt.size(ap.dtype)
        return (name, off * sz, (off + ext) * sz)
    except Exception:
        return None


class _EngProxy:
    _OPS = {
        'tensor_tensor', 'tensor_scalar', 'scalar_tensor_tensor', 'tensor_copy',
        'tensor_reduce', 'reduce_max', 'reduce_sum', 'tensor_add', 'tensor_sub',
        'tensor_mul', 'tensor_max', 'reciprocal', 'tensor_tensor_scan', 'select',
        'copy_predicated', 'memset', 'max', 'max_index', 'match_replace',
        'activation', 'add', 'mul', 'copy', 'sign', 'sqrt', 'square', 'iota',
        'transpose', 'affine_select', 'tensor_scalar_add', 'tensor_scalar_mul',
        'tensor_scalar_max', 'tensor_scalar_min', 'bn_stats', 'bn_aggr',
    }

    def __init__(self, B, eng, e):
        self._B = B
        self._eng = eng
        self._e = e

    def __getattr__(self, name):
        e = self._e
        f = getattr(e, name)
        if name not in self._OPS:
            return f
        B = self._B
        eng = self._eng

        def wrap(*args, **kw):
            spans = []
            for a in list(args) + list(kw.values()):
                if isinstance(a, bass.AP):
                    sp = _ap_span(a)
                    if sp:
                        spans.append(sp)
            lw = B.last_writes.get(eng, [])
            hit = False
            for (n1, l1, h1) in spans[1:] or spans:
                for (n2, l2, h2) in lw:
                    if n1 == n2 and l1 < h2 and l2 < h1:
                        hit = True
            if hit:
                if B.c[eng] > B.waited.get((eng, eng + '#self'), 0):
                    e.wait_ge(B.S[eng], B.c[eng])
                    B.waited[(eng, eng + '#self')] = B.c[eng]
                B.last_writes[eng] = []
            ins = f(*args, **kw)
            out = None
            for a in list(args) + list(kw.values()):
                if isinstance(a, bass.AP):
                    out = a
                    break
            if out is not None:
                sp = _ap_span(out)
                if sp:
                    lw = B.last_writes.setdefault(eng, [])
                    lw.append(sp)
                    if len(lw) > 2:
                        lw.pop(0)
            try:
                ins.then_inc(B.S[eng], 1)
                B.c[eng] += 1
            except Exception:
                pass
            return _NoInc(ins)
        return wrap


class _SePlan:

    def __init__(self, B):
        self.B = B
        self.base = B.c['se']
        self.ops = []

    def add(self, fn, *toks):
        self.ops.append((toks, fn))
        return ('se', self.base + len(self.ops))

    def flush(self):
        B = self.B
        assert B.c['se'] == self.base, (B.c['se'], self.base)
        ops = self.ops

        @B.on('se')
        def _(e):
            for toks, fn in ops:
                B.wait(e, 'se', *toks)
                fn(e)
        assert B.c['se'] == self.base + len(ops), (B.c['se'], self.base, len(ops))


class _NoInc:
    def __init__(self, ins):
        self.ins = ins

    def then_inc(self, sem, n):
        self.ins.then_inc(sem, n)
        return self

POOL_N = 46720
O_A = 0
O_B = 16384
O_C = 33024
O_D = 41216
N_A = 16384
N_B = 16640
N_C = 8192
N_D = 5504


class Builder:
    def __init__(self):
        self.nc = bass.Bass("TRN2", target_bir_lowering=False)
        self.c = {}
        self.waited = {}
        self.S = {}
        self.ctx = []
        self.t = {}
        self.last_writes = {}

    def sem(self, name):
        cm = self.nc.semaphore(name)
        h = cm.__enter__()
        self.ctx.append(cm)
        self.S[name] = h
        self.c[name] = 0
        return h

    def inc(self, instr, s, n=1):
        if isinstance(instr, _NoInc):
            if s in ('ve', 'se', 'gp') and self.S[s] is not None:
                return (s, self.c[s])
            instr.ins.then_inc(self.S[s], n)
            self.c[s] += n
            return (s, self.c[s])
        instr.then_inc(self.S[s], n)
        self.c[s] += n
        return (s, self.c[s])

    def tok(self, s):
        return (s, self.c[s])

    def wait(self, e, eng, *tokens):
        for tkn in tokens:
            if tkn is None:
                continue
            s, v = tkn
            if s == eng:
                continue
            k = (eng, s)
            if v > self.waited.get(k, 0):
                e.wait_ge(self.S[s], v)
                self.waited[k] = v

    def se_plan(self):
        return _SePlan(self)

    def on(self, eng):
        dec = getattr(self.block, ENG[eng])
        if eng in ('ve', 'se', 'gp'):
            B = self

            def wrapped(fn):
                return dec(lambda e: fn(_EngProxy(B, eng, e)))
            return wrapped
        return dec

    def bar(self, e, eng):
        v = self.c[eng]
        if v > self.waited.get((eng, eng + '#self'), 0):
            e.wait_ge(self.S[eng], v)
            self.waited[(eng, eng + '#self')] = v

    def dram(self, name, shape, dtype, kind="Internal"):
        h = self.nc.dram_tensor(name, list(shape), dtype, kind=kind)
        self.t[name] = h
        return h.ap()

    def sbuf(self, name, shape, dtype):
        cm = self.nc.sbuf_tensor(name, list(shape), dtype)
        h = cm.__enter__()
        self.ctx.append(cm)
        self.t[name] = h
        return h

    def psum(self, name, shape, dtype):
        cm = self.nc.psum_tensor(name, list(shape), dtype)
        h = cm.__enter__()
        self.ctx.append(cm)
        self.t[name] = h
        return h



def _bf(x):
    return np.ascontiguousarray(np.asarray(x).astype(nbf16))


def _f32(x):
    return np.ascontiguousarray(np.asarray(x, dtype=np.float32))


def _vec_pc(v, chunks):
    return _f32(np.asarray(v, np.float32).reshape(chunks, 128).T)


def host_prepare(w):
    d = {}
    d['l0_wqT'] = _bf(w['L0_attn_Wq'].T)
    d['l0_wkT'] = _bf(w['L0_attn_Wk'].T)
    d['l0_wvT'] = _bf(w['L0_attn_Wv'].T)
    d['l0_woT'] = _bf(w['L0_attn_Wo'].T)
    d['w1T_f32'] = _f32(np.asarray(w['L0_ffn_w1']).T)
    d['w2T_f32'] = _f32(np.asarray(w['L0_ffn_w2']).T)
    cw = np.asarray(w['L0_ffn_conv_w'], np.float32).reshape(HID0, 3)
    d['convw'] = _f32(cw.reshape(32, 128, 3).transpose(1, 0, 2).reshape(128, 96))
    d['convb'] = _vec_pc(w['L0_ffn_conv_b'], 32)
    d['ln1g'] = _vec_pc(w['L1_attn_ln1_g'], 8)
    d['ln1b'] = _vec_pc(w['L1_attn_ln1_b'], 8)
    d['ln2g'] = _vec_pc(w['L1_attn_ln2_g'], 8)
    d['ln2b'] = _vec_pc(w['L1_attn_ln2_b'], 8)
    d['wqkvT'] = _bf(w['L1_attn_W_qkv'])
    d['l1_woT'] = _bf(w['L1_attn_W_O'])
    d['bO'] = _vec_pc(w['L1_attn_b_O'], 8)
    d['wugT'] = _bf(w['L1_attn_W_ug'])
    d['bug'] = _f32(np.asarray(w['L1_attn_b_ug'], np.float32).reshape(9, 128).T)
    d['bwT'] = _bf(w['L1_attn_B_w'])
    d['Avec'] = _f32(np.asarray(w['L1_attn_A'], np.float32).reshape(128, 1))
    d['cwT'] = _bf(w['L1_attn_C_w'])
    d['mw1T'] = _bf(w['L1_attn_mlp_W1'])
    d['mb1'] = _vec_pc(w['L1_attn_mlp_b1'], 32)
    d['mw2T'] = _bf(w['L1_attn_mlp_W2'])
    d['mb2'] = _vec_pc(w['L1_attn_mlp_b2'], 8)
    d['gateT'] = _f32(np.asarray(w['L1_ffn_gate_w']).T.reshape(8, 128, EXP).transpose(1, 0, 2))
    d['wgT'] = _bf(np.transpose(np.asarray(w['L1_ffn_Wg']), (0, 2, 1)))
    d['wuT'] = _bf(np.transpose(np.asarray(w['L1_ffn_Wu']), (0, 2, 1)))
    d['wdT'] = _bf(np.transpose(np.asarray(w['L1_ffn_Wd']), (0, 2, 1)))
    d['wsgT'] = _bf(np.asarray(w['L1_ffn_Ws_g']).T)
    d['wsuT'] = _bf(np.asarray(w['L1_ffn_Ws_u']).T)
    d['wsdT'] = _bf(np.asarray(w['L1_ffn_Ws_d']).T)

    d['identb'] = _bf(np.eye(128))
    d['identf'] = _f32(np.eye(128))
    inv = 1.0 / (10000.0 ** (np.arange(0, DH, 2, dtype=np.float32) / DH))
    ang = np.outer(inv, np.arange(T, dtype=np.float32))
    rows = np.arange(128)
    fidx = (rows % 64) // 2
    sgn = np.where(rows % 2 == 0, -1.0, 1.0).astype(np.float32)
    d['ropeC'] = _bf(np.cos(ang)[fidx])
    d['ropeS'] = _bf(np.sin(ang)[fidx] * sgn[:, None])
    sw = np.zeros((128, 128), np.float32)
    sw[np.arange(128), np.arange(128) ^ 1] = 1.0
    d['swap'] = _bf(sw)
    qi = np.arange(128)[:, None]
    ki = np.arange(128)[None, :]
    mB = np.where(ki > qi, -1e9, 0.0).astype(np.float32)
    mA = np.where(ki <= qi, -1e9, 0.0).astype(np.float32)
    d['maskB'] = _f32(mB)
    d['maskL1'] = _f32(np.concatenate([mA, np.zeros((128, 128), np.float32), mB], 1))
    d['ones1'] = _f32(np.ones((1, 128)))
    d['ones128'] = _f32(np.ones((128, 1)))
    ind2 = np.zeros((2, 128), np.float32)
    ind2[0, :64] = 1.0
    ind2[1, 64:] = 1.0
    d['ind2'] = _f32(ind2)
    return d


SHARED_SPECS = {
    'l0_wqT': ([DM, DM], BF16), 'l0_wkT': ([DM, DM], BF16),
    'l0_wvT': ([DM, DM], BF16), 'l0_woT': ([DM, DM], BF16),
    'w1T_f32': ([DM, HID0], F32), 'w2T_f32': ([HID0, DM], F32),
    'convw': ([128, 96], F32), 'convb': ([128, 32], F32),
    'ln1g': ([128, 8], F32), 'ln1b': ([128, 8], F32),
    'ln2g': ([128, 8], F32), 'ln2b': ([128, 8], F32),
    'wqkvT': ([DM, 3 * DM], BF16), 'l1_woT': ([DM, DM], BF16),
    'bO': ([128, 8], F32), 'wugT': ([DM, 1152], BF16), 'bug': ([128, 9], F32),
    'bwT': ([DM, SD], BF16), 'Avec': ([128, 1], F32), 'cwT': ([SD, DM], BF16),
    'mw1T': ([DM, MLPH], BF16), 'mb1': ([128, 32], F32),
    'mw2T': ([MLPH, DM], BF16), 'mb2': ([128, 8], F32),
    'gateT': ([128, 8, EXP], F32),
    'wgT': ([EXP, DM, FSH], BF16), 'wuT': ([EXP, DM, FSH], BF16),
    'wdT': ([EXP, FSH, DM], BF16),
    'wsgT': ([DM, FSH], BF16), 'wsuT': ([DM, FSH], BF16), 'wsdT': ([FSH, DM], BF16),
    'identb': ([128, 128], BF16), 'identf': ([128, 128], F32),
    'ropeC': ([128, T], BF16), 'ropeS': ([128, T], BF16),
    'swap': ([128, 128], BF16),
    'maskB': ([128, 128], F32), 'maskL1': ([128, 384], F32),
    'ones1': ([1, 128], F32), 'ones128': ([128, 1], F32), 'ind2': ([2, 128], F32),
}

CONST_BF = ['identb', 'swap', 'ropeC', 'ropeS']
CONST_F32 = ['identf', 'maskB', 'maskL1', 'ones1', 'ones128', 'ind2', 'convw', 'convb',
             'ln1g', 'ln1b', 'ln2g', 'ln2b', 'bO', 'bug', 'Avec', 'mb1', 'mb2']



class St:
    pass


def build_program(upto=99, dbg=None):
    B = Builder()
    nc = B.nc
    st = St()
    st.B = B
    st.dbg = dbg or []

    st.din = {}
    st.din['x'] = B.dram('x', [T, DM], F32, kind="ExternalInput")
    for n, (shp, dt) in SHARED_SPECS.items():
        st.din[n] = B.dram(n, shp, dt, kind="ExternalInput")
    st.dout = B.dram('out', [T, DM], F32, kind="ExternalOutput")

    st.scr = {}
    st.scr['X'] = B.dram('scr_X', [128, 8, T], F32)
    st.scr['s0'] = B.dram('scr_s0', [128, 8, T], F32)
    st.scr['g0'] = B.dram('scr_g0', [128, 8, T], BF16)
    st.scr['xn32'] = B.dram('scr_xn32', [128, 8, T], F32)
    st.scr['xn16'] = B.dram('scr_xn16', [T, DM], BF16)
    st.scr['xn1'] = B.dram('scr_xn1', [128, 8, T], BF16)
    st.scr['w1q'] = B.dram('scr_w1q', [DM, HID0], BF16)
    st.scr['w2q'] = B.dram('scr_w2q', [HID0, DM], BF16)
    st.scr['acc'] = B.dram('scr_acc', [T, DM], F32)
    st.scr['h2'] = B.dram('scr_h2', [128, 8, T], BF16)
    st.scr['idx'] = B.dram('scr_idx', [EXP, CAP], U16)
    st.scr['val'] = B.dram('scr_val', [EXP, CAP], F32)

    st.pool = B.sbuf('pool', [128, POOL_N], F32)

    def R(off, n, c=None):
        assert off + n <= POOL_N, (off, n)
        ap = st.pool[:, off:off + n]
        if c is not None:
            ap = ap.rearrange("p (c n) -> p c n", c=c)
        return ap

    def Rb(off, nbf, c=None):
        assert off + nbf // 2 <= POOL_N, (off, nbf)
        ap = st.pool[:, off:off + nbf // 2].bitcast(BF16)
        if c is not None:
            ap = ap.rearrange("p (c n) -> p c n", c=c)
        return ap

    def Ri(off, ni):
        return st.pool[:, off:off + ni // 2].bitcast(I16)

    st.R, st.Rb, st.Ri = R, Rb, Ri

    st.cst = {}
    for n in CONST_BF:
        st.cst[n] = B.sbuf('c_' + n, SHARED_SPECS[n][0], BF16)
    for n in CONST_F32:
        st.cst[n] = B.sbuf('c_' + n, SHARED_SPECS[n][0], F32)
    st.sm = B.sbuf('small', [128, 3072], F32)

    st.psS = B.psum('psS', [128, 2048], F32)
    st.psA = B.psum('psA', [128, 512], F32)
    st.psB = B.psum('psB', [128, 512], F32)
    st.psC = B.psum('psC', [128, 512], F32)
    st.psT = B.psum('psT', [128, 1024], BF16)

    for s in ['te', 've', 'se', 'gp', 'dW', 'dG', 'dO']:
        B.sem(s)

    blk = nc.Block()
    B.block = blk.__enter__()
    B.ctx.append(blk)

    ph_consts(st)
    ph_p1_load_ln0(st)
    if upto >= 2:
        ph_p2_l0_attn(st)
    if upto >= 3:
        ph_p3_l0_ffn(st)
    if upto >= 4:
        ph_p4_combine0(st)
    if upto >= 5:
        ph_p5_ln1(st)
    if upto >= 6:
        ph_p6_moe(st)
    if upto >= 7:
        ph_p7_l1_attn(st)
    if upto >= 8:
        ph_p8_final(st)

    if st.dbg:
        ph_dbg(st)

    for c in reversed(B.ctx):
        c.__exit__(None, None, None)
    lower_extended_insts(nc)
    return B


def ph_dbg(st):
    B = st.B
    outs = []
    for name in st.dbg:
        if name.startswith('scr_'):
            ap = st.scr[name[4:]]
        else:
            ap = getattr(st, name)
            if not isinstance(ap, bass.AP):
                ap = ap[:, :]
        shp = list(ap.shape)
        d = B.dram('dbg_' + name, shp, ap.dtype, kind="ExternalOutput")
        outs.append((d, ap))

    @B.on('sy')
    def _(e):
        for s in ['te', 've', 'se', 'gp', 'dG', 'dW', 'dO']:
            if B.c[s]:
                e.wait_ge(B.S[s], B.c[s])
        n = 0
        for d, ap in outs:
            B.inc(e.dma_start(d, ap), 'dO', 16)
            n += 16
        e.wait_ge(B.S['dO'], B.c['dO'])


def ph_consts(st):
    B = st.B

    @B.on('sy')
    def _(e):
        for n, h in st.cst.items():
            B.inc(e.dma_start(h[:, :], st.din[n]), 'dW', 16)
    st.tk_consts = B.tok('dW')

    st.cv = {}
    vals = {'eps': 1e-5, 'neg8': -8.0, 'neg20': -20.0, 'none': 0.0}
    @B.on('ve')
    def _(e):
        for i, (n, v) in enumerate(vals.items()):
            slot = st.sm[:, 2960 + i:2961 + i]
            e.memset(slot, v)
            st.cv[n] = slot


def ph_p1_load_ln0(st):
    import os
    DIS = os.environ.get('P1DIS', '').split(',')
    """Load x token-major; transpose raw x -> X fm, spill to scr_X.
    LN0 in TM; quantize xq0 (TM) and transpose to xq0 fm bf16 (bC);
    xn0 fm bf16 -> bA[0:8192]. alpha0 row + sxq row in sm."""
    B = st.B
    R, Rb, Ri = st.R, st.Rb, st.Ri
    cst, sm = st.cst, st.sm
    psS, psT = st.psS, st.psT

    x_tm = R(O_B, 16384, c=16)
    st.x_tm = x_tm
    st.sm_ap = sm[:, 0:256]
    st.sm_hi = sm[:, 2944:3072]
    stage = R(O_A + 8192, 8192)
    xn16 = Rb(O_A, 16384, c=8)
    xq0 = Rb(O_C, 16384, c=8)
    st.xn16_0, st.xq0 = xn16, xq0

    flagf = sm[:, 3000:3001]
    flagi = Ri(0, 0)
    cvec = sm[:, 3002:3003]
    st.cvec = cvec
    st.cv257 = sm[:, 3003:3004]
    st.cvn256 = sm[:, 3004:3005]

    @B.on('sy')
    def _(e):
        for tt in range(16):
            B.inc(e.dma_start(x_tm[:, tt, :], st.din['x'][128 * tt:128 * (tt + 1), :]),
                  'dW', 16)
    tk_x = B.tok('dW')

    @B.on('ve')
    def _(e):
        B.inc(e.memset(flagf, 0.7), 've')
        B.bar(e, 've')
        B.inc(e.tensor_scalar(sm[:, 3001:3002].bitcast(I16)[:, 0:1], flagf, 0.0, None, Op.add), 've')
        B.bar(e, 've')
        B.inc(e.tensor_scalar(cvec, sm[:, 3001:3002].bitcast(I16)[:, 0:1], -0.5, 0.5,
                              Op.mult, Op.add), 've')
        B.bar(e, 've')
        B.inc(e.tensor_scalar(sm[:, 3003:3004], cvec, 257.0, None, Op.mult), 've')
        B.inc(e.tensor_scalar(sm[:, 3004:3005], cvec, -256.0, None, Op.mult), 've')

    @B.on('ve')
    def _(e):
        B.wait(e, 've', tk_x)
        for tt in range(16):
            ins = e.tensor_reduce(sm[:, tt:tt + 1], x_tm[:, tt, :], mybir.AxisListType.X, Op.add)
        B.inc(ins, 've')
        B.bar(e, 've')
        B.inc(e.tensor_scalar(sm[:, 16:32], sm[:, 0:16], -1.0 / DM, None, Op.mult), 've')
    tk_negmu = B.tok('ve')

    @B.on('se')
    def _(e):
        B.wait(e, 'se', tk_x)
        for tt in range(16):
            B.inc(e.activation(stage[:, 7168:8192], x_tm[:, tt, :], AF.Square,
                               accum_out=sm[:, 32 + tt:33 + tt]), 'se')
    tk_sq = B.tok('se')

    tk_sp = []
    tk_ev = []
    for c in (() if 'xspill' in DIS else range(8)):
        @B.on('te')
        def _(e, c=c):
            B.wait(e, 'te', tk_x)
            if c >= 1:
                B.wait(e, 'te', tk_ev[c - 1])
            for tt in range(16):
                ins = e.transpose(psS[:, 128 * tt:128 * (tt + 1)],
                                  x_tm[:, tt, 128 * c:128 * (c + 1)], cst['identf'][:, :])
            B.inc(ins, 'te')
        tk_t = B.tok('te')

        @B.on('ve')
        def _(e, c=c, tk_t=tk_t):
            B.wait(e, 've', tk_t)
            if c >= 2:
                B.wait(e, 've', tk_sp[c - 2])
            B.inc(e.tensor_copy(stage[:, 0:2048] if c % 2 == 0 else stage[:, 2048:4096],
                                psS[:, :]), 've')
        tk_ev.append(B.tok('ve'))

        @B.on('sy')
        def _(e, c=c):
            B.wait(e, 'sy', tk_ev[c])
            B.inc(e.dma_start(st.scr['X'][:, c, :],
                              stage[:, 0:2048] if c % 2 == 0 else stage[:, 2048:4096]),
                  'dW', 16)
        tk_sp.append(B.tok('dW'))
    st.tk_Xspill = tk_sp[-1] if tk_sp else None
    tk_xtrans = ('te', B.c['te'])

    @B.on('ve')
    def _(e):
        B.wait(e, 've', tk_sq)
        B.bar(e, 've')
        e.tensor_tensor(sm[:, 48:64], sm[:, 16:32], sm[:, 16:32], Op.mult)
        B.inc(e.engine_nop(), 've')
        B.bar(e, 've')
        e.tensor_scalar(sm[:, 112:128], sm[:, 32:48], 1.0 / DM, None, Op.mult)
        B.inc(e.tensor_tensor(sm[:, 48:64], sm[:, 112:128], sm[:, 48:64], Op.subtract), 've')
    tk_var = B.tok('ve')

    @B.on('se')
    def _(e):
        B.wait(e, 'se', tk_var)
        B.inc(e.activation(sm[:, 48:64], sm[:, 48:64], AF.Sqrt, bias=st.cv['eps'], scale=1.0), 'se')
    tk_sqv = B.tok('se')

    @B.on('ve')
    def _(e):
        B.wait(e, 've', tk_sqv, tk_xtrans)
        B.inc(e.reciprocal(sm[:, 48:64], sm[:, 48:64]), 've')
        B.bar(e, 've')
        for tt in range(16):
            B.inc(e.tensor_scalar(x_tm[:, tt, :], x_tm[:, tt, :],
                                  sm[:, 16 + tt:17 + tt], sm[:, 48 + tt:49 + tt],
                                  Op.add, Op.mult), 've')
    tk_xn = B.tok('ve')

    @B.on('ve')
    def _(e):
        for tt in range(16):
            ins = e.tensor_reduce(sm[:, 64 + tt:65 + tt], x_tm[:, tt, :], mybir.AxisListType.X,
                                  Op.max, apply_absolute_value=True)
        B.inc(ins, 've')
        B.bar(e, 've')
        B.inc(e.tensor_scalar(sm[:, 64:80], sm[:, 64:80], 1e-5, None, Op.max), 've')
        B.bar(e, 've')
        B.inc(e.reciprocal(sm[:, 80:96], sm[:, 64:80]), 've')
        B.bar(e, 've')
        e.tensor_scalar(sm[:, 80:96], sm[:, 80:96], 127.0, None, Op.mult)
        B.inc(e.tensor_scalar(sm[:, 96:112], sm[:, 64:80], 1.0 / 127.0, None, Op.mult), 've')
        B.bar(e, 've')
    tk_scales = B.tok('ve')

    u16_st = stage[:, 6144:6656].bitcast(I16)
    xqtm = stage[:, 6656:7168].bitcast(BF16)
    tk_q = []
    for tt in (() if 'quant' in DIS else range(16)):
        @B.on('ve')
        def _(e, tt=tt):
            B.wait(e, 've', tk_scales)
            B.inc(e.tensor_scalar(u16_st, x_tm[:, tt, :], sm[:, 80 + tt:81 + tt],
                                  st.cv257[:, 0:1], Op.mult, Op.add), 've')
            B.inc(e.tensor_scalar(xqtm, u16_st, st.cvn256[:, 0:1], None, Op.add), 've')
        tk_xq = B.tok('ve')

        @B.on('te')
        def _(e, tt=tt, tk_xq=tk_xq):
            B.wait(e, 'te', tk_xq)
            B.wait(e, 'te', ('ve', B.c['ve']))
            for c in range(8):
                ins = e.transpose(psT[:, 128 * (c % 8):128 * (c % 8 + 1)],
                                  xqtm.rearrange("p (c n) -> p c n", c=8)[:, c, :],
                                  cst['identb'][:, :])
            B.inc(ins, 'te')
        tk_tp = B.tok('te')

        @B.on('ve')
        def _(e, tt=tt, tk_tp=tk_tp):
            B.wait(e, 've', tk_tp)
            for c in range(8):
                ins = e.tensor_copy(xq0[:, c, 128 * tt:128 * (tt + 1)],
                                    psT[:, 128 * c:128 * (c + 1)])
            B.inc(ins, 've')
        tk_q.append(B.tok('ve'))

    for c in (() if 'xn16' in DIS else range(8)):
        @B.on('te')
        def _(e, c=c):
            B.wait(e, 'te', tk_xn)
            B.wait(e, 'te', ('ve', B.c['ve']))
            for tt in range(16):
                ins = e.transpose(psS[:, 128 * tt:128 * (tt + 1)],
                                  x_tm[:, tt, 128 * c:128 * (c + 1)], cst['identf'][:, :])
            B.inc(ins, 'te')
        tk_t = B.tok('te')

        @B.on('ve')
        def _(e, c=c, tk_t=tk_t):
            B.wait(e, 've', tk_t)
            B.inc(e.tensor_copy(xn16[:, c, :], psS[:, :]), 've')
    st.tk_p1_done = B.tok('ve')
    st.tk_xq0 = tk_q[-1] if tk_q else None



def _wq_slices(st):
    B = st.B
    R, Rb, Ri = st.R, st.Rb, st.Ri
    sm = st.sm
    cst = st.cst
    slab = [R(O_A + 12288, 2048), R(O_A + 14336, 2048)]
    sgn_w = R(O_A + 10240, 1024)
    u_w = R(O_A + 11264, 1024)

    units = []
    for i in range(16):
        units.append(('w1', bass.AP(st.din['w1T_f32'].tensor,
                                    (128 * (i // 2)) * HID0 + (i % 2) * 2048,
                                    [[HID0, 128], [1, 2048]])))
    for i in range(16):
        units.append(('w2', bass.AP(st.din['w2T_f32'].tensor, (i * 256) * DM,
                                    [[DM, 128], [DM * 128, 2], [1, 1024]])))

    def wq_dst(i, h):
        kind = units[i][0]
        if kind == 'w1':
            off = (128 * (i // 2)) * HID0 + (i % 2) * 2048 + 1024 * h
            return bass.AP(st.scr['w1q'].tensor, off, [[HID0, 128], [1, 1024]])
        j = i - 16
        off = (256 * j + 128 * h) * DM
        return bass.AP(st.scr['w2q'].tensor, off, [[DM, 128], [1, 1024]])

    mtok = {'dma': [None] * 32, 've': [None] * 32}
    qtok = {'dma': [None] * 32, 've': [None] * 32, 'out': [None] * 32}

    def mk_mean(lo, hi):
        def emit():
            for i in range(lo, hi):
                @B.on('gp')
                def _(e, i=i):
                    B.wait(e, 'gp', st.tk_p1_done, st.tk_p1_se)
                    if i >= 2:
                        B.wait(e, 'gp', mtok['ve'][i - 2])
                    mtok['dma'][i] = B.inc(e.dma_start(slab[i % 2], units[i][1]), 'dG', 16)

                @B.on('ve')
                def _(e, i=i):
                    B.wait(e, 've', mtok['dma'][i])
                    j = 0 if units[i][0] == 'w1' else 1
                    e.tensor_reduce(sm[:, 162:163], slab[i % 2], mybir.AxisListType.X,
                                    Op.add, apply_absolute_value=True)
                    if i % 16 == 0:
                        mtok['ve'][i] = B.inc(e.tensor_copy(sm[:, 160 + j:161 + j],
                                                            sm[:, 162:163]), 've')
                    else:
                        mtok['ve'][i] = B.inc(e.tensor_tensor(sm[:, 160 + j:161 + j],
                                                              sm[:, 160 + j:161 + j],
                                                              sm[:, 162:163], Op.add), 've')
        return emit

    def mk_final():
        def emit():
            tkv = ('ve', B.c['ve'])

            @B.on('te')
            def _(e):
                B.wait(e, 'te', tkv)
                B.inc(e.matmul(st.psA[0:1, 0:2], cst['ones128'][:, :], sm[:, 160:162],
                               start=True, stop=True), 'te')
            tkt = B.tok('te')

            @B.on('ve')
            def _(e):
                B.wait(e, 've', tkt)
                B.inc(e.tensor_scalar(sm[0:1, 163:165], st.psA[0:1, 0:2],
                                      1.0 / (4096.0 * 1024.0), 1e-5, Op.mult, Op.max), 've')
            tkv2 = ('ve', B.c['ve'])

            @B.on('te')
            def _(e):
                B.wait(e, 'te', tkv2)
                B.inc(e.matmul(st.psB[:, 0:2], cst['ones1'][0:1, :], sm[0:1, 163:165],
                               start=True, stop=True), 'te')
            tkt2 = B.tok('te')

            @B.on('ve')
            def _(e):
                B.wait(e, 've', tkt2)
                e.tensor_copy(sm[:, 176:178], st.psB[:, 0:2])
                e.reciprocal(sm[:, 178:180], sm[:, 176:178])
            st._wqfin = ('ve', B.c['ve'])
        return emit

    def mk_quant(lo, hi):
        def emit():
            for i in range(lo, hi):
                @B.on('gp')
                def _(e, i=i):
                    if i >= 2:
                        B.wait(e, 'gp', qtok['ve'][i - 2], qtok['out'][i - 2])
                    else:
                        B.wait(e, 'gp', st._wqfin)
                    qtok['dma'][i] = B.inc(e.dma_start(slab[i % 2], units[i][1]), 'dG', 16)
                for h in range(2):
                    src = slab[i % 2][:, 1024 * h:1024 * (h + 1)]
                    u16 = st.pool[:, O_A + 10240:O_A + 10752].bitcast(I16)
                    u_wn = st.pool[:, O_A + 10752:O_A + 11776]
                    wqv = st.pool[:, O_A + 11776 + 512 * h:O_A + 12288 + 512 * h - 512 + 512]
                    wqv = st.pool[:, O_A + 11776 + 256 * h:O_A + 11776 + 256 * h + 512][:, 0:512].bitcast(BF16) if False else st.pool[:, O_A + 11776 + 512 * h:O_A + 11776 + 512 * h + 512].bitcast(BF16)

                    @B.on('ve')
                    def _(e, i=i, h=h, src=src, u16=u16, u_wn=u_wn, wqv=wqv):
                        j = 0 if units[i][0] == 'w1' else 1
                        B.wait(e, 've', qtok['dma'][i])
                        if i >= 1 and qtok['out'][i - 1] is not None:
                            B.wait(e, 've', qtok['out'][i - 1])
                        e.tensor_scalar(u16, src, sm[:, 178 + j:179 + j],
                                        st.cv257[:, 0:1], Op.mult, Op.add)
                        e.tensor_scalar(u_wn, u16, st.cvn256[:, 0:1], 1.0, Op.add, Op.min)
                        tk = B.inc(e.tensor_scalar(wqv, u_wn, -1.0, None, Op.max), 've')
                        if h == 1:
                            qtok['ve'][i] = tk

                    @B.on('gp')
                    def _(e, i=i, h=h, wqv=wqv):
                        B.wait(e, 'gp', ('ve', B.c['ve']))
                        qtok['out'][i] = B.inc(e.dma_start(wq_dst(i, h), wqv), 'dG', 16)
        return emit

    return ([mk_mean(i, i + 1) for i in range(32)] + [mk_final()],
            [mk_quant(i, i + 1) for i in range(32)])


def ph_p2_l0_attn(st):
    B = st.B
    R, Rb, Ri = st.R, st.Rb, st.Ri
    cst, sm = st.cst, st.sm
    psS, psA, psB, psC, psT = st.psS, st.psA, st.psB, st.psC, st.psT
    xn16 = st.xn16_0

    st.tk_p1_se = ('se', B.c['se'])
    attn_full = Rb(O_B, 16384, c=8)
    qraw = Rb(O_B + 8192, 2048)
    kraw = Rb(O_B + 9216, 2048)
    qrot = Rb(O_B + 10240, 2048)
    krot = Rb(O_B + 11264, 2048)
    v2 = Rb(O_B + 12288, 2048)
    vtm = Rb(O_B + 13312, 2048, c=16)
    t1b = Rb(O_B + 14336, 2048)
    t2b = Rb(O_B + 15360, 2048)
    aqe = Rb(O_A + 8192, 2048)
    attnT = Rb(O_A + 9216, 2048, c=16)
    s0st = R(O_A + 10240, 2048)
    st.attn_full0 = attn_full

    wr = [Rb(O_D + 512 * i, 1024, c=8) for i in range(6)]
    mean_sl, quant_sl = _wq_slices(st)

    tk_wv = [None] * 8
    tk_dma = [None] * 8

    def load_pair(p):
        @B.on('sy')
        def _(e):
            if p >= 2 and tk_wv[p - 2] is not None:
                B.wait(e, 'sy', tk_wv[p - 2])
            base = (p % 2) * 3
            for wi, wn in enumerate(['l0_wqT', 'l0_wkT', 'l0_wvT']):
                src = bass.AP(st.din[wn].tensor, 128 * p, [[DM, 128], [DM * 128, 8], [1, 128]])
                B.inc(e.dma_start(wr[base + wi][:, :, :], src), 'dW', 16)
        tk_dma[p] = B.tok('dW')

    load_pair(0)
    load_pair(1)

    for p in range(8):
        base = (p % 2) * 3
        for wi, dst in enumerate([qraw, kraw, v2]):
            @B.on('te')
            def _(e, p=p, wi=wi):
                B.wait(e, 'te', tk_dma[p], st.tk_p1_done)
                B.wait(e, 'te', ('ve', B.c['ve']))
                for f in range(8):
                    for n in range(4):
                        ins = e.matmul(psS[:, 512 * n:512 * (n + 1)],
                                       wr[base + wi][:, f, :],
                                       xn16[:, f, 512 * n:512 * (n + 1)],
                                       start=(f == 0), stop=(f == 7))
                B.inc(ins, 'te')
            tk_mm = B.tok('te')

            @B.on('ve')
            def _(e, dst=dst, tk_mm=tk_mm):
                B.wait(e, 've', tk_mm)
                e.tensor_copy(dst, psS[:, :])
        tk_wv[p] = B.tok('ve')
        tk_wv[p] = ('te', B.c['te'])
        if p + 2 < 8:
            load_pair(p + 2)

        for srcb, rotb in [(qraw, qrot), (kraw, krot)]:
            @B.on('te')
            def _(e, srcb=srcb):
                B.wait(e, 'te', ('ve', B.c['ve']))
                for n in range(4):
                    ins = e.matmul(psS[:, 512 * n:512 * (n + 1)], cst['swap'][:, :],
                                   srcb[:, 512 * n:512 * (n + 1)], start=True, stop=True)
                B.inc(ins, 'te')
            tk_sw = B.tok('te')

            @B.on('ve')
            def _(e, srcb=srcb, rotb=rotb, tk_sw=tk_sw):
                B.wait(e, 've', tk_sw)
                e.tensor_tensor(t1b, srcb, cst['ropeC'][:, :], Op.mult)
                e.tensor_tensor(t2b, psS[:, :], cst['ropeS'][:, :], Op.mult)
                e.tensor_tensor(rotb, t1b, t2b, Op.add)

        for g in range(2):
            @B.on('te')
            def _(e, g=g):
                B.wait(e, 'te', ('ve', B.c['ve']))
                for kt in range(8):
                    ins = e.transpose(psT[:, 128 * kt:128 * (kt + 1)],
                                      v2[:, 128 * (8 * g + kt):128 * (8 * g + kt + 1)],
                                      cst['identb'][:, :])
                B.inc(ins, 'te')
            tk = B.tok('te')

            @B.on('ve')
            def _(e2, g=g, tk=tk):
                B.wait(e2, 've', tk)
                for kt in range(8):
                    e2.tensor_copy(vtm[:, 8 * g + kt, :], psT[:, 128 * kt:128 * (kt + 1)])
        tk_vtm = ('ve', B.c['ve'])

        if not hasattr(st, '_wq_q'):
            st._wq_q = mean_sl + quant_sl
        if p == 4 and st._wq_q:
            st._wq_q.pop(0)()

        tk_rope = ('ve', B.c['ve'])
        if not hasattr(st, '_scnt'):
            st._scnt = 0
            st._exp_tok = [None, None]
        plan = B.se_plan()
        for hl in range(2):
            hb = 64 * hl
            for qt in range(16):
                nk = (qt + 1) * 128
                nseg = (nk + 1023) // 1024
                for seg in range(nseg):
                    half = st._scnt % 2
                    st._scnt += 1
                    off = 1024 * half
                    c0 = 1024 * seg
                    nkseg = min(1024, nk - c0)
                    isdiag = (seg == nseg - 1)
                    @B.on('te')
                    def _(e, hl=hl, qt=qt, hb=hb, half=half, off=off, c0=c0, nkseg=nkseg):
                        if st._exp_tok[half] is not None:
                            B.wait(e, 'te', st._exp_tok[half])
                        B.wait(e, 'te', tk_rope)
                        for kn in range((nkseg + 511) // 512):
                            N = min(512, nkseg - 512 * kn)
                            ins = e.matmul(psS[:, off + 512 * kn:off + 512 * kn + N],
                                           qrot[hb:hb + 64, 128 * qt:128 * (qt + 1)],
                                           krot[hb:hb + 64, c0 + 512 * kn:c0 + 512 * kn + N],
                                           start=True, stop=True)
                        B.inc(ins, 'te')
                    tk_sc = B.tok('te')

                    if isdiag:
                        @B.on('ve')
                        def _(e, qt=qt, off=off, tk_sc=tk_sc):
                            B.wait(e, 've', tk_sc)
                            do = off + (qt % 8) * 128
                            B.inc(e.tensor_tensor(psS[:, do:do + 128], psS[:, do:do + 128],
                                                  cst['maskB'][:, :], Op.add), 've')
                        tk_msk = B.tok('ve')
                    else:
                        tk_msk = tk_sc

                    st._exp_tok[half] = plan.add(
                        (lambda seg, off, c0, nkseg: lambda e: e.activation(
                            aqe[:, c0:c0 + nkseg], psS[:, off:off + nkseg],
                            AF.Exp, bias=st.cv['neg8'], scale=0.125,
                            accum_out=sm[:, 128 + seg:129 + seg]))(seg, off, c0, nkseg),
                        tk_msk, tk_sc)
                tk_exp = st._exp_tok[half]

                @B.on('ve')
                def _(e, qt=qt, nk=nk, nseg=nseg, tk_exp=tk_exp):
                    B.wait(e, 've', tk_exp)
                    if nseg == 2:
                        e.tensor_tensor(sm[:, 128:129], sm[:, 128:129], sm[:, 129:130], Op.add)
                    e.reciprocal(sm[:, 130:131], sm[:, 128:129])
                    B.inc(e.tensor_scalar(aqe[:, 0:nk], aqe[:, 0:nk],
                                          sm[:, 130:131], None, Op.mult), 've')
                tk_nrm = B.tok('ve')

                nbat = (qt + 1 + 7) // 8
                for g in range(nbat):
                    kts = list(range(8 * g, min(8 * g + 8, qt + 1)))
                    @B.on('te')
                    def _(e, kts=kts, tk_nrm=tk_nrm):
                        B.wait(e, 'te', tk_nrm)
                        B.wait(e, 'te', ('ve', B.c['ve']))
                        for kt in kts:
                            ins = e.transpose(psT[:, 128 * (kt % 8):128 * (kt % 8 + 1)],
                                              aqe[:, 128 * kt:128 * (kt + 1)],
                                              cst['identb'][:, :])
                        B.inc(ins, 'te')
                    tk_tp = B.tok('te')

                    @B.on('ve')
                    def _(e, kts=kts, tk_tp=tk_tp):
                        B.wait(e, 've', tk_tp)
                        for kt in kts:
                            ins = e.tensor_copy(attnT[:, kt, :],
                                                psT[:, 128 * (kt % 8):128 * (kt % 8 + 1)])
                        B.inc(ins, 've')
                tk_aT = B.tok('ve')

                bank = [psA, psB, psC][(hl * 16 + qt) // 4 % 3]
                @B.on('te')
                def _(e, hl=hl, qt=qt, hb=hb, bank=bank, tk_aT=tk_aT):
                    B.wait(e, 'te', tk_aT)
                    if qt % 4 == 0:
                        B.wait(e, 'te', ('ve', B.c['ve']))
                    for kt in range(qt + 1):
                        ins = e.matmul(bank[hb:hb + 64, 128 * (qt % 4):128 * (qt % 4 + 1)],
                                       vtm[:, kt, hb:hb + 64], attnT[:, kt, :],
                                       start=(kt == 0), stop=(kt == qt),
                                       tile_position=(0, hb))
                    B.inc(ins, 'te')
                tk_av = B.tok('te')

                if qt % 4 == 3:
                    @B.on('ve')
                    def _(e, p=p, hl=hl, qt=qt, hb=hb, bank=bank, tk_av=tk_av):
                        B.wait(e, 've', tk_av)
                        B.inc(e.tensor_copy(
                            attn_full[hb:hb + 64, p, 512 * (qt // 4):512 * (qt // 4 + 1)],
                            bank[hb:hb + 64, :]), 've')
                    if st._wq_q:
                        st._wq_q.pop(0)()
        plan.flush()

    for f in st._wq_q:
        f()
    st._wq_q = []
    st.tk_wq_done = B.tok('dG')

    tk_wo = [None, None]
    for d in range(8):
        @B.on('sy')
        def _(e, d=d):
            B.wait(e, 'sy', tk_wv[6])
            if d >= 2:
                B.wait(e, 'sy', tk_wo[d % 2])
            src = bass.AP(st.din['l0_woT'].tensor, 128 * d, [[DM, 128], [DM * 128, 8], [1, 128]])
            B.inc(e.dma_start(wr[d % 2][:, :, :], src), 'dW', 16)
        tk_d = B.tok('dW')

        @B.on('te')
        def _(e, d=d, tk_d=tk_d):
            B.wait(e, 'te', tk_d)
            B.wait(e, 'te', ('ve', B.c['ve']))
            for f in range(8):
                for n in range(4):
                    ins = e.matmul(psS[:, 512 * n:512 * (n + 1)], wr[d % 2][:, f, :],
                                   attn_full[:, f, 512 * n:512 * (n + 1)],
                                   start=(f == 0), stop=(f == 7))
            B.inc(ins, 'te')
        tk_wo[d % 2] = B.tok('te')
        tk_mm = B.tok('te')

        @B.on('ve')
        def _(e, d=d, tk_mm=tk_mm):
            B.wait(e, 've', tk_mm)
            B.wait(e, 've', ('dG', B.c['dG']))
            if d >= 2:
                B.wait(e, 've', st.tk_s0sp[d - 2])
            B.inc(e.tensor_copy(s0st if d % 2 == 0 else st.R(O_A + 12288, 2048), psS[:, :]), 've')
        tk_ev = B.tok('ve')

        @B.on('sy')
        def _(e, d=d, tk_ev=tk_ev):
            B.wait(e, 'sy', tk_ev)
            B.inc(e.dma_start(st.scr['s0'][:, d, :],
                              s0st if d % 2 == 0 else st.R(O_A + 12288, 2048)), 'dW', 16)
        if not hasattr(st, 'tk_s0sp'):
            st.tk_s0sp = []
        st.tk_s0sp.append(B.tok('dW'))
    st.tk_p2_done = ('ve', B.c['ve'])


def ph_p3_l0_ffn(st):
    B = st.B
    R, Rb, Ri = st.R, st.Rb, st.Ri
    cst, sm = st.cst, st.sm
    psS, psA, psB, psC, psT = st.psS, st.psA, st.psB, st.psC, st.psT
    psTf = psT[:, :].bitcast(F32)
    xq0 = st.xq0
    pool = st.pool

    h1c = R(O_B, 16640).rearrange("p (c n) -> p c n", c=32)
    xqh = Rb(O_A, 16384, c=32)
    ring = [Rb(O_A + 8192 + 512 * i, 1024, c=8) for i in range(4)]
    prowb = R(O_A + 10240, 514)
    o1st = R(O_A + 10754, 512)
    o3st = R(O_A + 11266, 512)
    sgn3 = o1st
    u3 = o3st
    u16_3 = Ri(O_A + 11778, 512)
    shqb = R(O_A + 12290, 512)
    prow2b = R(O_A + 12802, 512)
    evstg = R(O_A + 13314, 512)
    g0st = Rb(O_A + 13826, 1024)
    arow = pool[0:1, O_D:O_D + 2050]
    hrow = pool[0:1, O_D + 2056:O_D + 2568]
    hrow2 = pool[0:1, O_D + 2568:O_D + 3080]
    arow_t = pool[0:16, O_D + 3084:O_D + 3212]
    rmax_t = pool[0:4, O_D + 3216:O_D + 3344]

    tk_p2te = ('te', B.c['te'])
    tk_p2dw = ('dW', B.c['dW'])

    @B.on('te')
    def _(e):
        B.wait(e, 'te', tk_p2dw)
        B.wait(e, 'te', ('ve', B.c['ve']))
        B.inc(e.transpose(psA[0:16, 0:128], sm[:, 96:112], cst['identf'][:, :]), 'te')
    tk1 = B.tok('te')

    @B.on('ve')
    def _(e):
        B.wait(e, 've', tk1)
        e.memset(arow, 0.0)
        B.inc(e.tensor_copy(arow_t, psA[0:16, 0:128]), 've')
    tk2 = B.tok('ve')

    @B.on('gp')
    def _(e):
        B.wait(e, 'gp', tk2)
        src = bass.AP(B.t['pool'], O_D + 3084, [[POOL_N, 16], [1, 128]])
        dst = bass.AP(B.t['pool'], O_D + 1, [[POOL_N, 1], [1, 2048]])
        B.inc(e.dma_start(dst, src), 'dG', 16)
    tk3 = B.tok('dG')

    @B.on('ve')
    def _(e):
        B.wait(e, 've', tk3)
        B.inc(e.tensor_scalar(arow[0:1, 1:2049], arow[0:1, 1:2049],
                              sm[0:1, 176:177], None, Op.mult), 've')
        B.inc(e.tensor_scalar(sm[0:1, 180:181], sm[0:1, 177:178], 1.0 / 127.0,
                              None, Op.mult), 've')
    tk_arow = B.tok('ve')

    tk_ld = [None] * 32
    tk_use = [None] * 32

    for tc in range(4):
        t0 = 512 * tc
        P0 = 1 if tc == 0 else 0
        P1_ = 513 if tc == 3 else 514

        @B.on('ve')
        def _(e, tc=tc):
            if tc == 0:
                B.wait(e, 've', tk_arow)
                e.memset(bass.AP(B.t['pool'], O_B + 0, [[POOL_N, 128], [520, 32], [1, 1]]), 0.0)
            if tc == 3:
                e.memset(bass.AP(B.t['pool'], O_B + 513, [[POOL_N, 128], [520, 32], [1, 1]]), 0.0)
        tkz = ('ve', B.c['ve'])

        @B.on('te')
        def _(e, tc=tc, t0=t0):
            B.wait(e, 'te', tk_arow)
            B.wait(e, 'te', ('ve', B.c['ve']))
            e.matmul(psC[:, 0:512], cst['ones1'][0:1, :], arow[0:1, t0:t0 + 512],
                     start=True, stop=True)
            B.inc(e.matmul(psA[:, 0:2], cst['ones1'][0:1, :], arow[0:1, t0 + 512:t0 + 514],
                           start=True, stop=True), 'te')
        tkp = B.tok('te')

        @B.on('ve')
        def _(e, tkp=tkp):
            B.wait(e, 've', tkp)
            e.tensor_copy(prowb[:, 0:512], psC[:, 0:512])
            B.inc(e.tensor_copy(prowb[:, 512:514], psA[:, 0:2]), 've')
        tk_prow = B.tok('ve')

        tk_evac = [None] * 32
        plan_cv = B.se_plan()
        for hc in range(33):
            if hc < 32:
                @B.on('sy')
                def _(e, hc=hc, tc=tc):
                    i = hc
                    if tc == 0 and hc < 4:
                        B.wait(e, 'sy', st.tk_wq_done, tk_p2te)
                        B.wait(e, 'sy', ('ve', B.c['ve']))
                        B.wait(e, 'sy', ('se', B.c['se']))
                    if tk_use[i % 4] is not None:
                        B.wait(e, 'sy', tk_use[i % 4])
                    src = bass.AP(st.scr['w1q'].tensor, 128 * hc,
                                  [[HID0, 128], [HID0 * 128, 8], [1, 128]])
                    tk_ld[i % 4] = B.inc(e.dma_start(ring[i % 4][:, :, :], src), 'dW', 16)

                @B.on('te')
                def _(e, hc=hc, tc=tc, t0=t0, P0=P0, P1_=P1_):
                    B.wait(e, 'te', tk_ld[hc % 4], st.tk_xq0, tk_prow)
                    if hc >= 2:
                        B.wait(e, 'te', tk_evac[hc - 2])
                    half = 1024 * (hc % 2)
                    n0 = min(512, P1_) - P0
                    ins = None
                    for f in range(8):
                        ins = e.matmul(psS[:, half + P0:half + P0 + n0],
                                       ring[hc % 4][:, f, :],
                                       xq0[:, f, t0 - 1 + P0:t0 - 1 + P0 + n0],
                                       start=(f == 0), stop=(f == 7))
                    if P1_ > 512:
                        n1 = P1_ - 512
                        for f in range(8):
                            ins = e.matmul(psS[:, half + 512:half + 512 + n1],
                                           ring[hc % 4][:, f, :],
                                           xq0[:, f, t0 + 511:t0 + 511 + n1],
                                           start=(f == 0), stop=(f == 7))
                    B.inc(ins, 'te')
                tk_use[hc % 4] = B.tok('te')
                tk_mm = B.tok('te')

                @B.on('ve')
                def _(e, hc=hc, P0=P0, P1_=P1_, tk_mm=tk_mm):
                    B.wait(e, 've', tk_mm)
                    half = 1024 * (hc % 2)
                    tk_evac[hc] = B.inc(
                        e.tensor_tensor(h1c[:, hc, P0:P1_], psS[:, half + P0:half + P1_],
                                        prowb[:, P0:P1_], Op.mult), 've')
            if hc >= 1:
                c = hc - 1
                tko1 = plan_cv.add(
                    (lambda c, tkv: lambda e: (
                        B.wait(e, 'se', tkv),
                        e.activation(o1st, h1c[:, c, 1:513], AF.Identity,
                                     bias=cst['convb'][:, c:c + 1],
                                     scale=cst['convw'][:, 3 * c + 1:3 * c + 2]))[-1]
                     )(c, tk_evac[c]))

                @B.on('ve')
                def _(e, c=c, tko1=tko1):
                    B.wait(e, 've', tko1)
                    e.scalar_tensor_tensor(o3st, h1c[:, c, 0:512],
                                           cst['convw'][:, 3 * c:3 * c + 1], o1st,
                                           Op.mult, Op.add)
                    B.inc(e.scalar_tensor_tensor(o1st, h1c[:, c, 2:514],
                                                 cst['convw'][:, 3 * c + 2:3 * c + 3],
                                                 o3st, Op.mult, Op.add), 've')
                tko3 = B.tok('ve')

                plan_cv.add(
                    (lambda c, tko3: lambda e: (
                        B.wait(e, 'se', tko3),
                        e.activation(h1c[:, c, 0:512], o1st, AF.Silu))[-1]
                     )(c, tko3))
        plan_cv.flush()
        tk_hs = B.tok('se')

        for g in range(4):
            for rr in range(2):
                @B.on('te')
                def _(e, g=g, rr=rr):
                    B.wait(e, 'te', tk_hs)
                    B.wait(e, 'te', ('ve', B.c['ve']))
                    for k in range(16):
                        hc = 16 * rr + k
                        ins = e.transpose(psS[:, 128 * k:128 * (k + 1)],
                                          h1c[:, hc, 128 * g:128 * (g + 1)],
                                          cst['identf'][:, :])
                    B.inc(ins, 'te')
                tkt = B.tok('te')

                @B.on('ve')
                def _(e, g=g, rr=rr, tkt=tkt):
                    B.wait(e, 've', tkt)
                    e.tensor_reduce(sm[:, 196:197], psS[:, :], mybir.AxisListType.X,
                                    Op.max, apply_absolute_value=True)
                    if rr == 0:
                        B.inc(e.tensor_copy(sm[:, 192 + g:193 + g], sm[:, 196:197]), 've')
                    else:
                        B.inc(e.tensor_tensor(sm[:, 192 + g:193 + g],
                                              sm[:, 192 + g:193 + g],
                                              sm[:, 196:197], Op.max), 've')
        tk_rm = B.tok('ve')

        @B.on('te')
        def _(e):
            B.wait(e, 'te', tk_rm)
            B.inc(e.transpose(psA[0:4, 0:128], sm[:, 192:196], cst['identf'][:, :]), 'te')
        tkt2 = B.tok('te')

        @B.on('ve')
        def _(e, tkt2=tkt2):
            B.wait(e, 've', tkt2)
            B.inc(e.tensor_copy(rmax_t, psA[0:4, 0:128]), 've')
        tkrt = B.tok('ve')

        @B.on('gp')
        def _(e, tkrt=tkrt):
            B.wait(e, 'gp', tkrt)
            src = bass.AP(B.t['pool'], O_D + 3216, [[POOL_N, 4], [1, 128]])
            dst = bass.AP(B.t['pool'], O_D + 2056, [[POOL_N, 1], [1, 512]])
            B.inc(e.dma_start(dst, src), 'dG', 16)
        tkhr = B.tok('dG')

        @B.on('ve')
        def _(e, tkhr=tkhr):
            B.wait(e, 've', tkhr)
            e.tensor_scalar(hrow, hrow, 1e-5, None, Op.max)
            e.reciprocal(hrow2, hrow)
            e.tensor_scalar(hrow2, hrow2, 127.0, None, Op.mult)
            B.inc(e.tensor_scalar(hrow, hrow, sm[0:1, 180:181], None, Op.mult), 've')
        tk_rows = B.tok('ve')

        @B.on('te')
        def _(e, tk_rows=tk_rows):
            B.wait(e, 'te', tk_rows)
            B.wait(e, 'te', ('ve', B.c['ve']))
            e.matmul(psC[:, 0:512], cst['ones1'][0:1, :], hrow2, start=True, stop=True)
            B.inc(e.matmul(psB[:, 0:512], cst['ones1'][0:1, :], hrow, start=True, stop=True), 'te')
        tkb = B.tok('te')

        @B.on('ve')
        def _(e, tkb=tkb):
            B.wait(e, 've', tkb)
            e.tensor_copy(shqb, psC[:, 0:512])
            B.inc(e.tensor_copy(prow2b, psB[:, 0:512]), 've')
        tk_bc = B.tok('ve')

        bank = [psS[:, 0:512], psS[:, 512:1024], psS[:, 1024:1536], psS[:, 1536:2048],
                psA[:, :], psB[:, :], psC[:, :], psTf]
        tk_q = [None] * 32
        for hc in range(32):
            @B.on('ve')
            def _(e, hc=hc):
                B.wait(e, 've', tk_bc)
                e.scalar_tensor_tensor(u16_3, h1c[:, hc, 0:512], st.cv257[:, 0:1],
                                       shqb, Op.add, Op.mult) if False else None
                e.tensor_tensor(u3, h1c[:, hc, 0:512], shqb, Op.mult)
                e.tensor_scalar(u16_3, u3, st.cv257[:, 0:1], None, Op.add)
                tk_q[hc] = B.inc(e.tensor_scalar(xqh[:, hc, :], u16_3,
                                                 st.cvn256[:, 0:1], None, Op.add), 've')

            @B.on('sy')
            def _(e, hc=hc, tc=tc):
                B.wait(e, 'sy', tk_use[hc % 4])
                src = bass.AP(st.scr['w2q'].tensor, (128 * hc) * DM,
                              [[DM, 128], [128, 8], [1, 128]])
                tk_ld[hc % 4] = B.inc(e.dma_start(ring[hc % 4][:, :, :], src), 'dW', 16)

            @B.on('te')
            def _(e, hc=hc):
                B.wait(e, 'te', tk_ld[hc % 4], tk_q[hc])
                if hc == 0:
                    B.wait(e, 'te', ('ve', B.c['ve']))
                ins = None
                for d in range(8):
                    ins = e.matmul(bank[d], ring[hc % 4][:, d, :], xqh[:, hc, :],
                                   start=(hc == 0), stop=(hc == 31))
                B.inc(ins, 'te')
            tk_use[hc % 4] = B.tok('te')
        tk_bl2 = B.tok('te')

        plan_ev = B.se_plan()
        tksl_prev = None
        for d in range(8):
            @B.on('ve')
            def _(e, d=d, tksl_prev=tksl_prev):
                B.wait(e, 've', tk_bl2)
                if tksl_prev is not None:
                    B.wait(e, 've', tksl_prev)
                B.inc(e.tensor_tensor(evstg, bank[d], prow2b, Op.mult), 've')
            tkev = B.tok('ve')

            if not hasattr(st, 'tk_g0sp'):
                st.tk_g0sp = []

            tksl = plan_ev.add(
                (lambda d, tkev, tkg: lambda e: (
                    B.wait(e, 'se', tkev),
                    B.wait(e, 'se', tkg) if tkg is not None else None,
                    e.activation(g0st[:, 0:512] if d % 2 == 0 else g0st[:, 512:1024],
                                 evstg, AF.Silu))[-1]
                 )(d, tkev, st.tk_g0sp[-2] if len(st.tk_g0sp) >= 2 else None))
            tksl_prev = tksl

            @B.on('sy')
            def _(e, d=d, tc=tc, tksl=tksl):
                B.wait(e, 'sy', tksl)
                B.inc(e.dma_start(st.scr['g0'][:, d, t0:t0 + 512],
                                  g0st[:, 0:512] if d % 2 == 0 else g0st[:, 512:1024]),
                      'dW', 16)
            st.tk_g0sp.append(B.tok('dW'))
        plan_ev.flush()
    st.tk_p3_done = ('dW', B.c['dW'])


def ph_p4_combine0(st):
    B = st.B
    R, Rb = st.R, st.Rb
    x1 = R(O_A, N_A, c=8)
    st.X1 = x1
    tk_in = [None] * 8
    for d in range(8):
        o = (d % 2) * 7168
        stX = R(O_B + o, 2048)
        stS = R(O_B + o + 2048, 2048)
        stG = Rb(O_B + o + 4096, 2048)
        tmp = R(O_B + o + 5120, 2048)

        @B.on('sy')
        def _(e, d=d):
            if d == 0:
                B.wait(e, 'sy', st.tk_p3_done)
                B.wait(e, 'sy', ('te', B.c['te']))
                B.wait(e, 'sy', ('ve', B.c['ve']))
                B.wait(e, 'sy', ('se', B.c['se']))
            if d >= 2:
                B.wait(e, 'sy', tk_in[d - 2])
            B.inc(e.dma_start(stX if d % 2 == 0 else R(O_B + 7168, 2048),
                              st.scr['X'][:, d, :]), 'dW', 16) if False else None
            B.inc(e.dma_start(stX, st.scr['X'][:, d, :]), 'dW', 16)
            B.inc(e.dma_start(stS, st.scr['s0'][:, d, :]), 'dW', 16)
            B.inc(e.dma_start(stG, st.scr['g0'][:, d, :]), 'dW', 16)
        tkd = B.tok('dW')

        @B.on('ve')
        def _(e, d=d, stX=stX, stS=stS, stG=stG, tmp=tmp, tkd=tkd):
            B.wait(e, 've', tkd)
            e.tensor_tensor(tmp, stS, stG, Op.mult)
            tk_in[d] = B.inc(e.tensor_tensor(x1[:, d, :], tmp, stX, Op.add), 've')
    st.tk_p4_done = B.tok('ve')


def ph_p5_ln1(st):
    B = st.B
    R, Rb = st.R, st.Rb
    cst, sm = st.cst, st.sm
    psS, psA, psB, psC = st.psS, st.psA, st.psB, st.psC
    x1 = st.X1
    pool = st.pool

    x_tm = R(O_B, 16384, c=16)
    xn16 = Rb(O_C, 16384, c=8)
    st.xn16_1 = xn16
    gatesb = R(O_D + 3344, 128, None).rearrange("p (c n) -> p c n", c=8)
    lgbuf = R(O_D + 3472, 256)
    st.lg = lgbuf

    @B.on('sy')
    def _(e):
        B.wait(e, 'sy', st.tk_p4_done)
        B.inc(e.dma_start(gatesb[:, :, :], st.din['gateT']), 'dW', 16)
        for d in range(8):
            B.inc(e.dma_start(st.scr['X'][:, d, :], x1[:, d, :]), 'dW', 16)
    tk_g = B.tok('dW')

    tk_ev = [None] * 8
    for c in range(8):
        @B.on('te')
        def _(e, c=c):
            B.wait(e, 'te', st.tk_p4_done)
            B.wait(e, 'te', ('ve', B.c['ve']))
            for tt in range(16):
                ins = e.transpose(psS[:, 128 * tt:128 * (tt + 1)],
                                  x1[:, c, 128 * tt:128 * (tt + 1)], cst['identf'][:, :])
            B.inc(ins, 'te')
        tkt = B.tok('te')

        @B.on('ve')
        def _(e, c=c, tkt=tkt):
            B.wait(e, 've', tkt)
            for tt in range(16):
                ins = e.tensor_copy(x_tm[:, tt, 128 * c:128 * (c + 1)],
                                    psS[:, 128 * tt:128 * (tt + 1)])
            B.inc(ins, 've')
        tk_ev[c] = B.tok('ve')
    tk_xtm = B.tok('ve')

    @B.on('ve')
    def _(e):
        B.wait(e, 've', tk_xtm)
        for tt in range(16):
            ins = e.tensor_reduce(sm[:, tt:tt + 1], x_tm[:, tt, :], mybir.AxisListType.X, Op.add)
        B.inc(ins, 've')
        B.bar(e, 've')
        B.inc(e.tensor_scalar(sm[:, 16:32], sm[:, 0:16], -1.0 / DM, None, Op.mult), 've')
    tk_mu = B.tok('ve')

    @B.on('se')
    def _(e):
        B.wait(e, 'se', tk_xtm)
        for tt in range(16):
            B.inc(e.activation(pool[:, O_A + 8192:O_A + 9216], x_tm[:, tt, :], AF.Square,
                               accum_out=sm[:, 32 + tt:33 + tt]), 'se')
    tk_sq = B.tok('se')

    @B.on('ve')
    def _(e):
        B.wait(e, 've', tk_sq, tk_mu)
        e.tensor_tensor(sm[:, 48:64], sm[:, 16:32], sm[:, 16:32], Op.mult)
        e.tensor_scalar(sm[:, 112:128], sm[:, 32:48], 1.0 / DM, None, Op.mult)
        B.inc(e.tensor_tensor(sm[:, 48:64], sm[:, 112:128], sm[:, 48:64], Op.subtract), 've')
    tk_var = B.tok('ve')

    @B.on('se')
    def _(e):
        B.wait(e, 'se', tk_var)
        B.inc(e.activation(sm[:, 48:64], sm[:, 48:64], AF.Sqrt, bias=st.cv['eps'], scale=1.0), 'se')
    tk_sqv = B.tok('se')

    @B.on('ve')
    def _(e):
        B.wait(e, 've', tk_sqv)
        B.inc(e.reciprocal(sm[:, 48:64], sm[:, 48:64]), 've')
        B.bar(e, 've')
        for tt in range(16):
            B.inc(e.tensor_scalar(x_tm[:, tt, :], x_tm[:, tt, :],
                                  sm[:, 16 + tt:17 + tt], sm[:, 48 + tt:49 + tt],
                                  Op.add, Op.mult), 've')
    tk_xn = B.tok('ve')

    @B.on('gp')
    def _(e):
        B.wait(e, 'gp', tk_xn)
        for tt in range(16):
            B.inc(e.dma_start(st.scr['xn16'][128 * tt:128 * (tt + 1), :],
                              x_tm[:, tt, :]), 'dG', 16)
    tk_xnsp = B.tok('dG')

    tk_gacc = None
    for c in range(8):
        @B.on('te')
        def _(e, c=c):
            B.wait(e, 'te', tk_xn)
            B.wait(e, 'te', ('ve', B.c['ve']))
            for tt in range(16):
                ins = e.transpose(psS[:, 128 * tt:128 * (tt + 1)],
                                  x_tm[:, tt, 128 * c:128 * (c + 1)], cst['identf'][:, :])
            B.inc(ins, 'te')
        tkt = B.tok('te')

        @B.on('ve')
        def _(e, c=c, tkt=tkt):
            B.wait(e, 've', tkt)
            if c == 0:
                B.wait(e, 've', tk_g)
            e.tensor_copy(x1[:, c, :], psS[:, :])
            B.inc(e.tensor_copy(xn16[:, c, :], x1[:, c, :]), 've')
        tk_evc = B.tok('ve')

        @B.on('sy')
        def _(e, c=c, tk_evc=tk_evc):
            B.wait(e, 'sy', tk_evc)
            B.inc(e.dma_start(st.scr['xn32'][:, c, :], x1[:, c, :]), 'dW', 16)

    tk_evall = B.tok('ve')

    @B.on('te')
    def _(e):
        B.wait(e, 'te', tk_evall, tk_g)
        for tt in range(16):
            for c in range(8):
                ins = e.matmul(psA[:, 16 * tt:16 * (tt + 1)],
                               x1[:, c, 128 * tt:128 * (tt + 1)],
                               gatesb[:, c, :], start=(c == 0), stop=(c == 7))
        B.inc(ins, 'te')
    tk_gacc = B.tok('te')

    @B.on('ve')
    def _(e):
        B.wait(e, 've', tk_gacc)
        B.inc(e.tensor_copy(st.lg, psA[:, 0:256]), 've')
    st.tk_lg = B.tok('ve')

    @B.on('ve')
    def _(e):
        B.wait(e, 've', st.tk_lg)
        for tt in range(16):
            ins = e.tensor_reduce(sm[:, tt:tt + 1], x_tm[:, tt, :], mybir.AxisListType.X, Op.add)
        B.inc(ins, 've')
        B.bar(e, 've')
        B.inc(e.tensor_scalar(sm[:, 16:32], sm[:, 0:16], -1.0 / DM, None, Op.mult), 've')
    tk_mu2 = B.tok('ve')

    @B.on('se')
    def _(e):
        B.wait(e, 'se', tk_mu2)
        for tt in range(16):
            B.inc(e.activation(pool[:, O_A + 8192:O_A + 9216], x_tm[:, tt, :], AF.Square,
                               accum_out=sm[:, 32 + tt:33 + tt]), 'se')
    tk_sq2 = B.tok('se')

    @B.on('ve')
    def _(e):
        B.wait(e, 've', tk_sq2)
        e.tensor_tensor(sm[:, 48:64], sm[:, 16:32], sm[:, 16:32], Op.mult)
        e.tensor_scalar(sm[:, 112:128], sm[:, 32:48], 1.0 / DM, None, Op.mult)
        B.inc(e.tensor_tensor(sm[:, 48:64], sm[:, 112:128], sm[:, 48:64], Op.subtract), 've')
    tk_var2 = B.tok('ve')

    @B.on('se')
    def _(e):
        B.wait(e, 'se', tk_var2)
        B.inc(e.activation(sm[:, 48:64], sm[:, 48:64], AF.Sqrt, bias=st.cv['eps'], scale=1.0), 'se')
    tk_sqv2 = B.tok('se')

    @B.on('ve')
    def _(e):
        B.wait(e, 've', tk_sqv2, tk_xnsp)
        B.inc(e.reciprocal(sm[:, 48:64], sm[:, 48:64]), 've')
        B.bar(e, 've')
        for tt in range(16):
            B.inc(e.tensor_scalar(x_tm[:, tt, :], x_tm[:, tt, :],
                                  sm[:, 16 + tt:17 + tt], sm[:, 48 + tt:49 + tt],
                                  Op.add, Op.mult), 've')
    tk_xn1 = B.tok('ve')

    tk_sp1 = [None] * 8
    for c in range(8):
        xst = Rb(O_A + 8192 + (c % 2) * 1024, 2048)
        @B.on('te')
        def _(e, c=c):
            B.wait(e, 'te', tk_xn1)
            B.wait(e, 'te', ('ve', B.c['ve']))
            for tt in range(16):
                ins = e.transpose(psS[:, 128 * tt:128 * (tt + 1)],
                                  x_tm[:, tt, 128 * c:128 * (c + 1)], cst['identf'][:, :])
            B.inc(ins, 'te')
        tkt = B.tok('te')

        @B.on('ve')
        def _(e, c=c, xst=xst, tkt=tkt):
            B.wait(e, 've', tkt)
            if c >= 2:
                B.wait(e, 've', tk_sp1[c - 2])
            B.inc(e.tensor_scalar(xst, psS[:, :], cst['ln1g'][:, c:c + 1],
                                  cst['ln1b'][:, c:c + 1], Op.mult, Op.add), 've')
        tke = B.tok('ve')

        @B.on('sy')
        def _(e, c=c, xst=xst, tke=tke):
            B.wait(e, 'sy', tke)
            B.inc(e.dma_start(st.scr['xn1'][:, c, :], xst), 'dW', 16)
        tk_sp1[c] = B.tok('dW')
    st.tk_p5_done = B.tok('dW')


def ph_p6_moe(st):
    B = st.B
    R, Rb, Ri = st.R, st.Rb, st.Ri
    cst, sm = st.cst, st.sm
    psS, psA, psB, psC, psT = st.psS, st.psA, st.psB, st.psC, st.psT
    psTf = psT[:, :].bitcast(F32)
    xn16 = st.xn16_1
    pool = st.pool

    sc_tm = R(O_B, 256)
    sc_et = R(O_B + 256, 2048)
    work = R(O_B + 2304, 2048)
    valb = R(O_B + 4352, 256)
    idxb = pool[:, O_B + 4608:O_B + 4736].bitcast(U16)
    idx16 = Ri(O_B + 4736, 256)
    wv = R(O_B + 4864, 32)
    wsd_res = Rb(O_B + 5120, 16384, c=16)
    eo_fm = R(O_B + 13312, 2048, c=8)
    hsh = Rb(O_A, 16384, c=16)
    ring = [Rb(O_A + 8192 + 512 * i, 1024, c=8) for i in range(8)]
    gst = R(O_A + 12288, 1024)
    shst = [R(O_A + 13312 + 1024 * i, 1024) for i in range(2)]
    xsel = [Rb(O_A, 2048, c=8), Rb(O_A + 1024, 2048, c=8)]
    he = Rb(O_A + 4096, 4096, c=16)
    gst2 = R(O_A + 12288, 256)
    eo_tm = R(O_A + 12544, 2048).rearrange("p (c n) -> p c n", c=2)

    tk_p5ve = ('ve', B.c['ve'])

    @B.on('gp')
    def _(e):
        B.wait(e, 'gp', st.tk_p5_done)
        e.load_library(library_config.mlp)
    tk_lib = ('gp', B.c['gp'])

    @B.on('ve')
    def _(e):
        B.wait(e, 've', st.tk_lg)
        for tt in range(16):
            ins = e.tensor_reduce(sm[:, 128 + tt:129 + tt], st.lg[:, 16 * tt:16 * (tt + 1)],
                                  mybir.AxisListType.X, Op.max)
        B.inc(ins, 've')
        B.bar(e, 've')
        B.inc(e.tensor_scalar(sm[:, 144:160], sm[:, 128:144], -1.0, None, Op.mult), 've')
    tk_nm = B.tok('ve')

    for tt in range(16):
        @B.on('se')
        def _(e, tt=tt):
            B.wait(e, 'se', tk_nm)
            B.inc(e.activation(sc_tm[:, 16 * tt:16 * (tt + 1)], st.lg[:, 16 * tt:16 * (tt + 1)],
                               AF.Exp, bias=sm[:, 144 + tt:145 + tt], scale=1.0,
                               accum_out=sm[:, 160 + tt:161 + tt]), 'se')
    tk_exp = B.tok('se')

    @B.on('ve')
    def _(e):
        B.wait(e, 've', tk_exp)
        B.inc(e.reciprocal(sm[:, 176:192], sm[:, 160:176]), 've')
        B.bar(e, 've')
        for tt in range(16):
            B.inc(e.tensor_scalar(sc_tm[:, 16 * tt:16 * (tt + 1)], sc_tm[:, 16 * tt:16 * (tt + 1)],
                                  sm[:, 176 + tt:177 + tt], None, Op.mult), 've')
    tk_sc = B.tok('ve')

    for g4 in range(4):
        @B.on('te')
        def _(e, g4=g4):
            B.wait(e, 'te', tk_sc)
            B.wait(e, 'te', ('ve', B.c['ve']))
            for k in range(4):
                tt = 4 * g4 + k
                ins = e.transpose(psA[0:16, 128 * k:128 * (k + 1)],
                                  sc_tm[:, 16 * tt:16 * (tt + 1)], cst['identf'][:, :])
            B.inc(ins, 'te')
        tkt = B.tok('te')

        @B.on('ve')
        def _(e, g4=g4, tkt=tkt):
            B.wait(e, 've', tkt)
            B.inc(e.tensor_copy(sc_et[0:16, 512 * g4:512 * (g4 + 1)], psA[0:16, :]), 've')
    tk_et = B.tok('ve')

    def tpk_chunk(i):
        @B.on('ve')
        def _(e, i=i):
            if i == 0:
                e.tensor_copy(work[0:16, :], sc_et[0:16, :])
            e.max(out=valb[0:16, 8 * i:8 * i + 8], in_=work[0:16, :])
            e.max_index(idxb[0:16, 8 * i:8 * i + 8], valb[0:16, 8 * i:8 * i + 8], work[0:16, :])
            e.match_replace(out=work[0:16, :], in_to_replace=valb[0:16, 8 * i:8 * i + 8],
                            in_values=work[0:16, :], imm_value=-1.0)
    st._tpk = [lambda i=i: tpk_chunk(i) for i in range(32)]

    tk_ld = [None] * 8
    tk_use = [None] * 8

    @B.on('sy')
    def _(e):
        B.wait(e, 'sy', tk_p5ve)
        src = bass.AP(st.din['wsdT'].tensor, 0, [[DM, 128], [DM * 128, 16], [1, DM]])
        B.inc(e.dma_start(wsd_res[:, :, :], src), 'dW', 16)
    tk_wsd = B.tok('dW')

    tk_acc = []
    st._shg = ('se', B.c['se'])
    for th in range(2):
        plan_sh = B.se_plan()
        for shc in range(16):
            u = (2 * (16 * th + shc)) % 8
            @B.on('sy')
            def _(e, th=th, shc=shc, u=u):
                if tk_use[u] is not None:
                    B.wait(e, 'sy', tk_use[u])
                if tk_use[u + 1 if u + 1 < 8 else 0] is not None:
                    B.wait(e, 'sy', tk_use[(u + 1) % 8])
                for wi, wn in enumerate(['wsgT', 'wsuT']):
                    src = bass.AP(st.din[wn].tensor, 128 * shc,
                                  [[FSH, 128], [FSH * 128, 8], [1, 128]])
                    tk_ld[(u + wi) % 8] = B.inc(e.dma_start(ring[(u + wi) % 8][:, :, :], src),
                                                'dW', 16)

            @B.on('te')
            def _(e, th=th, shc=shc, u=u):
                B.wait(e, 'te', tk_ld[u], tk_ld[(u + 1) % 8], st.tk_p5_done)
                B.wait(e, 'te', ('ve', B.c['ve']))
                B.wait(e, 'te', st._shg)
                ins = None
                for f in range(8):
                    for half in range(2):
                        ins = e.matmul(psS[:, 512 * half:512 * (half + 1)],
                                       ring[u][:, f, :],
                                       xn16[:, f, 1024 * th + 512 * half:1024 * th + 512 * (half + 1)],
                                       start=(f == 0), stop=(f == 7))
                for f in range(8):
                    for half in range(2):
                        ins = e.matmul(psS[:, 1024 + 512 * half:1024 + 512 * (half + 1)],
                                       ring[(u + 1) % 8][:, f, :],
                                       xn16[:, f, 1024 * th + 512 * half:1024 * th + 512 * (half + 1)],
                                       start=(f == 0), stop=(f == 7))
                B.inc(ins, 'te')
            tk_use[u] = tk_use[(u + 1) % 8] = B.tok('te')
            tk_mm = B.tok('te')

            tk_g = plan_sh.add(
                (lambda tk_mm, tk_ve: lambda e: (B.wait(e, 'se', tk_mm),
                                                 B.wait(e, 'se', tk_ve),
                                                 e.activation(gst, psS[:, 0:1024], AF.Gelu))[-1]
                 )(tk_mm, ('ve', B.c['ve'])))
            st._shg = tk_g

            @B.on('ve')
            def _(e, th=th, shc=shc, tk_g=tk_g):
                B.wait(e, 've', tk_g)
                B.inc(e.tensor_tensor(hsh[:, shc, :], gst, psS[:, 1024:2048], Op.mult), 've')
            if st._tpk:
                st._tpk.pop(0)()
        plan_sh.flush()
        tk_hsh = B.tok('ve')

        for tb in range(8):
            half = 1024 * (tb % 2)
            @B.on('te')
            def _(e, th=th, tb=tb, half=half):
                B.wait(e, 'te', tk_hsh, tk_wsd)
                if len(tk_acc) >= 2:
                    B.wait(e, 'te', tk_acc[-2][0])
                ins = None
                for shc in range(16):
                    for q in range(2):
                        ins = e.matmul(psS[:, half + 512 * q:half + 512 * (q + 1)],
                                       hsh[:, shc, 128 * tb:128 * (tb + 1)],
                                       wsd_res[:, shc, 512 * q:512 * (q + 1)],
                                       start=(shc == 0), stop=(shc == 15))
                B.inc(ins, 'te')
            tk_mm = B.tok('te')

            @B.on('ve')
            def _(e, tb=tb, half=half, tk_mm=tk_mm):
                B.wait(e, 've', tk_mm)
                if len(tk_acc) >= 2:
                    B.wait(e, 've', tk_acc[-2][1])
                B.inc(e.tensor_copy(shst[tb % 2], psS[:, half:half + 1024]), 've')
            tk_ev = B.tok('ve')

            @B.on('sy')
            def _(e, th=th, tb=tb, tk_ev=tk_ev):
                B.wait(e, 'sy', tk_ev)
                r0 = 1024 * th + 128 * tb
                B.inc(e.dma_start(st.scr['acc'][r0:r0 + 128, :], shst[tb % 2]), 'dW', 16)
            tk_acc.append((tk_ev, B.tok('dW')))
    tk_shared_done = B.tok('dW')
    tk_hsh_free = ('te', B.c['te'])

    for f in st._tpk:
        f()
    st._tpk = []
    tk_topk = B.tok('ve')

    @B.on('gp')
    def _(e):
        B.wait(e, 'gp', tk_topk)
        B.inc(e.dma_start(st.scr['idx'], idxb[0:16, :]), 'dG', 16)
        B.inc(e.dma_start(st.scr['val'], valb[0:16, :]), 'dG', 16)
    tk_bnc = B.tok('dG')

    @B.on('gp')
    def _(e):
        B.wait(e, 'gp', tk_bnc)
        with B.nc.allow_non_contiguous_dma(reason="tiny routing relayouts"):
            for ex in range(EXP):
                for rep in range(2):
                    src = bass.AP(st.scr['idx'].tensor, 256 * ex, [[1, 16], [16, 16]])
                    B.inc(e.dma_start(idx16[16 * rep:16 * rep + 16, 16 * ex:16 * (ex + 1)],
                                      src), 'dG', 16)
                srcv = bass.AP(st.scr['val'].tensor, 256 * ex, [[1, 128], [128, 2]])
                B.inc(e.dma_start(wv[:, 2 * ex:2 * (ex + 1)], srcv), 'dG', 16)
    tk_rel = B.tok('dG')

    tk_gather = [None, None]
    tk_xuse = [None, None]
    tk_scat = None
    tk_eo = None
    st._tk_wd_prev = None

    def gather(ex):
        @B.on('gp')
        def _(e, ex=ex):
            B.wait(e, 'gp', tk_rel, tk_hsh_free)
            if tk_xuse[ex % 2] is not None:
                B.wait(e, 'gp', tk_xuse[ex % 2])
            tk_gather[ex % 2] = B.inc(
                e.dma_gather(xsel[ex % 2][:, :, :], st.scr['xn16'],
                             idx16[:, 16 * ex:16 * (ex + 1)], CAP, CAP, DM,
                             transpose=True), 'dG', 16)

    gather(0)
    st._exg = ('se', B.c['se'])
    for ex in range(EXP):
        if ex + 1 < EXP:
            gather(ex + 1)
        plan_ex = B.se_plan()
        for oc in range(16):
            u = (2 * oc) % 8
            @B.on('sy')
            def _(e, ex=ex, oc=oc, u=u):
                for wi in range(2):
                    if tk_use[(u + wi) % 8] is not None:
                        B.wait(e, 'sy', tk_use[(u + wi) % 8])
                for wi, wn in enumerate(['wgT', 'wuT']):
                    src = bass.AP(st.din[wn].tensor, ex * DM * FSH + 128 * oc,
                                  [[FSH, 128], [FSH * 128, 8], [1, 128]])
                    tk_ld[(u + wi) % 8] = B.inc(e.dma_start(ring[(u + wi) % 8][:, :, :], src),
                                                'dW', 16)

            @B.on('te')
            def _(e, ex=ex, oc=oc, u=u):
                B.wait(e, 'te', tk_ld[u], tk_ld[(u + 1) % 8], tk_gather[ex % 2])
                B.wait(e, 'te', ('ve', B.c['ve']))
                B.wait(e, 'te', st._exg)
                ins = None
                for f in range(8):
                    ins = e.matmul(psA[:, 0:256], ring[u][:, f, :], xsel[ex % 2][:, f, :],
                                   start=(f == 0), stop=(f == 7))
                for f in range(8):
                    ins = e.matmul(psB[:, 0:256], ring[(u + 1) % 8][:, f, :], xsel[ex % 2][:, f, :],
                                   start=(f == 0), stop=(f == 7))
                B.inc(ins, 'te')
            tk_use[u] = tk_use[(u + 1) % 8] = B.tok('te')
            if oc == 15:
                tk_xuse[ex % 2] = B.tok('te')
            tk_mm = B.tok('te')

            tk_g = plan_ex.add(
                (lambda tk_mm, tk_ve: lambda e: (B.wait(e, 'se', tk_mm),
                                                 B.wait(e, 'se', tk_ve),
                                                 e.activation(gst2, psA[:, 0:256], AF.Gelu))[-1]
                 )(tk_mm, ('ve', B.c['ve'])))
            st._exg = tk_g

            @B.on('ve')
            def _(e, ex=ex, oc=oc, tk_g=tk_g):
                B.wait(e, 've', tk_g)
                if oc == 0 and st._tk_wd_prev is not None:
                    B.wait(e, 've', st._tk_wd_prev)
                B.inc(e.tensor_tensor(he[:, oc, :], gst2, psB[:, 0:256], Op.mult), 've')
        plan_ex.flush()
        tk_he = B.tok('ve')

        for oc in range(16):
            u = oc % 8
            @B.on('sy')
            def _(e, ex=ex, oc=oc, u=u):
                if tk_use[u] is not None:
                    B.wait(e, 'sy', tk_use[u])
                src = bass.AP(st.din['wdT'].tensor, ex * FSH * DM + 128 * oc * DM,
                              [[DM, 128], [128, 8], [1, 128]])
                tk_ld[u] = B.inc(e.dma_start(ring[u][:, :, :], src), 'dW', 16)

            @B.on('te')
            def _(e, ex=ex, oc=oc, u=u):
                B.wait(e, 'te', tk_ld[u], tk_he)
                if oc == 0:
                    B.wait(e, 'te', ('ve', B.c['ve']))
                ins = None
                rflat = ring[u].rearrange("p c n -> p (c n)")
                for sh in range(2):
                    for q in range(2):
                        ins = e.matmul(psS[:, 1024 * sh + 512 * q:1024 * sh + 512 * (q + 1)],
                                       he[:, oc, 128 * sh:128 * (sh + 1)],
                                       rflat[:, 512 * q:512 * (q + 1)],
                                       start=(oc == 0), stop=(oc == 15))
                B.inc(ins, 'te')
            tk_use[u] = B.tok('te')
        tk_wd = B.tok('te')
        st._tk_wd_prev = tk_wd

        @B.on('ve')
        def _(e, ex=ex):
            B.wait(e, 've', tk_wd, tk_rel, tk_shared_done)
            if tk_scat is not None:
                B.wait(e, 've', tk_scat)
            for sh in range(2):
                ins = e.tensor_scalar(eo_tm[:, sh, :], psS[:, 1024 * sh:1024 * (sh + 1)],
                                      wv[:, 2 * ex + sh:2 * ex + sh + 1], None, Op.mult)
            B.inc(ins, 've')
        tk_eo = B.tok('ve')

        @B.on('gp')
        def _(e, ex=ex, tk_eo=tk_eo):
            B.wait(e, 'gp', tk_eo, tk_shared_done)
            B.inc(e.dma_scatter_add(st.scr['acc'], eo_tm[:, :, :],
                                    idx16[:, 16 * ex:16 * (ex + 1)], CAP, CAP, DM), 'dG', 16)
        tk_scat = B.tok('dG')

    g1 = Rb(O_C, 16384, c=8)
    st.g1 = g1
    ast = [R(O_A + 8192 + 1024 * i, 1024) for i in range(2)]
    bst = [Rb(O_A + 10240 + 512 * i, 1024) for i in range(2)]
    tk_tt = [None, None]
    plan_g1 = B.se_plan()
    for tt in range(16):
        @B.on('sy')
        def _(e, tt=tt):
            B.wait(e, 'sy', tk_scat)
            if tk_tt[tt % 2] is not None:
                B.wait(e, 'sy', tk_tt[tt % 2])
            B.inc(e.dma_start(ast[tt % 2], st.scr['acc'][128 * tt:128 * (tt + 1), :]), 'dW', 16)
        tk_a = B.tok('dW')

        tk_s = plan_g1.add(
            (lambda tt, tk_a, tk_te: lambda e: (B.wait(e, 'se', tk_a),
                                                B.wait(e, 'se', tk_te),
                                                e.activation(bst[tt % 2], ast[tt % 2], AF.Silu))[-1]
             )(tt, tk_a, ('te', B.c['te'])))

        @B.on('te')
        def _(e, tt=tt, tk_s=tk_s):
            B.wait(e, 'te', tk_s)
            B.wait(e, 'te', ('ve', B.c['ve']))
            for c in range(8):
                ins = e.transpose(psT[:, 128 * c:128 * (c + 1)],
                                  bst[tt % 2].rearrange("p (c n) -> p c n", c=8)[:, c, :],
                                  cst['identb'][:, :])
            B.inc(ins, 'te')
        tk_tt[tt % 2] = B.tok('te')
        tkt = B.tok('te')

        @B.on('ve')
        def _(e, tt=tt, tkt=tkt):
            B.wait(e, 've', tkt)
            for c in range(8):
                ins = e.tensor_copy(g1[:, c, 128 * tt:128 * (tt + 1)],
                                    psT[:, 128 * c:128 * (c + 1)])
            B.inc(ins, 've')
    plan_g1.flush()
    st.tk_p6_done = B.tok('ve')


def ph_p7_l1_attn(st):
    import os
    DIS = os.environ.get('P7DIS', '').split(',')
    B = st.B
    R, Rb, Ri = st.R, st.Rb, st.Ri
    cst, sm = st.cst, st.sm
    psS, psA, psB, psC, psT = st.psS, st.psA, st.psB, st.psC, st.psT
    psTf = psT[:, :].bitcast(F32)
    pool = st.pool

    xn1 = Rb(O_A, 16384, c=8)
    gsig = Rb(O_A + 8192, 16384, c=8)
    xmid = R(O_A, N_A, c=8)
    attn_full = Rb(O_B, 16384, c=8)
    q2 = Rb(O_B + 8192, 2048)
    k2 = Rb(O_B + 9216, 2048)
    v2 = Rb(O_B + 10240, 2048)
    vtm = Rb(O_B + 11264, 2048, c=16)
    aqe = Rb(O_B + 12288, 2048)
    attnT = Rb(O_B + 13312, 2048, c=16)
    ubuf = R(O_B + 14336, 2048)
    cw_res = pool[:, O_D + 3072:O_D + 3584].bitcast(BF16)
    drive = R(O_D, 2048)
    states = pool[:, O_D + 2048:O_D + 3072].bitcast(BF16)
    ring = [Rb(O_D + 3584 + 512 * i, 1024, c=8) for i in range(3)]
    st.xmid = xmid

    tk_p6ve = ('ve', B.c['ve'])
    tk_p6te = ('te', B.c['te'])

    @B.on('sy')
    def _(e):
        B.wait(e, 'sy', st.tk_p6_done, tk_p6te)
        B.wait(e, 'sy', ('se', B.c['se']))
        for c in range(8):
            B.inc(e.dma_start(xn1[:, c, :], st.scr['xn1'][:, c, :]), 'dW', 16)
        B.inc(e.dma_start(cw_res, st.din['cwT']), 'dW', 16)
    tk_xn1 = B.tok('dW')

    tk_ld = [None] * 3
    tk_use = [None] * 3

    def load_pair(p, slot):
        @B.on('sy')
        def _(e, p=p, slot=slot):
            if tk_use[slot] is not None:
                B.wait(e, 'sy', tk_use[slot])
            pass

    for p in range(8):
        for wi in range(3):
            @B.on('sy')
            def _(e, p=p, wi=wi):
                if tk_use[wi] is not None:
                    B.wait(e, 'sy', tk_use[wi])
                if p == 0 and wi == 0:
                    B.wait(e, 'sy', st.tk_p6_done)
                src = bass.AP(st.din['wqkvT'].tensor, 1024 * wi + 128 * p,
                              [[3 * DM, 128], [3 * DM * 128, 8], [1, 128]])
                tk_ld[wi] = B.inc(e.dma_start(ring[wi][:, :, :], src), 'dW', 16)

        for wi, dst in enumerate([q2, k2, v2]):
            @B.on('te')
            def _(e, p=p, wi=wi):
                B.wait(e, 'te', tk_ld[wi], tk_xn1)
                B.wait(e, 'te', ('ve', B.c['ve']))
                ins = None
                for f in range(8):
                    for n in range(4):
                        ins = e.matmul(psS[:, 512 * n:512 * (n + 1)], ring[wi][:, f, :],
                                       xn1[:, f, 512 * n:512 * (n + 1)],
                                       start=(f == 0), stop=(f == 7))
                B.inc(ins, 'te')
            tk_use[wi] = B.tok('te')
            tk_mm = B.tok('te')

            @B.on('ve')
            def _(e, dst=dst, tk_mm=tk_mm):
                B.wait(e, 've', tk_mm)
                B.inc(e.tensor_copy(dst, psS[:, :]), 've')

        for g in range(2):
            @B.on('te')
            def _(e, g=g):
                B.wait(e, 'te', ('ve', B.c['ve']))
                for kt in range(8):
                    ins = e.transpose(psT[:, 128 * kt:128 * (kt + 1)],
                                      v2[:, 128 * (8 * g + kt):128 * (8 * g + kt + 1)],
                                      cst['identb'][:, :])
                B.inc(ins, 'te')
            tkv = B.tok('te')

            @B.on('ve')
            def _(e, g=g, tkv=tkv):
                B.wait(e, 've', tkv)
                for kt in range(8):
                    ins = e.tensor_copy(vtm[:, 8 * g + kt, :], psT[:, 128 * kt:128 * (kt + 1)])
                B.inc(ins, 've')

        tk_qkv_p = ('ve', B.c['ve'])
        plan_at = B.se_plan()
        for hl in range(2):
            hb = 64 * hl
            for c in range(8):
                for qt in range(2):
                    qg = 256 * c + 128 * qt
                    if c == 0:
                        k0 = 0
                        nk = 128 * (qt + 1)
                        moff = 384 - nk
                    else:
                        k0 = 256 * (c - 1) + 128 * qt
                        nk = 384
                        moff = 0
                    if not hasattr(st, '_l1cnt'):
                        st._l1cnt = 0
                        st._l1exp = [None, None]
                    sb = [psA, psC][st._l1cnt % 2]
                    sbi = st._l1cnt % 2
                    st._l1cnt += 1
                    @B.on('te')
                    def _(e, hb=hb, qg=qg, k0=k0, nk=nk, sb=sb, sbi=sbi):
                        if st._l1exp[sbi] is not None:
                            B.wait(e, 'te', st._l1exp[sbi])
                        B.wait(e, 'te', tk_qkv_p)
                        B.inc(e.matmul(sb[:, 0:nk], q2[hb:hb + 64, qg:qg + 128],
                                       k2[hb:hb + 64, k0:k0 + nk], start=True, stop=True),
                              'te')
                    tk_sc = B.tok('te')

                    @B.on('ve')
                    def _(e, nk=nk, moff=moff, sb=sb, tk_sc=tk_sc):
                        B.wait(e, 've', tk_sc)
                        B.inc(e.tensor_tensor(sb[:, 0:nk], sb[:, 0:nk],
                                              cst['maskL1'][:, moff:moff + nk], Op.add), 've')
                    tk_msk = B.tok('ve')

                    tk_exp = plan_at.add(
                        (lambda nk, sb, tk_msk: lambda e: (
                            B.wait(e, 'se', tk_msk),
                            e.activation(aqe[:, 0:nk], sb[:, 0:nk], AF.Exp,
                                         bias=st.cv['neg8'], scale=0.125,
                                         accum_out=sm[:, 128:129]))[-1]
                         )(nk, sb, tk_msk))
                    st._l1exp[sbi] = tk_exp

                    @B.on('ve')
                    def _(e, nk=nk, tk_exp=tk_exp):
                        B.wait(e, 've', tk_exp)
                        e.reciprocal(sm[:, 129:130], sm[:, 128:129])
                        B.inc(e.tensor_scalar(aqe[:, 0:nk], aqe[:, 0:nk],
                                              sm[:, 129:130], None, Op.mult), 've')
                    tk_nrm = B.tok('ve')

                    nkt = nk // 128
                    @B.on('te')
                    def _(e, nkt=nkt, tk_nrm=tk_nrm):
                        B.wait(e, 'te', tk_nrm)
                        B.wait(e, 'te', ('ve', B.c['ve']))
                        for kt in range(nkt):
                            ins = e.transpose(psT[:, 128 * kt:128 * (kt + 1)],
                                              aqe[:, 128 * kt:128 * (kt + 1)],
                                              cst['identb'][:, :])
                        B.inc(ins, 'te')
                    tk_tp = B.tok('te')

                    @B.on('ve')
                    def _(e, nkt=nkt, tk_tp=tk_tp):
                        B.wait(e, 've', tk_tp)
                        for kt in range(nkt):
                            ins = e.tensor_copy(attnT[:, kt, :], psT[:, 128 * kt:128 * (kt + 1)])
                        B.inc(ins, 've')
                    tk_aT = B.tok('ve')

                    @B.on('te')
                    def _(e, hb=hb, c=c, qt=qt, k0=k0, nkt=nkt, tk_aT=tk_aT):
                        B.wait(e, 'te', tk_aT)
                        if qt == 0:
                            B.wait(e, 'te', ('ve', B.c['ve']))
                        for kt in range(nkt):
                            vi = (k0 // 128) + kt
                            ins = e.matmul(psB[hb:hb + 64, 128 * qt:128 * (qt + 1)],
                                           vtm[:, vi, hb:hb + 64], attnT[:, kt, :],
                                           start=(kt == 0), stop=(kt == nkt - 1),
                                           tile_position=(0, hb))
                        B.inc(ins, 'te')
                    tk_av = B.tok('te')

                @B.on('ve')
                def _(e, p=p, hb=hb, c=c, tk_av=tk_av):
                    B.wait(e, 've', tk_av)
                    B.inc(e.tensor_copy(attn_full[hb:hb + 64, p, 256 * c:256 * (c + 1)],
                                        psB[hb:hb + 64, 0:256]), 've')
        plan_at.flush()
    tk_attn = B.tok('ve')

    st._ugsig = ('se', B.c['se'])
    plan_ug = B.se_plan()
    for j in range(9):
        wi = j % 3
        @B.on('sy')
        def _(e, j=j, wi=wi):
            B.wait(e, 'sy', tk_use[wi])
            src = bass.AP(st.din['wugT'].tensor, 128 * j,
                          [[1152, 128], [1152 * 128, 8], [1, 128]])
            tk_ld[wi] = B.inc(e.dma_start(ring[wi][:, :, :], src), 'dW', 16)

        @B.on('te')
        def _(e, j=j, wi=wi):
            B.wait(e, 'te', tk_ld[wi])
            B.wait(e, 'te', ('ve', B.c['ve']))
            B.wait(e, 'te', st._ugsig)
            ins = None
            for f in range(8):
                for n in range(4):
                    ins = e.matmul(psS[:, 512 * n:512 * (n + 1)], ring[wi][:, f, :],
                                   xn1[:, f, 512 * n:512 * (n + 1)],
                                   start=(f == 0), stop=(f == 7))
            B.inc(ins, 'te')
        tk_use[wi] = B.tok('te')
        tk_mm = B.tok('te')

        if j == 0:
            @B.on('ve')
            def _(e, tk_mm=tk_mm):
                B.wait(e, 've', tk_mm)
                B.inc(e.tensor_scalar(ubuf, psS[:, :], cst['bug'][:, 0:1], None, Op.add), 've')
        else:
            st._ugsig = plan_ug.add(
                (lambda j, tk_mm: lambda e: (
                    B.wait(e, 'se', tk_mm),
                    e.activation(gsig[:, j - 1, :], psS[:, :], AF.Sigmoid,
                                 bias=cst['bug'][:, j:j + 1], scale=1.0))[-1]
                 )(j, tk_mm))
    plan_ug.flush()
    tk_ug = B.tok('se')

    @B.on('sy')
    def _(e):
        B.wait(e, 'sy', tk_use[0])
        src = bass.AP(st.din['bwT'].tensor, 0, [[SD, 128], [SD * 128, 8], [1, 128]])
        tk_ld[0] = B.inc(e.dma_start(ring[0][:, :, :], src), 'dW', 16)

    @B.on('te')
    def _(e):
        B.wait(e, 'te', tk_ld[0])
        B.wait(e, 'te', ('ve', B.c['ve']))
        B.wait(e, 'te', st._ugsig)
        ins = None
        for f in range(8):
            for n in range(4):
                ins = e.matmul(psS[:, 512 * n:512 * (n + 1)], ring[0][:, f, :],
                               xn1[:, f, 512 * n:512 * (n + 1)],
                               start=(f == 0), stop=(f == 7))
        B.inc(ins, 'te')
    tk_use[0] = B.tok('te')
    tk_bw = B.tok('te')

    @B.on('ve')
    def _(e):
        B.wait(e, 've', tk_bw)
        B.inc(e.tensor_tensor(drive, psS[:, :], ubuf, Op.add), 've')
        B.bar(e, 've')
        B.inc(e.tensor_tensor_scan(states, cst['Avec'][:, 0:1].to_broadcast([128, T]),
                                   drive, 0.0, Op.mult, Op.add), 've')
    tk_scan = B.tok('ve')

    st.attn_full1 = attn_full
    st.gsig = gsig
    st.states_ap = states
    st.drive_ap = drive
    if 'combine' in DIS:
        st.tk_p7_done = B.tok('ve')
        return
    st1 = R(O_B + 12288, 2048)
    st2 = R(O_B + 8192, 2048)
    xnst = [R(O_B + 10240, 2048), R(O_B + 14336, 2048)]
    tk_xnld = [None, None]
    tk_cmb = [None, None]
    for d in range(8):
        @B.on('sy')
        def _(e, d=d):
            B.wait(e, 'sy', tk_attn, tk_scan)
            if tk_cmb[d % 2] is not None:
                B.wait(e, 'sy', tk_cmb[d % 2])
            tk_xnld[d % 2] = B.inc(e.dma_start(xnst[d % 2], st.scr['xn32'][:, d, :]), 'dW', 16)

        wi = d % 3
        @B.on('sy')
        def _(e, d=d, wi=wi):
            B.wait(e, 'sy', tk_use[wi])
            src = bass.AP(st.din['l1_woT'].tensor, 128 * d, [[DM, 128], [DM * 128, 8], [1, 128]])
            tk_ld[wi] = B.inc(e.dma_start(ring[wi][:, :, :], src), 'dW', 16)

        @B.on('te')
        def _(e, d=d, wi=wi):
            B.wait(e, 'te', tk_ld[wi], tk_attn, tk_scan)
            B.wait(e, 'te', ('ve', B.c['ve']))
            ins = None
            for f in range(8):
                for n in range(4):
                    ins = e.matmul(psS[:, 512 * n:512 * (n + 1)], ring[wi][:, f, :],
                                   attn_full[:, f, 512 * n:512 * (n + 1)],
                                   start=(f == 0), stop=(f == 7))
            ybank = [psA, psB, psC, psTf]
            for n in range(4):
                ins = e.matmul(ybank[n][:, 0:512], cw_res[:, 128 * d:128 * (d + 1)],
                               states[:, 512 * n:512 * (n + 1)], start=True, stop=True)
            B.inc(ins, 'te')
        tk_use[wi] = B.tok('te')
        tk_mm = B.tok('te')

        @B.on('ve')
        def _(e, d=d, tk_mm=tk_mm):
            B.wait(e, 've', tk_mm, tk_ug, tk_xnld[d % 2])
            ybank = [psA, psB, psC, psTf]
            e.tensor_scalar(st1, psS[:, :], cst['bO'][:, d:d + 1], None, Op.add)
            for n in range(4):
                e.tensor_tensor(st2[:, 512 * n:512 * (n + 1)], st1[:, 512 * n:512 * (n + 1)],
                                ybank[n][:, 0:512], Op.subtract)
            e.tensor_tensor(st2, st2, gsig[:, d, :], Op.mult)
            for n in range(4):
                e.tensor_tensor(st2[:, 512 * n:512 * (n + 1)], st2[:, 512 * n:512 * (n + 1)],
                                ybank[n][:, 0:512], Op.add)
            tk_cmb[d % 2] = B.inc(e.tensor_tensor(xmid[:, d, :], st2, xnst[d % 2], Op.add), 've')
    tk_xmid = B.tok('ve')
    if 'mlp' in DIS:
        st.tk_p7_done = tk_xmid
        return

    x_tm = R(O_B, 16384, c=16)
    for c in range(8):
        @B.on('te')
        def _(e, c=c):
            B.wait(e, 'te', tk_xmid)
            B.wait(e, 'te', ('ve', B.c['ve']))
            for tt in range(16):
                ins = e.transpose(psS[:, 128 * tt:128 * (tt + 1)],
                                  xmid[:, c, 128 * tt:128 * (tt + 1)], cst['identf'][:, :])
            B.inc(ins, 'te')
        tkt = B.tok('te')

        @B.on('ve')
        def _(e, c=c, tkt=tkt):
            B.wait(e, 've', tkt)
            for tt in range(16):
                ins = e.tensor_copy(x_tm[:, tt, 128 * c:128 * (c + 1)],
                                    psS[:, 128 * tt:128 * (tt + 1)])
            B.inc(ins, 've')
    tk_tm = B.tok('ve')

    @B.on('ve')
    def _(e):
        B.wait(e, 've', tk_tm)
        for tt in range(16):
            ins = e.tensor_reduce(sm[:, tt:tt + 1], x_tm[:, tt, :], mybir.AxisListType.X, Op.add)
        B.inc(ins, 've')
        B.bar(e, 've')
        B.inc(e.tensor_scalar(sm[:, 16:32], sm[:, 0:16], -1.0 / DM, None, Op.mult), 've')
    tk_mu = B.tok('ve')

    @B.on('se')
    def _(e):
        B.wait(e, 'se', tk_mu)
        for tt in range(16):
            B.inc(e.activation(pool[:, O_D:O_D + 1024], x_tm[:, tt, :],
                               AF.Square, accum_out=sm[:, 32 + tt:33 + tt]), 'se')
    tk_sq = B.tok('se')

    @B.on('ve')
    def _(e):
        B.wait(e, 've', tk_sq)
        e.tensor_tensor(sm[:, 48:64], sm[:, 16:32], sm[:, 16:32], Op.mult)
        e.tensor_scalar(sm[:, 112:128], sm[:, 32:48], 1.0 / DM, None, Op.mult)
        B.inc(e.tensor_tensor(sm[:, 48:64], sm[:, 112:128], sm[:, 48:64], Op.subtract), 've')
    tk_var = B.tok('ve')

    @B.on('se')
    def _(e):
        B.wait(e, 'se', tk_var)
        B.inc(e.activation(sm[:, 48:64], sm[:, 48:64], AF.Sqrt, bias=st.cv['eps'], scale=1.0), 'se')
    tk_sqv = B.tok('se')

    @B.on('ve')
    def _(e):
        B.wait(e, 've', tk_sqv)
        B.inc(e.reciprocal(sm[:, 48:64], sm[:, 48:64]), 've')
        B.bar(e, 've')
        for tt in range(16):
            B.inc(e.tensor_scalar(x_tm[:, tt, :], x_tm[:, tt, :],
                                  sm[:, 16 + tt:17 + tt], sm[:, 48 + tt:49 + tt],
                                  Op.add, Op.mult), 've')
    tk_h2n = B.tok('ve')

    h2stg = [pool[:, O_D + i * 1024:O_D + i * 1024 + 1024].bitcast(BF16) for i in range(2)]
    tk_h2sp = [None, None]
    for c in range(8):
        @B.on('te')
        def _(e, c=c):
            B.wait(e, 'te', tk_h2n)
            B.wait(e, 'te', ('ve', B.c['ve']))
            for tt in range(16):
                ins = e.transpose(psS[:, 128 * tt:128 * (tt + 1)],
                                  x_tm[:, tt, 128 * c:128 * (c + 1)], cst['identf'][:, :])
            B.inc(ins, 'te')
        tkt = B.tok('te')

        @B.on('ve')
        def _(e, c=c, tkt=tkt):
            B.wait(e, 've', tkt)
            if tk_h2sp[c % 2] is not None:
                B.wait(e, 've', tk_h2sp[c % 2])
            B.inc(e.tensor_scalar(h2stg[c % 2], psS[:, :], cst['ln2g'][:, c:c + 1],
                                  cst['ln2b'][:, c:c + 1], Op.mult, Op.add), 've')
        tke = B.tok('ve')

        @B.on('sy')
        def _(e, c=c, tke=tke):
            B.wait(e, 'sy', tke)
            B.inc(e.dma_start(st.scr['h2'][:, c, :], h2stg[c % 2]), 'dW', 16)
        tk_h2sp[c % 2] = B.tok('dW')
    tk_h2 = B.tok('dW')

    h3q = Rb(O_B, 16384, c=32)
    h2buf = [Rb(O_D + 0, 4096, c=8), Rb(O_D + 2048, 4096, c=8)]
    st.h3q = h3q
    st.h2buf1 = h2buf[1]
    bank8 = [psS[:, 0:512], psS[:, 512:1024], psS[:, 1024:1536], psS[:, 1536:2048],
             psA, psB, psC, psTf]
    mring = [Rb(O_B + 8192 + 512 * i, 1024, c=8) for i in range(4)]
    tk_mld = [None] * 4
    tk_muse = [None] * 4
    tk_h2ld = [None, None]
    for q in range(4):
        @B.on('sy')
        def _(e, q=q):
            B.wait(e, 'sy', tk_h2)
            if tk_h2ld[q % 2] is not None:
                B.wait(e, 'sy', ('te', B.c['te']))
            for f in range(8):
                ins = e.dma_start(h2buf[q % 2][:, f, :],
                                  st.scr['h2'][:, f, 512 * q:512 * (q + 1)])
                B.inc(ins, 'dW', 16)
        tk_h2ld[q % 2] = B.tok('dW')

        plan_w1 = B.se_plan()
        g_tok = [None] * 32
        for hc in range(32):
            u = hc % 4
            @B.on('sy')
            def _(e, q=q, hc=hc, u=u):
                if tk_muse[u] is not None:
                    B.wait(e, 'sy', tk_muse[u])
                if q == 0 and hc == 0:
                    B.wait(e, 'sy', ('te', B.c['te']))
                src = bass.AP(st.din['mw1T'].tensor, 128 * hc,
                              [[MLPH, 128], [MLPH * 128, 8], [1, 128]])
                tk_mld[u] = B.inc(e.dma_start(mring[u][:, :, :], src), 'dW', 16)

            pb = [psA, psB][hc % 2]
            @B.on('te')
            def _(e, q=q, hc=hc, u=u, pb=pb):
                B.wait(e, 'te', tk_mld[u], tk_h2ld[q % 2])
                if hc == 0:
                    B.wait(e, 'te', ('ve', B.c['ve']))
                if hc >= 2:
                    B.wait(e, 'te', g_tok[hc - 2])
                ins = None
                for f in range(8):
                    ins = e.matmul(pb[:, 0:512], mring[u][:, f, :], h2buf[q % 2][:, f, :],
                                   start=(f == 0), stop=(f == 7))
                B.inc(ins, 'te')
            tk_muse[u] = B.tok('te')
            tk_mm = B.tok('te')

            g_tok[hc] = plan_w1.add(
                (lambda q, hc, pb, tk_mm: lambda e: (
                    B.wait(e, 'se', tk_mm),
                    e.activation(h3q[:, hc, :], pb[:, 0:512], AF.Gelu,
                                 bias=cst['mb1'][:, hc:hc + 1], scale=1.0))[-1]
                 )(q, hc, pb, tk_mm))
        plan_w1.flush()
        tk_h3 = B.tok('se')

        for hc in range(32):
            u = hc % 4
            @B.on('sy')
            def _(e, q=q, hc=hc, u=u):
                B.wait(e, 'sy', tk_muse[u])
                src = bass.AP(st.din['mw2T'].tensor, (128 * hc) * DM,
                              [[DM, 128], [128, 8], [1, 128]])
                tk_mld[u] = B.inc(e.dma_start(mring[u][:, :, :], src), 'dW', 16)

            @B.on('te')
            def _(e, q=q, hc=hc, u=u):
                B.wait(e, 'te', tk_mld[u], tk_h3)
                if hc == 0:
                    B.wait(e, 'te', ('ve', B.c['ve']))
                ins = None
                for d in range(8):
                    ins = e.matmul(bank8[d][:, 0:512], mring[u][:, d, :], h3q[:, hc, :],
                                   start=(hc == 0), stop=(hc == 31))
                B.inc(ins, 'te')
            tk_muse[u] = B.tok('te')
        tk_w2 = B.tok('te')

        @B.on('ve')
        def _(e, q=q, tk_w2=tk_w2):
            B.wait(e, 've', tk_w2)
            for d in range(8):
                ins = e.scalar_tensor_tensor(xmid[:, d, 512 * q:512 * (q + 1)],
                                             bank8[d][:, 0:512], cst['mb2'][:, d:d + 1],
                                             xmid[:, d, 512 * q:512 * (q + 1)],
                                             Op.add, Op.add)
            B.inc(ins, 've')
    st.tk_p7_done = B.tok('ve')


def ph_p8_final(st):
    B = st.B
    R, Rb = st.R, st.Rb
    cst = st.cst
    psS = st.psS
    pool = st.pool
    xmid = st.xmid
    g1 = st.g1

    Xst = [R(O_B + 2048 * i, 2048) for i in range(2)]
    tst = R(O_B + 4096, 2048)
    ost = [R(O_B + 6144 + 1024 * i, 1024) for i in range(2)]
    tk_x = [None, None]
    tk_c = [None, None]
    for d in range(8):
        @B.on('sy')
        def _(e, d=d):
            B.wait(e, 'sy', st.tk_p7_done)
            if tk_c[d % 2] is not None:
                B.wait(e, 'sy', tk_c[d % 2])
            tk_x[d % 2] = B.inc(e.dma_start(Xst[d % 2], st.scr['X'][:, d, :]), 'dW', 16)

        @B.on('ve')
        def _(e, d=d):
            B.wait(e, 've', st.tk_p7_done, tk_x[d % 2], st.tk_p6_done)
            e.tensor_tensor(tst, xmid[:, d, :], g1[:, d, :], Op.mult)
            tk_c[d % 2] = B.inc(e.tensor_tensor(xmid[:, d, :], tst, Xst[d % 2], Op.add), 've')
    tk_x2 = B.tok('ve')

    tk_out = [None, None]
    for tt in range(16):
        @B.on('te')
        def _(e, tt=tt):
            B.wait(e, 'te', tk_x2)
            B.wait(e, 'te', ('ve', B.c['ve']))
            for d in range(8):
                ins = e.transpose(psS[:, 128 * d:128 * (d + 1)],
                                  xmid[:, d, 128 * tt:128 * (tt + 1)], cst['identf'][:, :])
            B.inc(ins, 'te')
        tkt = B.tok('te')

        @B.on('ve')
        def _(e, tt=tt, tkt=tkt):
            B.wait(e, 've', tkt)
            if tk_out[tt % 2] is not None:
                B.wait(e, 've', tk_out[tt % 2])
            B.inc(e.tensor_copy(ost[tt % 2], psS[:, 0:1024]), 've')
        tke = B.tok('ve')

        @B.on('sy')
        def _(e, tt=tt, tke=tke):
            B.wait(e, 'sy', tke)
            B.inc(e.dma_start(st.dout[128 * tt:128 * (tt + 1), :], ost[tt % 2]), 'dO', 16)
        tk_out[tt % 2] = B.tok('dO')

    @B.on('sy')
    def _(e):
        e.wait_ge(B.S['dO'], B.c['dO'])



_BUILD_CACHE = {}


def kernel(**inputs):
    x = np.ascontiguousarray(np.asarray(inputs['L0_attn_x'], np.float32))
    shared = host_prepare({k: v for k, v in inputs.items() if k != 'L0_attn_x'})
    B = build_program(upto=99)
    in_maps = []
    for c in range(NCORE):
        m = {'x': np.ascontiguousarray(x[c])}
        m.update(shared)
        in_maps.append(m)
    res = run_bass_kernel_spmd(B.nc, in_maps, core_ids=list(range(NCORE)))
    out = np.stack([np.asarray(r['out'], np.float32) for r in res.results], axis=0)
    return out


# version 05 (final)
# speedup vs baseline: 1.4605x; 1.0815x over previous
import numpy as np
import ml_dtypes
import concourse.bass as bass
import concourse.mybir as mybir
from concourse.alu_op_type import AluOpType as Op
from concourse.bass_utils import run_bass_kernel_spmd
from concourse import library_config
from concourse.library_overlay import lower_extended_insts

F32 = mybir.dt.float32
BF16 = mybir.dt.bfloat16
I16 = mybir.dt.int16
U16 = mybir.dt.uint16
AF = mybir.ActivationFunctionType
nbf16 = ml_dtypes.bfloat16

T = 2048
DM = 1024
NCORE = 8
NH = 16
DH = 64
HID0 = 4096
WIN = 256
SD = 128
EXP = 16
CAP = 256
FSH = 2048
MLPH = 4096

ENG = {'te': 'tensor', 've': 'vector', 'se': 'scalar', 'gp': 'gpsimd', 'sy': 'sync'}


def _ap_span(ap):
    try:
        t = ap.tensor
        name = t.name
        off = ap.offset
        ext = 1
        for step, cnt in ap.ap:
            ext += abs(step) * (cnt - 1)
        sz = mybir.dt.size(ap.dtype)
        return (name, off * sz, (off + ext) * sz)
    except Exception:
        return None


class _EngProxy:
    _OPS = {
        'tensor_tensor', 'tensor_scalar', 'scalar_tensor_tensor', 'tensor_copy',
        'tensor_reduce', 'reduce_max', 'reduce_sum', 'tensor_add', 'tensor_sub',
        'tensor_mul', 'tensor_max', 'reciprocal', 'tensor_tensor_scan', 'select',
        'copy_predicated', 'memset', 'max', 'max_index', 'match_replace',
        'activation', 'add', 'mul', 'copy', 'sign', 'sqrt', 'square', 'iota',
        'transpose', 'affine_select', 'tensor_scalar_add', 'tensor_scalar_mul',
        'tensor_scalar_max', 'tensor_scalar_min', 'bn_stats', 'bn_aggr',
    }

    def __init__(self, B, eng, e):
        self._B = B
        self._eng = eng
        self._e = e

    def __getattr__(self, name):
        e = self._e
        f = getattr(e, name)
        if name not in self._OPS:
            return f
        B = self._B
        eng = self._eng

        def wrap(*args, **kw):
            spans = []
            for a in list(args) + list(kw.values()):
                if isinstance(a, bass.AP):
                    sp = _ap_span(a)
                    if sp:
                        spans.append(sp)
            lw = B.last_writes.get(eng, [])
            hit = False
            for (n1, l1, h1) in spans[1:] or spans:
                for (n2, l2, h2) in lw:
                    if n1 == n2 and l1 < h2 and l2 < h1:
                        hit = True
            if hit:
                if B.c[eng] > B.waited.get((eng, eng + '#self'), 0):
                    e.wait_ge(B.S[eng], B.c[eng])
                    B.waited[(eng, eng + '#self')] = B.c[eng]
                B.last_writes[eng] = []
            ins = f(*args, **kw)
            out = None
            for a in list(args) + list(kw.values()):
                if isinstance(a, bass.AP):
                    out = a
                    break
            if out is not None:
                sp = _ap_span(out)
                if sp:
                    lw = B.last_writes.setdefault(eng, [])
                    lw.append(sp)
                    if len(lw) > 2:
                        lw.pop(0)
            try:
                ins.then_inc(B.S[eng], 1)
                B.c[eng] += 1
            except Exception:
                pass
            return _NoInc(ins)
        return wrap


class _SePlan:

    def __init__(self, B):
        self.B = B
        self.base = B.c['se']
        self.ops = []

    def add(self, fn, *toks):
        self.ops.append((toks, fn))
        return ('se', self.base + len(self.ops))

    def flush(self):
        B = self.B
        assert B.c['se'] == self.base, (B.c['se'], self.base)
        ops = self.ops

        @B.on('se')
        def _(e):
            for toks, fn in ops:
                B.wait(e, 'se', *toks)
                fn(e)
        assert B.c['se'] == self.base + len(ops), (B.c['se'], self.base, len(ops))


class _NoInc:
    def __init__(self, ins):
        self.ins = ins

    def then_inc(self, sem, n):
        self.ins.then_inc(sem, n)
        return self

POOL_N = 46720
O_A = 0
O_B = 16384
O_C = 33024
O_D = 41216
N_A = 16384
N_B = 16640
N_C = 8192
N_D = 5504


class Builder:
    def __init__(self):
        self.nc = bass.Bass("TRN2", target_bir_lowering=False)
        self.c = {}
        self.waited = {}
        self.S = {}
        self.ctx = []
        self.t = {}
        self.last_writes = {}

    def sem(self, name):
        cm = self.nc.semaphore(name)
        h = cm.__enter__()
        self.ctx.append(cm)
        self.S[name] = h
        self.c[name] = 0
        return h

    def inc(self, instr, s, n=1):
        if isinstance(instr, _NoInc):
            if s in ('ve', 'se', 'gp') and self.S[s] is not None:
                return (s, self.c[s])
            instr.ins.then_inc(self.S[s], n)
            self.c[s] += n
            return (s, self.c[s])
        instr.then_inc(self.S[s], n)
        self.c[s] += n
        return (s, self.c[s])

    def tok(self, s):
        return (s, self.c[s])

    def wait(self, e, eng, *tokens):
        for tkn in tokens:
            if tkn is None:
                continue
            s, v = tkn
            if s == eng:
                continue
            k = (eng, s)
            if v > self.waited.get(k, 0):
                e.wait_ge(self.S[s], v)
                self.waited[k] = v

    def se_plan(self):
        return _SePlan(self)

    def on(self, eng):
        dec = getattr(self.block, ENG[eng])
        if eng in ('ve', 'se', 'gp'):
            B = self

            def wrapped(fn):
                return dec(lambda e: fn(_EngProxy(B, eng, e)))
            return wrapped
        return dec

    def bar(self, e, eng):
        v = self.c[eng]
        if v > self.waited.get((eng, eng + '#self'), 0):
            e.wait_ge(self.S[eng], v)
            self.waited[(eng, eng + '#self')] = v

    def dram(self, name, shape, dtype, kind="Internal"):
        h = self.nc.dram_tensor(name, list(shape), dtype, kind=kind)
        self.t[name] = h
        return h.ap()

    def sbuf(self, name, shape, dtype):
        cm = self.nc.sbuf_tensor(name, list(shape), dtype)
        h = cm.__enter__()
        self.ctx.append(cm)
        self.t[name] = h
        return h

    def psum(self, name, shape, dtype):
        cm = self.nc.psum_tensor(name, list(shape), dtype)
        h = cm.__enter__()
        self.ctx.append(cm)
        self.t[name] = h
        return h



def _bf(x):
    return np.ascontiguousarray(np.asarray(x).astype(nbf16))


def _f32(x):
    return np.ascontiguousarray(np.asarray(x, dtype=np.float32))


def _vec_pc(v, chunks):
    return _f32(np.asarray(v, np.float32).reshape(chunks, 128).T)


def host_prepare(w):
    d = {}
    d['l0_wqT'] = _bf(w['L0_attn_Wq'].T)
    d['l0_wkT'] = _bf(w['L0_attn_Wk'].T)
    d['l0_wvT'] = _bf(w['L0_attn_Wv'].T)
    d['l0_woT'] = _bf(w['L0_attn_Wo'].T)
    d['w1T_f32'] = _f32(np.asarray(w['L0_ffn_w1']).T)
    d['w2T_f32'] = _f32(np.asarray(w['L0_ffn_w2']).T)
    cw = np.asarray(w['L0_ffn_conv_w'], np.float32).reshape(HID0, 3)
    d['convw'] = _f32(cw.reshape(32, 128, 3).transpose(1, 0, 2).reshape(128, 96))
    d['convb'] = _vec_pc(w['L0_ffn_conv_b'], 32)
    d['ln1g'] = _vec_pc(w['L1_attn_ln1_g'], 8)
    d['ln1b'] = _vec_pc(w['L1_attn_ln1_b'], 8)
    d['ln2g'] = _vec_pc(w['L1_attn_ln2_g'], 8)
    d['ln2b'] = _vec_pc(w['L1_attn_ln2_b'], 8)
    d['wqkvT'] = _bf(w['L1_attn_W_qkv'])
    d['l1_woT'] = _bf(w['L1_attn_W_O'])
    d['bO'] = _vec_pc(w['L1_attn_b_O'], 8)
    d['wugT'] = _bf(w['L1_attn_W_ug'])
    d['bug'] = _f32(np.asarray(w['L1_attn_b_ug'], np.float32).reshape(9, 128).T)
    d['bwT'] = _bf(w['L1_attn_B_w'])
    d['Avec'] = _f32(np.asarray(w['L1_attn_A'], np.float32).reshape(128, 1))
    d['cwT'] = _bf(w['L1_attn_C_w'])
    d['mw1T'] = _bf(w['L1_attn_mlp_W1'])
    d['mb1'] = _vec_pc(w['L1_attn_mlp_b1'], 32)
    d['mw2T'] = _bf(w['L1_attn_mlp_W2'])
    d['mb2'] = _vec_pc(w['L1_attn_mlp_b2'], 8)
    d['gateT'] = _f32(np.asarray(w['L1_ffn_gate_w']).T.reshape(8, 128, EXP).transpose(1, 0, 2))
    d['wgT'] = _bf(np.transpose(np.asarray(w['L1_ffn_Wg']), (0, 2, 1)))
    d['wuT'] = _bf(np.transpose(np.asarray(w['L1_ffn_Wu']), (0, 2, 1)))
    d['wdT'] = _bf(np.transpose(np.asarray(w['L1_ffn_Wd']), (0, 2, 1)))
    d['wsgT'] = _bf(np.asarray(w['L1_ffn_Ws_g']).T)
    d['wsuT'] = _bf(np.asarray(w['L1_ffn_Ws_u']).T)
    d['wsdT'] = _bf(np.asarray(w['L1_ffn_Ws_d']).T)

    d['identb'] = _bf(np.eye(128))
    d['identf'] = _f32(np.eye(128))
    inv = 1.0 / (10000.0 ** (np.arange(0, DH, 2, dtype=np.float32) / DH))
    ang = np.outer(inv, np.arange(T, dtype=np.float32))
    rows = np.arange(128)
    fidx = (rows % 64) // 2
    sgn = np.where(rows % 2 == 0, -1.0, 1.0).astype(np.float32)
    d['ropeC'] = _bf(np.cos(ang)[fidx])
    d['ropeS'] = _bf(np.sin(ang)[fidx] * sgn[:, None])
    sw = np.zeros((128, 128), np.float32)
    sw[np.arange(128), np.arange(128) ^ 1] = 1.0
    d['swap'] = _bf(sw)
    qi = np.arange(128)[:, None]
    ki = np.arange(128)[None, :]
    mB = np.where(ki > qi, -1e9, 0.0).astype(np.float32)
    mA = np.where(ki <= qi, -1e9, 0.0).astype(np.float32)
    d['maskBT'] = _f32(mB.T.copy())
    d['maskL1'] = _f32(np.concatenate([mA, np.zeros((128, 128), np.float32), mB], 1))
    d['ones1'] = _f32(np.ones((1, 128)))
    d['ones128b'] = _bf(np.ones((128, 1)))
    d['ones128'] = _f32(np.ones((128, 1)))
    ind2 = np.zeros((2, 128), np.float32)
    ind2[0, :64] = 1.0
    ind2[1, 64:] = 1.0
    d['ind2'] = _f32(ind2)
    return d


SHARED_SPECS = {
    'l0_wqT': ([DM, DM], BF16), 'l0_wkT': ([DM, DM], BF16),
    'l0_wvT': ([DM, DM], BF16), 'l0_woT': ([DM, DM], BF16),
    'w1T_f32': ([DM, HID0], F32), 'w2T_f32': ([HID0, DM], F32),
    'convw': ([128, 96], F32), 'convb': ([128, 32], F32),
    'ln1g': ([128, 8], F32), 'ln1b': ([128, 8], F32),
    'ln2g': ([128, 8], F32), 'ln2b': ([128, 8], F32),
    'wqkvT': ([DM, 3 * DM], BF16), 'l1_woT': ([DM, DM], BF16),
    'bO': ([128, 8], F32), 'wugT': ([DM, 1152], BF16), 'bug': ([128, 9], F32),
    'bwT': ([DM, SD], BF16), 'Avec': ([128, 1], F32), 'cwT': ([SD, DM], BF16),
    'mw1T': ([DM, MLPH], BF16), 'mb1': ([128, 32], F32),
    'mw2T': ([MLPH, DM], BF16), 'mb2': ([128, 8], F32),
    'gateT': ([128, 8, EXP], F32),
    'wgT': ([EXP, DM, FSH], BF16), 'wuT': ([EXP, DM, FSH], BF16),
    'wdT': ([EXP, FSH, DM], BF16),
    'wsgT': ([DM, FSH], BF16), 'wsuT': ([DM, FSH], BF16), 'wsdT': ([FSH, DM], BF16),
    'identb': ([128, 128], BF16), 'identf': ([128, 128], F32),
    'ropeC': ([128, T], BF16), 'ropeS': ([128, T], BF16),
    'swap': ([128, 128], BF16),
    'maskBT': ([128, 128], F32), 'maskL1': ([128, 384], F32),
    'ones128b': ([128, 1], BF16),
    'ones1': ([1, 128], F32), 'ones128': ([128, 1], F32), 'ind2': ([2, 128], F32),
}

CONST_BF = ['identb', 'swap', 'ropeC', 'ropeS', 'ones128b']
CONST_F32 = ['identf', 'maskBT', 'maskL1', 'ones1', 'ones128', 'ind2', 'convw', 'convb',
             'ln1g', 'ln1b', 'ln2g', 'ln2b', 'bO', 'bug', 'Avec', 'mb1', 'mb2']



class St:
    pass


def build_program(upto=99, dbg=None):
    B = Builder()
    nc = B.nc
    st = St()
    st.B = B
    st.dbg = dbg or []

    st.din = {}
    st.din['x'] = B.dram('x', [T, DM], F32, kind="ExternalInput")
    for n, (shp, dt) in SHARED_SPECS.items():
        st.din[n] = B.dram(n, shp, dt, kind="ExternalInput")
    st.dout = B.dram('out', [T, DM], F32, kind="ExternalOutput")

    st.scr = {}
    st.scr['X'] = B.dram('scr_X', [128, 8, T], F32)
    st.scr['s0'] = B.dram('scr_s0', [128, 8, T], F32)
    st.scr['g0'] = B.dram('scr_g0', [128, 8, T], BF16)
    st.scr['xn32'] = B.dram('scr_xn32', [128, 8, T], F32)
    st.scr['xn16'] = B.dram('scr_xn16', [T, DM], BF16)
    st.scr['xn1'] = B.dram('scr_xn1', [128, 8, T], BF16)
    st.scr['w1q'] = B.dram('scr_w1q', [DM, HID0], BF16)
    st.scr['w2q'] = B.dram('scr_w2q', [HID0, DM], BF16)
    st.scr['acc'] = B.dram('scr_acc', [T, DM], F32)
    st.scr['h2'] = B.dram('scr_h2', [128, 8, T], BF16)
    st.scr['idx'] = B.dram('scr_idx', [EXP, CAP], U16)
    st.scr['val'] = B.dram('scr_val', [EXP, CAP], F32)

    st.pool = B.sbuf('pool', [128, POOL_N], F32)

    def R(off, n, c=None):
        assert off + n <= POOL_N, (off, n)
        ap = st.pool[:, off:off + n]
        if c is not None:
            ap = ap.rearrange("p (c n) -> p c n", c=c)
        return ap

    def Rb(off, nbf, c=None):
        assert off + nbf // 2 <= POOL_N, (off, nbf)
        ap = st.pool[:, off:off + nbf // 2].bitcast(BF16)
        if c is not None:
            ap = ap.rearrange("p (c n) -> p c n", c=c)
        return ap

    def Ri(off, ni):
        return st.pool[:, off:off + ni // 2].bitcast(I16)

    st.R, st.Rb, st.Ri = R, Rb, Ri

    st.cst = {}
    for n in CONST_BF:
        st.cst[n] = B.sbuf('c_' + n, SHARED_SPECS[n][0], BF16)
    for n in CONST_F32:
        st.cst[n] = B.sbuf('c_' + n, SHARED_SPECS[n][0], F32)
    st.sm = B.sbuf('small', [128, 3072], F32)

    st.psS = B.psum('psS', [128, 2048], F32)
    st.psA = B.psum('psA', [128, 512], F32)
    st.psB = B.psum('psB', [128, 512], F32)
    st.psC = B.psum('psC', [128, 512], F32)
    st.psT = B.psum('psT', [128, 1024], BF16)

    for s in ['te', 've', 'se', 'gp', 'dW', 'dG', 'dO']:
        B.sem(s)

    blk = nc.Block()
    B.block = blk.__enter__()
    B.ctx.append(blk)

    ph_consts(st)
    ph_p1_load_ln0(st)
    if upto >= 2:
        ph_p2_l0_attn(st)
    if upto >= 3:
        ph_p3_l0_ffn(st)
    if upto >= 4:
        ph_p4_combine0(st)
    if upto >= 5:
        ph_p5_ln1(st)
    if upto >= 6:
        ph_p6_moe(st)
    if upto >= 7:
        ph_p7_l1_attn(st)
    if upto >= 8:
        ph_p8_final(st)

    if st.dbg:
        ph_dbg(st)

    for c in reversed(B.ctx):
        c.__exit__(None, None, None)
    lower_extended_insts(nc)
    return B


def ph_dbg(st):
    B = st.B
    outs = []
    for name in st.dbg:
        if name.startswith('scr_'):
            ap = st.scr[name[4:]]
        else:
            ap = getattr(st, name)
            if not isinstance(ap, bass.AP):
                ap = ap[:, :]
        shp = list(ap.shape)
        d = B.dram('dbg_' + name, shp, ap.dtype, kind="ExternalOutput")
        outs.append((d, ap))

    @B.on('sy')
    def _(e):
        for s in ['te', 've', 'se', 'gp', 'dG', 'dW', 'dO']:
            if B.c[s]:
                e.wait_ge(B.S[s], B.c[s])
        n = 0
        for d, ap in outs:
            B.inc(e.dma_start(d, ap), 'dO', 16)
            n += 16
        e.wait_ge(B.S['dO'], B.c['dO'])


def ph_consts(st):
    B = st.B

    @B.on('sy')
    def _(e):
        for n, h in st.cst.items():
            B.inc(e.dma_start(h[:, :], st.din[n]), 'dW', 16)
    st.tk_consts = B.tok('dW')

    st.cv = {}
    vals = {'eps': 1e-5, 'neg8': -8.0, 'neg20': -20.0, 'none': 0.0}
    @B.on('ve')
    def _(e):
        for i, (n, v) in enumerate(vals.items()):
            slot = st.sm[:, 2960 + i:2961 + i]
            e.memset(slot, v)
            st.cv[n] = slot


def ph_p1_load_ln0(st):
    import os
    DIS = os.environ.get('P1DIS', '').split(',')
    """Load x token-major; transpose raw x -> X fm, spill to scr_X.
    LN0 in TM; quantize xq0 (TM) and transpose to xq0 fm bf16 (bC);
    xn0 fm bf16 -> bA[0:8192]. alpha0 row + sxq row in sm."""
    B = st.B
    R, Rb, Ri = st.R, st.Rb, st.Ri
    cst, sm = st.cst, st.sm
    psS, psT = st.psS, st.psT

    x_tm = R(O_B, 16384, c=16)
    st.x_tm = x_tm
    st.sm_ap = sm[:, 0:256]
    st.sm_hi = sm[:, 2944:3072]
    stage = R(O_A + 8192, 8192)
    xn16 = Rb(O_A, 16384, c=8)
    xq0 = Rb(O_C, 16384, c=8)
    st.xn16_0, st.xq0 = xn16, xq0

    flagf = sm[:, 3000:3001]
    flagi = Ri(0, 0)
    cvec = sm[:, 3002:3003]
    st.cvec = cvec
    st.cv257 = sm[:, 3003:3004]
    st.cvn256 = sm[:, 3004:3005]

    @B.on('sy')
    def _(e):
        for tt in range(16):
            B.inc(e.dma_start(x_tm[:, tt, :], st.din['x'][128 * tt:128 * (tt + 1), :]),
                  'dW', 16)
    tk_x = B.tok('dW')

    @B.on('ve')
    def _(e):
        B.inc(e.memset(flagf, 0.7), 've')
        B.bar(e, 've')
        B.inc(e.tensor_scalar(sm[:, 3001:3002].bitcast(I16)[:, 0:1], flagf, 0.0, None, Op.add), 've')
        B.bar(e, 've')
        B.inc(e.tensor_scalar(cvec, sm[:, 3001:3002].bitcast(I16)[:, 0:1], -0.5, 0.5,
                              Op.mult, Op.add), 've')
        B.bar(e, 've')
        B.inc(e.tensor_scalar(sm[:, 3003:3004], cvec, 257.0, None, Op.mult), 've')
        B.inc(e.tensor_scalar(sm[:, 3004:3005], cvec, -256.0, None, Op.mult), 've')

    @B.on('ve')
    def _(e):
        B.wait(e, 've', tk_x)
        for tt in range(16):
            ins = e.tensor_reduce(sm[:, tt:tt + 1], x_tm[:, tt, :], mybir.AxisListType.X, Op.add)
        B.inc(ins, 've')
        B.bar(e, 've')
        B.inc(e.tensor_scalar(sm[:, 16:32], sm[:, 0:16], -1.0 / DM, None, Op.mult), 've')
    tk_negmu = B.tok('ve')

    @B.on('se')
    def _(e):
        B.wait(e, 'se', tk_x)
        for tt in range(16):
            B.inc(e.activation(stage[:, 7168:8192], x_tm[:, tt, :], AF.Square,
                               accum_out=sm[:, 32 + tt:33 + tt]), 'se')
    tk_sq = B.tok('se')

    tk_sp = []
    tk_ev = []
    for c in (() if 'xspill' in DIS else range(8)):
        @B.on('te')
        def _(e, c=c):
            B.wait(e, 'te', tk_x)
            if c >= 1:
                B.wait(e, 'te', tk_ev[c - 1])
            for tt in range(16):
                ins = e.transpose(psS[:, 128 * tt:128 * (tt + 1)],
                                  x_tm[:, tt, 128 * c:128 * (c + 1)], cst['identf'][:, :])
            B.inc(ins, 'te')
        tk_t = B.tok('te')

        @B.on('ve')
        def _(e, c=c, tk_t=tk_t):
            B.wait(e, 've', tk_t)
            if c >= 2:
                B.wait(e, 've', tk_sp[c - 2])
            B.inc(e.tensor_copy(stage[:, 0:2048] if c % 2 == 0 else stage[:, 2048:4096],
                                psS[:, :]), 've')
        tk_ev.append(B.tok('ve'))

        @B.on('sy')
        def _(e, c=c):
            B.wait(e, 'sy', tk_ev[c])
            B.inc(e.dma_start(st.scr['X'][:, c, :],
                              stage[:, 0:2048] if c % 2 == 0 else stage[:, 2048:4096]),
                  'dW', 16)
        tk_sp.append(B.tok('dW'))
    st.tk_Xspill = tk_sp[-1] if tk_sp else None
    tk_xtrans = ('te', B.c['te'])

    @B.on('ve')
    def _(e):
        B.wait(e, 've', tk_sq)
        B.bar(e, 've')
        e.tensor_tensor(sm[:, 48:64], sm[:, 16:32], sm[:, 16:32], Op.mult)
        B.inc(e.engine_nop(), 've')
        B.bar(e, 've')
        e.tensor_scalar(sm[:, 112:128], sm[:, 32:48], 1.0 / DM, None, Op.mult)
        B.inc(e.tensor_tensor(sm[:, 48:64], sm[:, 112:128], sm[:, 48:64], Op.subtract), 've')
    tk_var = B.tok('ve')

    @B.on('se')
    def _(e):
        B.wait(e, 'se', tk_var)
        B.inc(e.activation(sm[:, 48:64], sm[:, 48:64], AF.Sqrt, bias=st.cv['eps'], scale=1.0), 'se')
    tk_sqv = B.tok('se')

    @B.on('ve')
    def _(e):
        B.wait(e, 've', tk_sqv, tk_xtrans)
        B.inc(e.reciprocal(sm[:, 48:64], sm[:, 48:64]), 've')
        B.bar(e, 've')
        for tt in range(16):
            B.inc(e.tensor_scalar(x_tm[:, tt, :], x_tm[:, tt, :],
                                  sm[:, 16 + tt:17 + tt], sm[:, 48 + tt:49 + tt],
                                  Op.add, Op.mult), 've')
    tk_xn = B.tok('ve')

    @B.on('ve')
    def _(e):
        for tt in range(16):
            ins = e.tensor_reduce(sm[:, 64 + tt:65 + tt], x_tm[:, tt, :], mybir.AxisListType.X,
                                  Op.max, apply_absolute_value=True)
        B.inc(ins, 've')
        B.bar(e, 've')
        B.inc(e.tensor_scalar(sm[:, 64:80], sm[:, 64:80], 1e-5, None, Op.max), 've')
        B.bar(e, 've')
        B.inc(e.reciprocal(sm[:, 80:96], sm[:, 64:80]), 've')
        B.bar(e, 've')
        e.tensor_scalar(sm[:, 80:96], sm[:, 80:96], 127.0, None, Op.mult)
        B.inc(e.tensor_scalar(sm[:, 96:112], sm[:, 64:80], 1.0 / 127.0, None, Op.mult), 've')
        B.bar(e, 've')
    tk_scales = B.tok('ve')

    u16_st = stage[:, 6144:6656].bitcast(I16)
    xqtm = stage[:, 6656:7168].bitcast(BF16)
    tk_q = []
    for tt in (() if 'quant' in DIS else range(16)):
        @B.on('ve')
        def _(e, tt=tt):
            B.wait(e, 've', tk_scales)
            B.inc(e.tensor_scalar(u16_st, x_tm[:, tt, :], sm[:, 80 + tt:81 + tt],
                                  st.cv257[:, 0:1], Op.mult, Op.add), 've')
            B.inc(e.tensor_scalar(xqtm, u16_st, st.cvn256[:, 0:1], None, Op.add), 've')
        tk_xq = B.tok('ve')

        @B.on('te')
        def _(e, tt=tt, tk_xq=tk_xq):
            B.wait(e, 'te', tk_xq)
            B.wait(e, 'te', ('ve', B.c['ve']))
            for c in range(8):
                ins = e.transpose(psT[:, 128 * (c % 8):128 * (c % 8 + 1)],
                                  xqtm.rearrange("p (c n) -> p c n", c=8)[:, c, :],
                                  cst['identb'][:, :])
            B.inc(ins, 'te')
        tk_tp = B.tok('te')

        @B.on('ve')
        def _(e, tt=tt, tk_tp=tk_tp):
            B.wait(e, 've', tk_tp)
            for c in range(8):
                ins = e.tensor_copy(xq0[:, c, 128 * tt:128 * (tt + 1)],
                                    psT[:, 128 * c:128 * (c + 1)])
            B.inc(ins, 've')
        tk_q.append(B.tok('ve'))

    for c in (() if 'xn16' in DIS else range(8)):
        @B.on('te')
        def _(e, c=c):
            B.wait(e, 'te', tk_xn)
            B.wait(e, 'te', ('ve', B.c['ve']))
            for tt in range(16):
                ins = e.transpose(psS[:, 128 * tt:128 * (tt + 1)],
                                  x_tm[:, tt, 128 * c:128 * (c + 1)], cst['identf'][:, :])
            B.inc(ins, 'te')
        tk_t = B.tok('te')

        @B.on('ve')
        def _(e, c=c, tk_t=tk_t):
            B.wait(e, 've', tk_t)
            B.inc(e.tensor_copy(xn16[:, c, :], psS[:, :]), 've')
    st.tk_p1_done = B.tok('ve')
    st.tk_xq0 = tk_q[-1] if tk_q else None



def _wq_slices(st):
    B = st.B
    R, Rb, Ri = st.R, st.Rb, st.Ri
    sm = st.sm
    cst = st.cst
    slab = [R(O_A + 12288, 2048), R(O_A + 14336, 2048)]
    sgn_w = R(O_A + 10240, 1024)
    u_w = R(O_A + 11264, 1024)

    units = []
    for i in range(16):
        units.append(('w1', bass.AP(st.din['w1T_f32'].tensor,
                                    (128 * (i // 2)) * HID0 + (i % 2) * 2048,
                                    [[HID0, 128], [1, 2048]])))
    for i in range(16):
        units.append(('w2', bass.AP(st.din['w2T_f32'].tensor, (i * 256) * DM,
                                    [[DM, 128], [DM * 128, 2], [1, 1024]])))

    def wq_dst(i, h):
        kind = units[i][0]
        if kind == 'w1':
            off = (128 * (i // 2)) * HID0 + (i % 2) * 2048 + 1024 * h
            return bass.AP(st.scr['w1q'].tensor, off, [[HID0, 128], [1, 1024]])
        j = i - 16
        off = (256 * j + 128 * h) * DM
        return bass.AP(st.scr['w2q'].tensor, off, [[DM, 128], [1, 1024]])

    mtok = {'dma': [None] * 32, 've': [None] * 32}
    qtok = {'dma': [None] * 32, 've': [None] * 32, 'out': [None] * 32}

    def mk_mean(lo, hi):
        def emit():
            for i in range(lo, hi):
                @B.on('gp')
                def _(e, i=i):
                    B.wait(e, 'gp', st.tk_p1_done, st.tk_p1_se)
                    if i >= 2:
                        B.wait(e, 'gp', mtok['ve'][i - 2])
                    mtok['dma'][i] = B.inc(e.dma_start(slab[i % 2], units[i][1]), 'dG', 16)

                @B.on('ve')
                def _(e, i=i):
                    B.wait(e, 've', mtok['dma'][i])
                    j = 0 if units[i][0] == 'w1' else 1
                    e.tensor_reduce(sm[:, 162:163], slab[i % 2], mybir.AxisListType.X,
                                    Op.add, apply_absolute_value=True)
                    if i % 16 == 0:
                        mtok['ve'][i] = B.inc(e.tensor_copy(sm[:, 160 + j:161 + j],
                                                            sm[:, 162:163]), 've')
                    else:
                        mtok['ve'][i] = B.inc(e.tensor_tensor(sm[:, 160 + j:161 + j],
                                                              sm[:, 160 + j:161 + j],
                                                              sm[:, 162:163], Op.add), 've')
        return emit

    def mk_final():
        def emit():
            tkv = ('ve', B.c['ve'])

            @B.on('te')
            def _(e):
                B.wait(e, 'te', tkv)
                B.inc(e.matmul(st.psA[0:1, 0:2], cst['ones128'][:, :], sm[:, 160:162],
                               start=True, stop=True), 'te')
            tkt = B.tok('te')

            @B.on('ve')
            def _(e):
                B.wait(e, 've', tkt)
                B.inc(e.tensor_scalar(sm[0:1, 163:165], st.psA[0:1, 0:2],
                                      1.0 / (4096.0 * 1024.0), 1e-5, Op.mult, Op.max), 've')
            tkv2 = ('ve', B.c['ve'])

            @B.on('te')
            def _(e):
                B.wait(e, 'te', tkv2)
                B.inc(e.matmul(st.psB[:, 0:2], cst['ones1'][0:1, :], sm[0:1, 163:165],
                               start=True, stop=True), 'te')
            tkt2 = B.tok('te')

            @B.on('ve')
            def _(e):
                B.wait(e, 've', tkt2)
                e.tensor_copy(sm[:, 176:178], st.psB[:, 0:2])
                e.reciprocal(sm[:, 178:180], sm[:, 176:178])
            st._wqfin = ('ve', B.c['ve'])
        return emit

    def mk_quant(lo, hi):
        def emit():
            for i in range(lo, hi):
                @B.on('gp')
                def _(e, i=i):
                    if i >= 2:
                        B.wait(e, 'gp', qtok['ve'][i - 2], qtok['out'][i - 2])
                    else:
                        B.wait(e, 'gp', st._wqfin)
                    qtok['dma'][i] = B.inc(e.dma_start(slab[i % 2], units[i][1]), 'dG', 16)
                for h in range(2):
                    src = slab[i % 2][:, 1024 * h:1024 * (h + 1)]
                    u16 = st.pool[:, O_A + 10240:O_A + 10752].bitcast(I16)
                    u_wn = st.pool[:, O_A + 10752:O_A + 11776]
                    wqv = st.pool[:, O_A + 11776 + 512 * h:O_A + 12288 + 512 * h - 512 + 512]
                    wqv = st.pool[:, O_A + 11776 + 256 * h:O_A + 11776 + 256 * h + 512][:, 0:512].bitcast(BF16) if False else st.pool[:, O_A + 11776 + 512 * h:O_A + 11776 + 512 * h + 512].bitcast(BF16)

                    @B.on('ve')
                    def _(e, i=i, h=h, src=src, u16=u16, u_wn=u_wn, wqv=wqv):
                        j = 0 if units[i][0] == 'w1' else 1
                        B.wait(e, 've', qtok['dma'][i])
                        if i >= 1 and qtok['out'][i - 1] is not None:
                            B.wait(e, 've', qtok['out'][i - 1])
                        e.tensor_scalar(u16, src, sm[:, 178 + j:179 + j],
                                        st.cv257[:, 0:1], Op.mult, Op.add)
                        e.tensor_scalar(u_wn, u16, st.cvn256[:, 0:1], 1.0, Op.add, Op.min)
                        tk = B.inc(e.tensor_scalar(wqv, u_wn, -1.0, None, Op.max), 've')
                        if h == 1:
                            qtok['ve'][i] = tk

                    @B.on('gp')
                    def _(e, i=i, h=h, wqv=wqv):
                        B.wait(e, 'gp', ('ve', B.c['ve']))
                        qtok['out'][i] = B.inc(e.dma_start(wq_dst(i, h), wqv), 'dG', 16)
        return emit

    return ([mk_mean(i, i + 1) for i in range(32)] + [mk_final()],
            [mk_quant(i, i + 1) for i in range(32)])


def ph_p2_l0_attn(st):
    B = st.B
    R, Rb, Ri = st.R, st.Rb, st.Ri
    cst, sm = st.cst, st.sm
    psS, psA, psB, psC, psT = st.psS, st.psA, st.psB, st.psC, st.psT
    xn16 = st.xn16_0

    st.tk_p1_se = ('se', B.c['se'])
    attn_full = Rb(O_B, 16384, c=8)
    qraw = Rb(O_B + 8192, 2048)
    kraw = Rb(O_B + 9216, 2048)
    qrot = Rb(O_B + 10240, 2048)
    krot = Rb(O_B + 11264, 2048)
    v2 = Rb(O_B + 12288, 2048)
    vtm = Rb(O_B + 13312, 2048, c=16)
    t1b = Rb(O_B + 14336, 2048)
    t2b = Rb(O_B + 15360, 2048)
    def at4(kt):
        return (Rb(O_B + 8192 + 256 * kt, 512) if kt < 8
                else Rb(O_A + 8192 + 256 * (kt - 8), 512))
    s0st = R(O_A + 10240, 2048)
    st.attn_full0 = attn_full

    wr = [Rb(O_D + 512 * i, 1024, c=8) for i in range(6)]
    mean_sl, quant_sl = _wq_slices(st)

    tk_wv = [None] * 8
    tk_dma = [None] * 8

    def load_pair(p):
        @B.on('sy')
        def _(e):
            if p >= 2 and tk_wv[p - 2] is not None:
                B.wait(e, 'sy', tk_wv[p - 2])
            base = (p % 2) * 3
            for wi, wn in enumerate(['l0_wqT', 'l0_wkT', 'l0_wvT']):
                src = bass.AP(st.din[wn].tensor, 128 * p, [[DM, 128], [DM * 128, 8], [1, 128]])
                B.inc(e.dma_start(wr[base + wi][:, :, :], src), 'dW', 16)
        tk_dma[p] = B.tok('dW')

    load_pair(0)
    load_pair(1)

    for p in range(8):
        base = (p % 2) * 3
        for wi, dst in enumerate([qraw, kraw, v2]):
            @B.on('te')
            def _(e, p=p, wi=wi):
                B.wait(e, 'te', tk_dma[p], st.tk_p1_done)
                B.wait(e, 'te', ('ve', B.c['ve']))
                for f in range(8):
                    for n in range(4):
                        ins = e.matmul(psS[:, 512 * n:512 * (n + 1)],
                                       wr[base + wi][:, f, :],
                                       xn16[:, f, 512 * n:512 * (n + 1)],
                                       start=(f == 0), stop=(f == 7))
                B.inc(ins, 'te')
            tk_mm = B.tok('te')

            @B.on('ve')
            def _(e, dst=dst, tk_mm=tk_mm):
                B.wait(e, 've', tk_mm)
                e.tensor_copy(dst, psS[:, :])
        tk_wv[p] = B.tok('ve')
        tk_wv[p] = ('te', B.c['te'])
        if p + 2 < 8:
            load_pair(p + 2)

        for srcb, rotb in [(qraw, qrot), (kraw, krot)]:
            @B.on('te')
            def _(e, srcb=srcb):
                B.wait(e, 'te', ('ve', B.c['ve']))
                for n in range(4):
                    ins = e.matmul(psS[:, 512 * n:512 * (n + 1)], cst['swap'][:, :],
                                   srcb[:, 512 * n:512 * (n + 1)], start=True, stop=True)
                B.inc(ins, 'te')
            tk_sw = B.tok('te')

            @B.on('ve')
            def _(e, srcb=srcb, rotb=rotb, tk_sw=tk_sw):
                B.wait(e, 've', tk_sw)
                e.tensor_tensor(t1b, srcb, cst['ropeC'][:, :], Op.mult)
                e.tensor_tensor(t2b, psS[:, :], cst['ropeS'][:, :], Op.mult)
                e.tensor_tensor(rotb, t1b, t2b, Op.add)

        for g in range(2):
            @B.on('te')
            def _(e, g=g):
                B.wait(e, 'te', ('ve', B.c['ve']))
                for kt in range(8):
                    ins = e.transpose(psT[:, 128 * kt:128 * (kt + 1)],
                                      v2[:, 128 * (8 * g + kt):128 * (8 * g + kt + 1)],
                                      cst['identb'][:, :])
                B.inc(ins, 'te')
            tk = B.tok('te')

            @B.on('ve')
            def _(e2, g=g, tk=tk):
                B.wait(e2, 've', tk)
                for kt in range(8):
                    e2.tensor_copy(vtm[:, 8 * g + kt, :], psT[:, 128 * kt:128 * (kt + 1)])
        tk_vtm = ('ve', B.c['ve'])

        if not hasattr(st, '_wq_q'):
            st._wq_q = mean_sl + quant_sl
        if p == 4 and st._wq_q:
            st._wq_q.pop(0)()

        tk_rope = ('ve', B.c['ve'])
        if not hasattr(st, '_p2scnt'):
            st._p2scnt = 0
            st._p2exp = [None, None, None]
            st._p2avfree = [None, None]
            st._p2dnfree = [None, None]
            st._p2bcfree = None
        psTf = psT[:, :].bitcast(F32)
        plan = B.se_plan()
        for hl in range(2):
            hb = 64 * hl
            for g in range(4):
                K = 4 * g + 4
                expg = []
                @B.on('ve')
                def _(e, g=g):
                    B.wait(e, 've', tk_rope)
                    for j in range(1, 4):
                        e.memset(at4(4 * g + j)[:, 0:128 * j], 0.0)
                tk_ms = B.tok('ve')

                for kt in range(K):
                    slot = st._p2scnt % 3
                    st._p2scnt += 1
                    c_lo = 128 * (kt - 4 * g) if kt >= 4 * g else 0
                    @B.on('te')
                    def _(e, hb=hb, g=g, kt=kt, slot=slot, c_lo=c_lo):
                        if st._p2exp[slot] is not None:
                            B.wait(e, 'te', st._p2exp[slot])
                        B.wait(e, 'te', tk_rope)
                        B.inc(e.matmul(psS[:, 512 * slot + c_lo:512 * slot + 512],
                                       krot[hb:hb + 64, 128 * kt:128 * (kt + 1)],
                                       qrot[hb:hb + 64, 512 * g + c_lo:512 * (g + 1)],
                                       start=True, stop=True), 'te')
                    tk_sc = B.tok('te')

                    if kt >= 4 * g:
                        @B.on('ve')
                        def _(e, slot=slot, c_lo=c_lo, tk_sc=tk_sc):
                            B.wait(e, 've', tk_sc)
                            B.inc(e.tensor_tensor(
                                psS[:, 512 * slot + c_lo:512 * slot + c_lo + 128],
                                psS[:, 512 * slot + c_lo:512 * slot + c_lo + 128],
                                cst['maskBT'][:, :], Op.add), 've')
                        tk_mk = B.tok('ve')
                    else:
                        tk_mk = tk_sc

                    st._p2exp[slot] = plan.add(
                        (lambda kt, slot, c_lo: lambda e: e.activation(
                            at4(kt)[:, c_lo:512], psS[:, 512 * slot + c_lo:512 * slot + 512],
                            AF.Exp, bias=st.cv['neg8'], scale=0.125))(kt, slot, c_lo),
                        tk_mk, tk_sc)
                    expg.append(st._p2exp[slot])

                av = [psA, psB][g % 2]
                dn = [psC, psTf][g % 2]
                @B.on('te')
                def _(e, hb=hb, g=g, K=K, av=av, dn=dn, tk_ms=tk_ms, expg=expg):
                    if st._p2avfree[g % 2] is not None:
                        B.wait(e, 'te', st._p2avfree[g % 2])
                    if st._p2dnfree[g % 2] is not None:
                        B.wait(e, 'te', st._p2dnfree[g % 2])
                    B.wait(e, 'te', tk_ms)
                    for kt in range(K):
                        B.wait(e, 'te', expg[kt])
                        ins = e.matmul(av[hb:hb + 64, 0:512], vtm[:, kt, hb:hb + 64],
                                       at4(kt)[:, 0:512], start=(kt == 0), stop=(kt == K - 1),
                                       tile_position=(0, hb))
                        ins = e.matmul(dn[0:1, 0:512], cst['ones128b'][:, 0:1],
                                       at4(kt)[:, 0:512], start=(kt == 0), stop=(kt == K - 1))
                    B.inc(ins, 'te')
                tk_avd = B.tok('te')

                @B.on('ve')
                def _(e, g=g, dn=dn, tk_avd=tk_avd):
                    B.wait(e, 've', tk_avd)
                    B.inc(e.reciprocal(sm[0:1, 512 + 512 * (g % 2):1024 + 512 * (g % 2)],
                                       dn[0:1, 0:512]), 've')
                st._p2dnfree[g % 2] = B.tok('ve')
                tk_rc = B.tok('ve')

                @B.on('te')
                def _(e, g=g, tk_rc=tk_rc):
                    B.wait(e, 'te', tk_rc)
                    if st._p2bcfree is not None:
                        B.wait(e, 'te', st._p2bcfree)
                    B.inc(e.matmul(psS[:, 1536:2048], cst['ones1'][0:1, :],
                                   sm[0:1, 512 + 512 * (g % 2):1024 + 512 * (g % 2)],
                                   start=True, stop=True), 'te')
                tk_bc = B.tok('te')

                @B.on('ve')
                def _(e, p=p, hl=hl, hb=hb, g=g, av=av, tk_bc=tk_bc):
                    B.wait(e, 've', tk_bc)
                    e.tensor_copy(sm[hb:hb + 64, 1536:2048], psS[hb:hb + 64, 1536:2048])
                    B.inc(e.tensor_tensor(
                        attn_full[hb:hb + 64, p, 512 * g:512 * (g + 1)],
                        av[hb:hb + 64, 0:512], sm[hb:hb + 64, 1536:2048], Op.mult), 've')
                st._p2avfree[g % 2] = B.tok('ve')
                st._p2bcfree = st._p2avfree[g % 2]

                if st._wq_q:
                    st._wq_q.pop(0)()
        plan.flush()

    for f in st._wq_q:
        f()
    st._wq_q = []
    st.tk_wq_done = B.tok('dG')

    tk_wo = [None, None]
    for d in range(8):
        @B.on('sy')
        def _(e, d=d):
            B.wait(e, 'sy', tk_wv[6])
            if d >= 2:
                B.wait(e, 'sy', tk_wo[d % 2])
            src = bass.AP(st.din['l0_woT'].tensor, 128 * d, [[DM, 128], [DM * 128, 8], [1, 128]])
            B.inc(e.dma_start(wr[d % 2][:, :, :], src), 'dW', 16)
        tk_d = B.tok('dW')

        @B.on('te')
        def _(e, d=d, tk_d=tk_d):
            B.wait(e, 'te', tk_d)
            B.wait(e, 'te', ('ve', B.c['ve']))
            for f in range(8):
                for n in range(4):
                    ins = e.matmul(psS[:, 512 * n:512 * (n + 1)], wr[d % 2][:, f, :],
                                   attn_full[:, f, 512 * n:512 * (n + 1)],
                                   start=(f == 0), stop=(f == 7))
            B.inc(ins, 'te')
        tk_wo[d % 2] = B.tok('te')
        tk_mm = B.tok('te')

        @B.on('ve')
        def _(e, d=d, tk_mm=tk_mm):
            B.wait(e, 've', tk_mm)
            B.wait(e, 've', ('dG', B.c['dG']))
            if d >= 2:
                B.wait(e, 've', st.tk_s0sp[d - 2])
            B.inc(e.tensor_copy(s0st if d % 2 == 0 else st.R(O_A + 12288, 2048), psS[:, :]), 've')
        tk_ev = B.tok('ve')

        @B.on('sy')
        def _(e, d=d, tk_ev=tk_ev):
            B.wait(e, 'sy', tk_ev)
            B.inc(e.dma_start(st.scr['s0'][:, d, :],
                              s0st if d % 2 == 0 else st.R(O_A + 12288, 2048)), 'dW', 16)
        if not hasattr(st, 'tk_s0sp'):
            st.tk_s0sp = []
        st.tk_s0sp.append(B.tok('dW'))
    st.tk_p2_done = ('ve', B.c['ve'])


def ph_p3_l0_ffn(st):
    B = st.B
    R, Rb, Ri = st.R, st.Rb, st.Ri
    cst, sm = st.cst, st.sm
    psS, psA, psB, psC, psT = st.psS, st.psA, st.psB, st.psC, st.psT
    psTf = psT[:, :].bitcast(F32)
    xq0 = st.xq0
    pool = st.pool

    h1c = R(O_B, 16640).rearrange("p (c n) -> p c n", c=32)
    xqh = Rb(O_A, 16384, c=32)
    ring = [Rb(O_A + 8192 + 512 * i, 1024, c=8) for i in range(4)]
    prowb = R(O_A + 10240, 514)
    o1st = R(O_A + 10754, 512)
    o3st = R(O_A + 11266, 512)
    sgn3 = o1st
    u3 = o3st
    u16_3 = Ri(O_A + 11778, 512)
    shqb = R(O_A + 12290, 512)
    prow2b = R(O_A + 12802, 512)
    evstg = R(O_A + 13314, 512)
    g0st = Rb(O_A + 13826, 1024)
    arow = pool[0:1, O_D:O_D + 2050]
    hrow = pool[0:1, O_D + 2056:O_D + 2568]
    hrow2 = pool[0:1, O_D + 2568:O_D + 3080]
    arow_t = pool[0:16, O_D + 3084:O_D + 3212]
    rmax_t = pool[0:4, O_D + 3216:O_D + 3344]

    tk_p2te = ('te', B.c['te'])
    tk_p2dw = ('dW', B.c['dW'])

    @B.on('te')
    def _(e):
        B.wait(e, 'te', tk_p2dw)
        B.wait(e, 'te', ('ve', B.c['ve']))
        B.inc(e.transpose(psA[0:16, 0:128], sm[:, 96:112], cst['identf'][:, :]), 'te')
    tk1 = B.tok('te')

    @B.on('ve')
    def _(e):
        B.wait(e, 've', tk1)
        e.memset(arow, 0.0)
        B.inc(e.tensor_copy(arow_t, psA[0:16, 0:128]), 've')
    tk2 = B.tok('ve')

    @B.on('gp')
    def _(e):
        B.wait(e, 'gp', tk2)
        src = bass.AP(B.t['pool'], O_D + 3084, [[POOL_N, 16], [1, 128]])
        dst = bass.AP(B.t['pool'], O_D + 1, [[POOL_N, 1], [1, 2048]])
        B.inc(e.dma_start(dst, src), 'dG', 16)
    tk3 = B.tok('dG')

    @B.on('ve')
    def _(e):
        B.wait(e, 've', tk3)
        B.inc(e.tensor_scalar(arow[0:1, 1:2049], arow[0:1, 1:2049],
                              sm[0:1, 176:177], None, Op.mult), 've')
        B.inc(e.tensor_scalar(sm[0:1, 180:181], sm[0:1, 177:178], 1.0 / 127.0,
                              None, Op.mult), 've')
    tk_arow = B.tok('ve')

    tk_ld = [None] * 32
    tk_use = [None] * 32

    for tc in range(4):
        t0 = 512 * tc
        P0 = 1 if tc == 0 else 0
        P1_ = 513 if tc == 3 else 514

        @B.on('ve')
        def _(e, tc=tc):
            if tc == 0:
                B.wait(e, 've', tk_arow)
                e.memset(bass.AP(B.t['pool'], O_B + 0, [[POOL_N, 128], [520, 32], [1, 1]]), 0.0)
            if tc == 3:
                e.memset(bass.AP(B.t['pool'], O_B + 513, [[POOL_N, 128], [520, 32], [1, 1]]), 0.0)
        tkz = ('ve', B.c['ve'])

        @B.on('te')
        def _(e, tc=tc, t0=t0):
            B.wait(e, 'te', tk_arow)
            B.wait(e, 'te', ('ve', B.c['ve']))
            e.matmul(psC[:, 0:512], cst['ones1'][0:1, :], arow[0:1, t0:t0 + 512],
                     start=True, stop=True)
            B.inc(e.matmul(psA[:, 0:2], cst['ones1'][0:1, :], arow[0:1, t0 + 512:t0 + 514],
                           start=True, stop=True), 'te')
        tkp = B.tok('te')

        @B.on('ve')
        def _(e, tkp=tkp):
            B.wait(e, 've', tkp)
            e.tensor_copy(prowb[:, 0:512], psC[:, 0:512])
            B.inc(e.tensor_copy(prowb[:, 512:514], psA[:, 0:2]), 've')
        tk_prow = B.tok('ve')

        tk_evac = [None] * 32
        plan_cv = B.se_plan()
        for hc in range(33):
            if hc < 32:
                @B.on('sy')
                def _(e, hc=hc, tc=tc):
                    i = hc
                    if tc == 0 and hc < 4:
                        B.wait(e, 'sy', st.tk_wq_done, tk_p2te)
                        B.wait(e, 'sy', ('ve', B.c['ve']))
                        B.wait(e, 'sy', ('se', B.c['se']))
                    if tk_use[i % 4] is not None:
                        B.wait(e, 'sy', tk_use[i % 4])
                    src = bass.AP(st.scr['w1q'].tensor, 128 * hc,
                                  [[HID0, 128], [HID0 * 128, 8], [1, 128]])
                    tk_ld[i % 4] = B.inc(e.dma_start(ring[i % 4][:, :, :], src), 'dW', 16)

                @B.on('te')
                def _(e, hc=hc, tc=tc, t0=t0, P0=P0, P1_=P1_):
                    B.wait(e, 'te', tk_ld[hc % 4], st.tk_xq0, tk_prow)
                    if hc >= 2:
                        B.wait(e, 'te', tk_evac[hc - 2])
                    half = 1024 * (hc % 2)
                    n0 = min(512, P1_) - P0
                    ins = None
                    for f in range(8):
                        ins = e.matmul(psS[:, half + P0:half + P0 + n0],
                                       ring[hc % 4][:, f, :],
                                       xq0[:, f, t0 - 1 + P0:t0 - 1 + P0 + n0],
                                       start=(f == 0), stop=(f == 7))
                    if P1_ > 512:
                        n1 = P1_ - 512
                        for f in range(8):
                            ins = e.matmul(psS[:, half + 512:half + 512 + n1],
                                           ring[hc % 4][:, f, :],
                                           xq0[:, f, t0 + 511:t0 + 511 + n1],
                                           start=(f == 0), stop=(f == 7))
                    B.inc(ins, 'te')
                tk_use[hc % 4] = B.tok('te')
                tk_mm = B.tok('te')

                @B.on('ve')
                def _(e, hc=hc, P0=P0, P1_=P1_, tk_mm=tk_mm):
                    B.wait(e, 've', tk_mm)
                    half = 1024 * (hc % 2)
                    tk_evac[hc] = B.inc(
                        e.tensor_tensor(h1c[:, hc, P0:P1_], psS[:, half + P0:half + P1_],
                                        prowb[:, P0:P1_], Op.mult), 've')
            if hc >= 1:
                c = hc - 1
                tko1 = plan_cv.add(
                    (lambda c, tkv: lambda e: (
                        B.wait(e, 'se', tkv),
                        e.activation(o1st, h1c[:, c, 1:513], AF.Identity,
                                     bias=cst['convb'][:, c:c + 1],
                                     scale=cst['convw'][:, 3 * c + 1:3 * c + 2]))[-1]
                     )(c, tk_evac[c]))

                @B.on('ve')
                def _(e, c=c, tko1=tko1):
                    B.wait(e, 've', tko1)
                    e.scalar_tensor_tensor(o3st, h1c[:, c, 0:512],
                                           cst['convw'][:, 3 * c:3 * c + 1], o1st,
                                           Op.mult, Op.add)
                    B.inc(e.scalar_tensor_tensor(o1st, h1c[:, c, 2:514],
                                                 cst['convw'][:, 3 * c + 2:3 * c + 3],
                                                 o3st, Op.mult, Op.add), 've')
                tko3 = B.tok('ve')

                plan_cv.add(
                    (lambda c, tko3: lambda e: (
                        B.wait(e, 'se', tko3),
                        e.activation(h1c[:, c, 0:512], o1st, AF.Silu))[-1]
                     )(c, tko3))
        plan_cv.flush()
        tk_hs = B.tok('se')

        for g in range(4):
            for rr in range(2):
                @B.on('te')
                def _(e, g=g, rr=rr):
                    B.wait(e, 'te', tk_hs)
                    B.wait(e, 'te', ('ve', B.c['ve']))
                    for k in range(16):
                        hc = 16 * rr + k
                        ins = e.transpose(psS[:, 128 * k:128 * (k + 1)],
                                          h1c[:, hc, 128 * g:128 * (g + 1)],
                                          cst['identf'][:, :])
                    B.inc(ins, 'te')
                tkt = B.tok('te')

                @B.on('ve')
                def _(e, g=g, rr=rr, tkt=tkt):
                    B.wait(e, 've', tkt)
                    e.tensor_reduce(sm[:, 196:197], psS[:, :], mybir.AxisListType.X,
                                    Op.max, apply_absolute_value=True)
                    if rr == 0:
                        B.inc(e.tensor_copy(sm[:, 192 + g:193 + g], sm[:, 196:197]), 've')
                    else:
                        B.inc(e.tensor_tensor(sm[:, 192 + g:193 + g],
                                              sm[:, 192 + g:193 + g],
                                              sm[:, 196:197], Op.max), 've')
        tk_rm = B.tok('ve')

        @B.on('te')
        def _(e):
            B.wait(e, 'te', tk_rm)
            B.inc(e.transpose(psA[0:4, 0:128], sm[:, 192:196], cst['identf'][:, :]), 'te')
        tkt2 = B.tok('te')

        @B.on('ve')
        def _(e, tkt2=tkt2):
            B.wait(e, 've', tkt2)
            B.inc(e.tensor_copy(rmax_t, psA[0:4, 0:128]), 've')
        tkrt = B.tok('ve')

        @B.on('gp')
        def _(e, tkrt=tkrt):
            B.wait(e, 'gp', tkrt)
            src = bass.AP(B.t['pool'], O_D + 3216, [[POOL_N, 4], [1, 128]])
            dst = bass.AP(B.t['pool'], O_D + 2056, [[POOL_N, 1], [1, 512]])
            B.inc(e.dma_start(dst, src), 'dG', 16)
        tkhr = B.tok('dG')

        @B.on('ve')
        def _(e, tkhr=tkhr):
            B.wait(e, 've', tkhr)
            e.tensor_scalar(hrow, hrow, 1e-5, None, Op.max)
            e.reciprocal(hrow2, hrow)
            e.tensor_scalar(hrow2, hrow2, 127.0, None, Op.mult)
            B.inc(e.tensor_scalar(hrow, hrow, sm[0:1, 180:181], None, Op.mult), 've')
        tk_rows = B.tok('ve')

        @B.on('te')
        def _(e, tk_rows=tk_rows):
            B.wait(e, 'te', tk_rows)
            B.wait(e, 'te', ('ve', B.c['ve']))
            e.matmul(psC[:, 0:512], cst['ones1'][0:1, :], hrow2, start=True, stop=True)
            B.inc(e.matmul(psB[:, 0:512], cst['ones1'][0:1, :], hrow, start=True, stop=True), 'te')
        tkb = B.tok('te')

        @B.on('ve')
        def _(e, tkb=tkb):
            B.wait(e, 've', tkb)
            e.tensor_copy(shqb, psC[:, 0:512])
            B.inc(e.tensor_copy(prow2b, psB[:, 0:512]), 've')
        tk_bc = B.tok('ve')

        bank = [psS[:, 0:512], psS[:, 512:1024], psS[:, 1024:1536], psS[:, 1536:2048],
                psA[:, :], psB[:, :], psC[:, :], psTf]
        tk_q = [None] * 32
        for hc in range(32):
            @B.on('ve')
            def _(e, hc=hc):
                B.wait(e, 've', tk_bc)
                e.scalar_tensor_tensor(u16_3, h1c[:, hc, 0:512], st.cv257[:, 0:1],
                                       shqb, Op.add, Op.mult) if False else None
                e.tensor_tensor(u3, h1c[:, hc, 0:512], shqb, Op.mult)
                e.tensor_scalar(u16_3, u3, st.cv257[:, 0:1], None, Op.add)
                tk_q[hc] = B.inc(e.tensor_scalar(xqh[:, hc, :], u16_3,
                                                 st.cvn256[:, 0:1], None, Op.add), 've')

            @B.on('sy')
            def _(e, hc=hc, tc=tc):
                B.wait(e, 'sy', tk_use[hc % 4])
                src = bass.AP(st.scr['w2q'].tensor, (128 * hc) * DM,
                              [[DM, 128], [128, 8], [1, 128]])
                tk_ld[hc % 4] = B.inc(e.dma_start(ring[hc % 4][:, :, :], src), 'dW', 16)

            @B.on('te')
            def _(e, hc=hc):
                B.wait(e, 'te', tk_ld[hc % 4], tk_q[hc])
                if hc == 0:
                    B.wait(e, 'te', ('ve', B.c['ve']))
                ins = None
                for d in range(8):
                    ins = e.matmul(bank[d], ring[hc % 4][:, d, :], xqh[:, hc, :],
                                   start=(hc == 0), stop=(hc == 31))
                B.inc(ins, 'te')
            tk_use[hc % 4] = B.tok('te')
        tk_bl2 = B.tok('te')

        plan_ev = B.se_plan()
        tksl_prev = None
        for d in range(8):
            @B.on('ve')
            def _(e, d=d, tksl_prev=tksl_prev):
                B.wait(e, 've', tk_bl2)
                if tksl_prev is not None:
                    B.wait(e, 've', tksl_prev)
                B.inc(e.tensor_tensor(evstg, bank[d], prow2b, Op.mult), 've')
            tkev = B.tok('ve')

            if not hasattr(st, 'tk_g0sp'):
                st.tk_g0sp = []

            tksl = plan_ev.add(
                (lambda d, tkev, tkg: lambda e: (
                    B.wait(e, 'se', tkev),
                    B.wait(e, 'se', tkg) if tkg is not None else None,
                    e.activation(g0st[:, 0:512] if d % 2 == 0 else g0st[:, 512:1024],
                                 evstg, AF.Silu))[-1]
                 )(d, tkev, st.tk_g0sp[-2] if len(st.tk_g0sp) >= 2 else None))
            tksl_prev = tksl

            @B.on('sy')
            def _(e, d=d, tc=tc, tksl=tksl):
                B.wait(e, 'sy', tksl)
                B.inc(e.dma_start(st.scr['g0'][:, d, t0:t0 + 512],
                                  g0st[:, 0:512] if d % 2 == 0 else g0st[:, 512:1024]),
                      'dW', 16)
            st.tk_g0sp.append(B.tok('dW'))
        plan_ev.flush()
    st.tk_p3_done = ('dW', B.c['dW'])


def ph_p4_combine0(st):
    B = st.B
    R, Rb = st.R, st.Rb
    x1 = R(O_A, N_A, c=8)
    st.X1 = x1
    tk_in = [None] * 8
    for d in range(8):
        o = (d % 2) * 7168
        stX = R(O_B + o, 2048)
        stS = R(O_B + o + 2048, 2048)
        stG = Rb(O_B + o + 4096, 2048)
        tmp = R(O_B + o + 5120, 2048)

        @B.on('sy')
        def _(e, d=d):
            if d == 0:
                B.wait(e, 'sy', st.tk_p3_done)
                B.wait(e, 'sy', ('te', B.c['te']))
                B.wait(e, 'sy', ('ve', B.c['ve']))
                B.wait(e, 'sy', ('se', B.c['se']))
            if d >= 2:
                B.wait(e, 'sy', tk_in[d - 2])
            B.inc(e.dma_start(stX if d % 2 == 0 else R(O_B + 7168, 2048),
                              st.scr['X'][:, d, :]), 'dW', 16) if False else None
            B.inc(e.dma_start(stX, st.scr['X'][:, d, :]), 'dW', 16)
            B.inc(e.dma_start(stS, st.scr['s0'][:, d, :]), 'dW', 16)
            B.inc(e.dma_start(stG, st.scr['g0'][:, d, :]), 'dW', 16)
        tkd = B.tok('dW')

        @B.on('ve')
        def _(e, d=d, stX=stX, stS=stS, stG=stG, tmp=tmp, tkd=tkd):
            B.wait(e, 've', tkd)
            e.tensor_tensor(tmp, stS, stG, Op.mult)
            tk_in[d] = B.inc(e.tensor_tensor(x1[:, d, :], tmp, stX, Op.add), 've')
    st.tk_p4_done = B.tok('ve')


def ph_p5_ln1(st):
    B = st.B
    R, Rb = st.R, st.Rb
    cst, sm = st.cst, st.sm
    psS, psA, psB, psC = st.psS, st.psA, st.psB, st.psC
    x1 = st.X1
    pool = st.pool

    x_tm = R(O_B, 16384, c=16)
    xn16 = Rb(O_C, 16384, c=8)
    st.xn16_1 = xn16
    gatesb = R(O_D + 3344, 128, None).rearrange("p (c n) -> p c n", c=8)
    lgbuf = R(O_D + 3472, 256)
    st.lg = lgbuf

    @B.on('sy')
    def _(e):
        B.wait(e, 'sy', st.tk_p4_done)
        B.inc(e.dma_start(gatesb[:, :, :], st.din['gateT']), 'dW', 16)
        for d in range(8):
            B.inc(e.dma_start(st.scr['X'][:, d, :], x1[:, d, :]), 'dW', 16)
    tk_g = B.tok('dW')

    tk_ev = [None] * 8
    for c in range(8):
        @B.on('te')
        def _(e, c=c):
            B.wait(e, 'te', st.tk_p4_done)
            B.wait(e, 'te', ('ve', B.c['ve']))
            for tt in range(16):
                ins = e.transpose(psS[:, 128 * tt:128 * (tt + 1)],
                                  x1[:, c, 128 * tt:128 * (tt + 1)], cst['identf'][:, :])
            B.inc(ins, 'te')
        tkt = B.tok('te')

        @B.on('ve')
        def _(e, c=c, tkt=tkt):
            B.wait(e, 've', tkt)
            for tt in range(16):
                ins = e.tensor_copy(x_tm[:, tt, 128 * c:128 * (c + 1)],
                                    psS[:, 128 * tt:128 * (tt + 1)])
            B.inc(ins, 've')
        tk_ev[c] = B.tok('ve')
    tk_xtm = B.tok('ve')

    @B.on('ve')
    def _(e):
        B.wait(e, 've', tk_xtm)
        for tt in range(16):
            ins = e.tensor_reduce(sm[:, tt:tt + 1], x_tm[:, tt, :], mybir.AxisListType.X, Op.add)
        B.inc(ins, 've')
        B.bar(e, 've')
        B.inc(e.tensor_scalar(sm[:, 16:32], sm[:, 0:16], -1.0 / DM, None, Op.mult), 've')
    tk_mu = B.tok('ve')

    @B.on('se')
    def _(e):
        B.wait(e, 'se', tk_xtm)
        for tt in range(16):
            B.inc(e.activation(pool[:, O_A + 8192:O_A + 9216], x_tm[:, tt, :], AF.Square,
                               accum_out=sm[:, 32 + tt:33 + tt]), 'se')
    tk_sq = B.tok('se')

    @B.on('ve')
    def _(e):
        B.wait(e, 've', tk_sq, tk_mu)
        e.tensor_tensor(sm[:, 48:64], sm[:, 16:32], sm[:, 16:32], Op.mult)
        e.tensor_scalar(sm[:, 112:128], sm[:, 32:48], 1.0 / DM, None, Op.mult)
        B.inc(e.tensor_tensor(sm[:, 48:64], sm[:, 112:128], sm[:, 48:64], Op.subtract), 've')
    tk_var = B.tok('ve')

    @B.on('se')
    def _(e):
        B.wait(e, 'se', tk_var)
        B.inc(e.activation(sm[:, 48:64], sm[:, 48:64], AF.Sqrt, bias=st.cv['eps'], scale=1.0), 'se')
    tk_sqv = B.tok('se')

    @B.on('ve')
    def _(e):
        B.wait(e, 've', tk_sqv)
        B.inc(e.reciprocal(sm[:, 48:64], sm[:, 48:64]), 've')
        B.bar(e, 've')
        for tt in range(16):
            B.inc(e.tensor_scalar(x_tm[:, tt, :], x_tm[:, tt, :],
                                  sm[:, 16 + tt:17 + tt], sm[:, 48 + tt:49 + tt],
                                  Op.add, Op.mult), 've')
    tk_xn = B.tok('ve')

    @B.on('gp')
    def _(e):
        B.wait(e, 'gp', tk_xn)
        for tt in range(16):
            B.inc(e.dma_start(st.scr['xn16'][128 * tt:128 * (tt + 1), :],
                              x_tm[:, tt, :]), 'dG', 16)
    tk_xnsp = B.tok('dG')

    tk_gacc = None
    for c in range(8):
        @B.on('te')
        def _(e, c=c):
            B.wait(e, 'te', tk_xn)
            B.wait(e, 'te', ('ve', B.c['ve']))
            for tt in range(16):
                ins = e.transpose(psS[:, 128 * tt:128 * (tt + 1)],
                                  x_tm[:, tt, 128 * c:128 * (c + 1)], cst['identf'][:, :])
            B.inc(ins, 'te')
        tkt = B.tok('te')

        @B.on('ve')
        def _(e, c=c, tkt=tkt):
            B.wait(e, 've', tkt)
            if c == 0:
                B.wait(e, 've', tk_g)
            e.tensor_copy(x1[:, c, :], psS[:, :])
            B.inc(e.tensor_copy(xn16[:, c, :], x1[:, c, :]), 've')
        tk_evc = B.tok('ve')

        @B.on('sy')
        def _(e, c=c, tk_evc=tk_evc):
            B.wait(e, 'sy', tk_evc)
            B.inc(e.dma_start(st.scr['xn32'][:, c, :], x1[:, c, :]), 'dW', 16)

    tk_evall = B.tok('ve')

    @B.on('te')
    def _(e):
        B.wait(e, 'te', tk_evall, tk_g)
        for tt in range(16):
            for c in range(8):
                ins = e.matmul(psA[:, 16 * tt:16 * (tt + 1)],
                               x1[:, c, 128 * tt:128 * (tt + 1)],
                               gatesb[:, c, :], start=(c == 0), stop=(c == 7))
        B.inc(ins, 'te')
    tk_gacc = B.tok('te')

    @B.on('ve')
    def _(e):
        B.wait(e, 've', tk_gacc)
        B.inc(e.tensor_copy(st.lg, psA[:, 0:256]), 've')
    st.tk_lg = B.tok('ve')

    @B.on('ve')
    def _(e):
        B.wait(e, 've', st.tk_lg)
        for tt in range(16):
            ins = e.tensor_reduce(sm[:, tt:tt + 1], x_tm[:, tt, :], mybir.AxisListType.X, Op.add)
        B.inc(ins, 've')
        B.bar(e, 've')
        B.inc(e.tensor_scalar(sm[:, 16:32], sm[:, 0:16], -1.0 / DM, None, Op.mult), 've')
    tk_mu2 = B.tok('ve')

    @B.on('se')
    def _(e):
        B.wait(e, 'se', tk_mu2)
        for tt in range(16):
            B.inc(e.activation(pool[:, O_A + 8192:O_A + 9216], x_tm[:, tt, :], AF.Square,
                               accum_out=sm[:, 32 + tt:33 + tt]), 'se')
    tk_sq2 = B.tok('se')

    @B.on('ve')
    def _(e):
        B.wait(e, 've', tk_sq2)
        e.tensor_tensor(sm[:, 48:64], sm[:, 16:32], sm[:, 16:32], Op.mult)
        e.tensor_scalar(sm[:, 112:128], sm[:, 32:48], 1.0 / DM, None, Op.mult)
        B.inc(e.tensor_tensor(sm[:, 48:64], sm[:, 112:128], sm[:, 48:64], Op.subtract), 've')
    tk_var2 = B.tok('ve')

    @B.on('se')
    def _(e):
        B.wait(e, 'se', tk_var2)
        B.inc(e.activation(sm[:, 48:64], sm[:, 48:64], AF.Sqrt, bias=st.cv['eps'], scale=1.0), 'se')
    tk_sqv2 = B.tok('se')

    @B.on('ve')
    def _(e):
        B.wait(e, 've', tk_sqv2, tk_xnsp)
        B.inc(e.reciprocal(sm[:, 48:64], sm[:, 48:64]), 've')
        B.bar(e, 've')
        for tt in range(16):
            B.inc(e.tensor_scalar(x_tm[:, tt, :], x_tm[:, tt, :],
                                  sm[:, 16 + tt:17 + tt], sm[:, 48 + tt:49 + tt],
                                  Op.add, Op.mult), 've')
    tk_xn1 = B.tok('ve')

    tk_sp1 = [None] * 8
    for c in range(8):
        xst = Rb(O_A + 8192 + (c % 2) * 1024, 2048)
        @B.on('te')
        def _(e, c=c):
            B.wait(e, 'te', tk_xn1)
            B.wait(e, 'te', ('ve', B.c['ve']))
            for tt in range(16):
                ins = e.transpose(psS[:, 128 * tt:128 * (tt + 1)],
                                  x_tm[:, tt, 128 * c:128 * (c + 1)], cst['identf'][:, :])
            B.inc(ins, 'te')
        tkt = B.tok('te')

        @B.on('ve')
        def _(e, c=c, xst=xst, tkt=tkt):
            B.wait(e, 've', tkt)
            if c >= 2:
                B.wait(e, 've', tk_sp1[c - 2])
            B.inc(e.tensor_scalar(xst, psS[:, :], cst['ln1g'][:, c:c + 1],
                                  cst['ln1b'][:, c:c + 1], Op.mult, Op.add), 've')
        tke = B.tok('ve')

        @B.on('sy')
        def _(e, c=c, xst=xst, tke=tke):
            B.wait(e, 'sy', tke)
            B.inc(e.dma_start(st.scr['xn1'][:, c, :], xst), 'dW', 16)
        tk_sp1[c] = B.tok('dW')
    st.tk_p5_done = B.tok('dW')


def ph_p6_moe(st):
    B = st.B
    R, Rb, Ri = st.R, st.Rb, st.Ri
    cst, sm = st.cst, st.sm
    psS, psA, psB, psC, psT = st.psS, st.psA, st.psB, st.psC, st.psT
    psTf = psT[:, :].bitcast(F32)
    xn16 = st.xn16_1
    pool = st.pool

    sc_tm = R(O_B, 256)
    sc_et = R(O_B + 256, 2048)
    work = R(O_B + 2304, 2048)
    valb = R(O_B + 4352, 256)
    idxb = pool[:, O_B + 4608:O_B + 4736].bitcast(U16)
    idx16 = Ri(O_B + 4736, 256)
    wv = R(O_B + 4864, 32)
    wsd_res = Rb(O_B + 5120, 16384, c=16)
    eo_fm = R(O_B + 13312, 2048, c=8)
    hsh = Rb(O_A, 16384, c=16)
    ring = [Rb(O_A + 8192 + 512 * i, 1024, c=8) for i in range(8)]
    gst = R(O_A + 12288, 1024)
    shst = [R(O_A + 13312 + 1024 * i, 1024) for i in range(2)]
    xsel = [Rb(O_A, 2048, c=8), Rb(O_A + 1024, 2048, c=8)]
    he = Rb(O_A + 4096, 4096, c=16)
    gst2 = R(O_A + 12288, 256)
    eo_tm = R(O_A + 12544, 2048).rearrange("p (c n) -> p c n", c=2)

    tk_p5ve = ('ve', B.c['ve'])

    @B.on('gp')
    def _(e):
        B.wait(e, 'gp', st.tk_p5_done)
        e.load_library(library_config.mlp)
    tk_lib = ('gp', B.c['gp'])

    @B.on('ve')
    def _(e):
        B.wait(e, 've', st.tk_lg)
        for tt in range(16):
            ins = e.tensor_reduce(sm[:, 128 + tt:129 + tt], st.lg[:, 16 * tt:16 * (tt + 1)],
                                  mybir.AxisListType.X, Op.max)
        B.inc(ins, 've')
        B.bar(e, 've')
        B.inc(e.tensor_scalar(sm[:, 144:160], sm[:, 128:144], -1.0, None, Op.mult), 've')
    tk_nm = B.tok('ve')

    for tt in range(16):
        @B.on('se')
        def _(e, tt=tt):
            B.wait(e, 'se', tk_nm)
            B.inc(e.activation(sc_tm[:, 16 * tt:16 * (tt + 1)], st.lg[:, 16 * tt:16 * (tt + 1)],
                               AF.Exp, bias=sm[:, 144 + tt:145 + tt], scale=1.0,
                               accum_out=sm[:, 160 + tt:161 + tt]), 'se')
    tk_exp = B.tok('se')

    @B.on('ve')
    def _(e):
        B.wait(e, 've', tk_exp)
        B.inc(e.reciprocal(sm[:, 176:192], sm[:, 160:176]), 've')
        B.bar(e, 've')
        for tt in range(16):
            B.inc(e.tensor_scalar(sc_tm[:, 16 * tt:16 * (tt + 1)], sc_tm[:, 16 * tt:16 * (tt + 1)],
                                  sm[:, 176 + tt:177 + tt], None, Op.mult), 've')
    tk_sc = B.tok('ve')

    for g4 in range(4):
        @B.on('te')
        def _(e, g4=g4):
            B.wait(e, 'te', tk_sc)
            B.wait(e, 'te', ('ve', B.c['ve']))
            for k in range(4):
                tt = 4 * g4 + k
                ins = e.transpose(psA[0:16, 128 * k:128 * (k + 1)],
                                  sc_tm[:, 16 * tt:16 * (tt + 1)], cst['identf'][:, :])
            B.inc(ins, 'te')
        tkt = B.tok('te')

        @B.on('ve')
        def _(e, g4=g4, tkt=tkt):
            B.wait(e, 've', tkt)
            B.inc(e.tensor_copy(sc_et[0:16, 512 * g4:512 * (g4 + 1)], psA[0:16, :]), 've')
    tk_et = B.tok('ve')

    def tpk_chunk(i):
        @B.on('ve')
        def _(e, i=i):
            if i == 0:
                e.tensor_copy(work[0:16, :], sc_et[0:16, :])
            e.max(out=valb[0:16, 8 * i:8 * i + 8], in_=work[0:16, :])
            e.max_index(idxb[0:16, 8 * i:8 * i + 8], valb[0:16, 8 * i:8 * i + 8], work[0:16, :])
            e.match_replace(out=work[0:16, :], in_to_replace=valb[0:16, 8 * i:8 * i + 8],
                            in_values=work[0:16, :], imm_value=-1.0)
    st._tpk = [lambda i=i: tpk_chunk(i) for i in range(32)]

    tk_ld = [None] * 8
    tk_use = [None] * 8

    @B.on('sy')
    def _(e):
        B.wait(e, 'sy', tk_p5ve)
        src = bass.AP(st.din['wsdT'].tensor, 0, [[DM, 128], [DM * 128, 16], [1, DM]])
        B.inc(e.dma_start(wsd_res[:, :, :], src), 'dW', 16)
    tk_wsd = B.tok('dW')

    tk_acc = []
    st._shg = ('se', B.c['se'])
    for th in range(2):
        plan_sh = B.se_plan()
        for shc in range(16):
            u = (2 * (16 * th + shc)) % 8
            @B.on('sy')
            def _(e, th=th, shc=shc, u=u):
                if tk_use[u] is not None:
                    B.wait(e, 'sy', tk_use[u])
                if tk_use[u + 1 if u + 1 < 8 else 0] is not None:
                    B.wait(e, 'sy', tk_use[(u + 1) % 8])
                for wi, wn in enumerate(['wsgT', 'wsuT']):
                    src = bass.AP(st.din[wn].tensor, 128 * shc,
                                  [[FSH, 128], [FSH * 128, 8], [1, 128]])
                    tk_ld[(u + wi) % 8] = B.inc(e.dma_start(ring[(u + wi) % 8][:, :, :], src),
                                                'dW', 16)

            @B.on('te')
            def _(e, th=th, shc=shc, u=u):
                B.wait(e, 'te', tk_ld[u], tk_ld[(u + 1) % 8], st.tk_p5_done)
                B.wait(e, 'te', ('ve', B.c['ve']))
                B.wait(e, 'te', st._shg)
                ins = None
                for f in range(8):
                    for half in range(2):
                        ins = e.matmul(psS[:, 512 * half:512 * (half + 1)],
                                       ring[u][:, f, :],
                                       xn16[:, f, 1024 * th + 512 * half:1024 * th + 512 * (half + 1)],
                                       start=(f == 0), stop=(f == 7))
                for f in range(8):
                    for half in range(2):
                        ins = e.matmul(psS[:, 1024 + 512 * half:1024 + 512 * (half + 1)],
                                       ring[(u + 1) % 8][:, f, :],
                                       xn16[:, f, 1024 * th + 512 * half:1024 * th + 512 * (half + 1)],
                                       start=(f == 0), stop=(f == 7))
                B.inc(ins, 'te')
            tk_use[u] = tk_use[(u + 1) % 8] = B.tok('te')
            tk_mm = B.tok('te')

            tk_g = plan_sh.add(
                (lambda tk_mm, tk_ve: lambda e: (B.wait(e, 'se', tk_mm),
                                                 B.wait(e, 'se', tk_ve),
                                                 e.activation(gst, psS[:, 0:1024], AF.Gelu))[-1]
                 )(tk_mm, ('ve', B.c['ve'])))
            st._shg = tk_g

            @B.on('ve')
            def _(e, th=th, shc=shc, tk_g=tk_g):
                B.wait(e, 've', tk_g)
                B.inc(e.tensor_tensor(hsh[:, shc, :], gst, psS[:, 1024:2048], Op.mult), 've')
            if st._tpk:
                st._tpk.pop(0)()
        plan_sh.flush()
        tk_hsh = B.tok('ve')

        for tb in range(8):
            half = 1024 * (tb % 2)
            @B.on('te')
            def _(e, th=th, tb=tb, half=half):
                B.wait(e, 'te', tk_hsh, tk_wsd)
                if len(tk_acc) >= 2:
                    B.wait(e, 'te', tk_acc[-2][0])
                ins = None
                for shc in range(16):
                    for q in range(2):
                        ins = e.matmul(psS[:, half + 512 * q:half + 512 * (q + 1)],
                                       hsh[:, shc, 128 * tb:128 * (tb + 1)],
                                       wsd_res[:, shc, 512 * q:512 * (q + 1)],
                                       start=(shc == 0), stop=(shc == 15))
                B.inc(ins, 'te')
            tk_mm = B.tok('te')

            @B.on('ve')
            def _(e, tb=tb, half=half, tk_mm=tk_mm):
                B.wait(e, 've', tk_mm)
                if len(tk_acc) >= 2:
                    B.wait(e, 've', tk_acc[-2][1])
                B.inc(e.tensor_copy(shst[tb % 2], psS[:, half:half + 1024]), 've')
            tk_ev = B.tok('ve')

            @B.on('sy')
            def _(e, th=th, tb=tb, tk_ev=tk_ev):
                B.wait(e, 'sy', tk_ev)
                r0 = 1024 * th + 128 * tb
                B.inc(e.dma_start(st.scr['acc'][r0:r0 + 128, :], shst[tb % 2]), 'dW', 16)
            tk_acc.append((tk_ev, B.tok('dW')))
    tk_shared_done = B.tok('dW')
    tk_hsh_free = ('te', B.c['te'])

    for f in st._tpk:
        f()
    st._tpk = []
    tk_topk = B.tok('ve')

    @B.on('gp')
    def _(e):
        B.wait(e, 'gp', tk_topk)
        B.inc(e.dma_start(st.scr['idx'], idxb[0:16, :]), 'dG', 16)
        B.inc(e.dma_start(st.scr['val'], valb[0:16, :]), 'dG', 16)
    tk_bnc = B.tok('dG')

    @B.on('gp')
    def _(e):
        B.wait(e, 'gp', tk_bnc)
        with B.nc.allow_non_contiguous_dma(reason="tiny routing relayouts"):
            for ex in range(EXP):
                for rep in range(2):
                    src = bass.AP(st.scr['idx'].tensor, 256 * ex, [[1, 16], [16, 16]])
                    B.inc(e.dma_start(idx16[16 * rep:16 * rep + 16, 16 * ex:16 * (ex + 1)],
                                      src), 'dG', 16)
                srcv = bass.AP(st.scr['val'].tensor, 256 * ex, [[1, 128], [128, 2]])
                B.inc(e.dma_start(wv[:, 2 * ex:2 * (ex + 1)], srcv), 'dG', 16)
    tk_rel = B.tok('dG')

    tk_gather = [None, None]
    tk_xuse = [None, None]
    tk_scat = None
    tk_eo = None
    st._tk_wd_prev = None

    def gather(ex):
        @B.on('gp')
        def _(e, ex=ex):
            B.wait(e, 'gp', tk_rel, tk_hsh_free)
            if tk_xuse[ex % 2] is not None:
                B.wait(e, 'gp', tk_xuse[ex % 2])
            tk_gather[ex % 2] = B.inc(
                e.dma_gather(xsel[ex % 2][:, :, :], st.scr['xn16'],
                             idx16[:, 16 * ex:16 * (ex + 1)], CAP, CAP, DM,
                             transpose=True), 'dG', 16)

    gather(0)
    st._exg = ('se', B.c['se'])
    for ex in range(EXP):
        if ex + 1 < EXP:
            gather(ex + 1)
        plan_ex = B.se_plan()
        for oc in range(16):
            u = (2 * oc) % 8
            @B.on('sy')
            def _(e, ex=ex, oc=oc, u=u):
                for wi in range(2):
                    if tk_use[(u + wi) % 8] is not None:
                        B.wait(e, 'sy', tk_use[(u + wi) % 8])
                for wi, wn in enumerate(['wgT', 'wuT']):
                    src = bass.AP(st.din[wn].tensor, ex * DM * FSH + 128 * oc,
                                  [[FSH, 128], [FSH * 128, 8], [1, 128]])
                    tk_ld[(u + wi) % 8] = B.inc(e.dma_start(ring[(u + wi) % 8][:, :, :], src),
                                                'dW', 16)

            @B.on('te')
            def _(e, ex=ex, oc=oc, u=u):
                B.wait(e, 'te', tk_ld[u], tk_ld[(u + 1) % 8], tk_gather[ex % 2])
                B.wait(e, 'te', ('ve', B.c['ve']))
                B.wait(e, 'te', st._exg)
                ins = None
                for f in range(8):
                    ins = e.matmul(psA[:, 0:256], ring[u][:, f, :], xsel[ex % 2][:, f, :],
                                   start=(f == 0), stop=(f == 7))
                for f in range(8):
                    ins = e.matmul(psB[:, 0:256], ring[(u + 1) % 8][:, f, :], xsel[ex % 2][:, f, :],
                                   start=(f == 0), stop=(f == 7))
                B.inc(ins, 'te')
            tk_use[u] = tk_use[(u + 1) % 8] = B.tok('te')
            if oc == 15:
                tk_xuse[ex % 2] = B.tok('te')
            tk_mm = B.tok('te')

            tk_g = plan_ex.add(
                (lambda tk_mm, tk_ve: lambda e: (B.wait(e, 'se', tk_mm),
                                                 B.wait(e, 'se', tk_ve),
                                                 e.activation(gst2, psA[:, 0:256], AF.Gelu))[-1]
                 )(tk_mm, ('ve', B.c['ve'])))
            st._exg = tk_g

            @B.on('ve')
            def _(e, ex=ex, oc=oc, tk_g=tk_g):
                B.wait(e, 've', tk_g)
                if oc == 0 and st._tk_wd_prev is not None:
                    B.wait(e, 've', st._tk_wd_prev)
                B.inc(e.tensor_tensor(he[:, oc, :], gst2, psB[:, 0:256], Op.mult), 've')
        plan_ex.flush()
        tk_he = B.tok('ve')

        for oc in range(16):
            u = oc % 8
            @B.on('sy')
            def _(e, ex=ex, oc=oc, u=u):
                if tk_use[u] is not None:
                    B.wait(e, 'sy', tk_use[u])
                src = bass.AP(st.din['wdT'].tensor, ex * FSH * DM + 128 * oc * DM,
                              [[DM, 128], [128, 8], [1, 128]])
                tk_ld[u] = B.inc(e.dma_start(ring[u][:, :, :], src), 'dW', 16)

            @B.on('te')
            def _(e, ex=ex, oc=oc, u=u):
                B.wait(e, 'te', tk_ld[u], tk_he)
                if oc == 0:
                    B.wait(e, 'te', ('ve', B.c['ve']))
                ins = None
                rflat = ring[u].rearrange("p c n -> p (c n)")
                for sh in range(2):
                    for q in range(2):
                        ins = e.matmul(psS[:, 1024 * sh + 512 * q:1024 * sh + 512 * (q + 1)],
                                       he[:, oc, 128 * sh:128 * (sh + 1)],
                                       rflat[:, 512 * q:512 * (q + 1)],
                                       start=(oc == 0), stop=(oc == 15))
                B.inc(ins, 'te')
            tk_use[u] = B.tok('te')
        tk_wd = B.tok('te')
        st._tk_wd_prev = tk_wd

        @B.on('ve')
        def _(e, ex=ex):
            B.wait(e, 've', tk_wd, tk_rel, tk_shared_done)
            if tk_scat is not None:
                B.wait(e, 've', tk_scat)
            for sh in range(2):
                ins = e.tensor_scalar(eo_tm[:, sh, :], psS[:, 1024 * sh:1024 * (sh + 1)],
                                      wv[:, 2 * ex + sh:2 * ex + sh + 1], None, Op.mult)
            B.inc(ins, 've')
        tk_eo = B.tok('ve')

        @B.on('gp')
        def _(e, ex=ex, tk_eo=tk_eo):
            B.wait(e, 'gp', tk_eo, tk_shared_done)
            B.inc(e.dma_scatter_add(st.scr['acc'], eo_tm[:, :, :],
                                    idx16[:, 16 * ex:16 * (ex + 1)], CAP, CAP, DM), 'dG', 16)
        tk_scat = B.tok('dG')

    g1 = Rb(O_C, 16384, c=8)
    st.g1 = g1
    ast = [R(O_A + 8192 + 1024 * i, 1024) for i in range(2)]
    bst = [Rb(O_A + 10240 + 512 * i, 1024) for i in range(2)]
    tk_tt = [None, None]
    plan_g1 = B.se_plan()
    for tt in range(16):
        @B.on('sy')
        def _(e, tt=tt):
            B.wait(e, 'sy', tk_scat)
            if tk_tt[tt % 2] is not None:
                B.wait(e, 'sy', tk_tt[tt % 2])
            B.inc(e.dma_start(ast[tt % 2], st.scr['acc'][128 * tt:128 * (tt + 1), :]), 'dW', 16)
        tk_a = B.tok('dW')

        tk_s = plan_g1.add(
            (lambda tt, tk_a, tk_te: lambda e: (B.wait(e, 'se', tk_a),
                                                B.wait(e, 'se', tk_te),
                                                e.activation(bst[tt % 2], ast[tt % 2], AF.Silu))[-1]
             )(tt, tk_a, ('te', B.c['te'])))

        @B.on('te')
        def _(e, tt=tt, tk_s=tk_s):
            B.wait(e, 'te', tk_s)
            B.wait(e, 'te', ('ve', B.c['ve']))
            for c in range(8):
                ins = e.transpose(psT[:, 128 * c:128 * (c + 1)],
                                  bst[tt % 2].rearrange("p (c n) -> p c n", c=8)[:, c, :],
                                  cst['identb'][:, :])
            B.inc(ins, 'te')
        tk_tt[tt % 2] = B.tok('te')
        tkt = B.tok('te')

        @B.on('ve')
        def _(e, tt=tt, tkt=tkt):
            B.wait(e, 've', tkt)
            for c in range(8):
                ins = e.tensor_copy(g1[:, c, 128 * tt:128 * (tt + 1)],
                                    psT[:, 128 * c:128 * (c + 1)])
            B.inc(ins, 've')
    plan_g1.flush()
    st.tk_p6_done = B.tok('ve')


def ph_p7_l1_attn(st):
    import os
    DIS = os.environ.get('P7DIS', '').split(',')
    B = st.B
    R, Rb, Ri = st.R, st.Rb, st.Ri
    cst, sm = st.cst, st.sm
    psS, psA, psB, psC, psT = st.psS, st.psA, st.psB, st.psC, st.psT
    psTf = psT[:, :].bitcast(F32)
    pool = st.pool

    xn1 = Rb(O_A, 16384, c=8)
    gsig = Rb(O_A + 8192, 16384, c=8)
    xmid = R(O_A, N_A, c=8)
    attn_full = Rb(O_B, 16384, c=8)
    q2 = Rb(O_B + 8192, 2048)
    k2 = Rb(O_B + 9216, 2048)
    v2 = Rb(O_B + 10240, 2048)
    vtm = Rb(O_B + 11264, 2048, c=16)
    aqe = Rb(O_B + 12288, 2048)
    attnT = Rb(O_B + 13312, 2048, c=16)
    ubuf = R(O_B + 14336, 2048)
    cw_res = pool[:, O_D + 3072:O_D + 3584].bitcast(BF16)
    drive = R(O_D, 2048)
    states = pool[:, O_D + 2048:O_D + 3072].bitcast(BF16)
    ring = [Rb(O_D + 3584 + 512 * i, 1024, c=8) for i in range(3)]
    st.xmid = xmid

    tk_p6ve = ('ve', B.c['ve'])
    tk_p6te = ('te', B.c['te'])

    @B.on('sy')
    def _(e):
        B.wait(e, 'sy', st.tk_p6_done, tk_p6te)
        B.wait(e, 'sy', ('se', B.c['se']))
        for c in range(8):
            B.inc(e.dma_start(xn1[:, c, :], st.scr['xn1'][:, c, :]), 'dW', 16)
        B.inc(e.dma_start(cw_res, st.din['cwT']), 'dW', 16)
    tk_xn1 = B.tok('dW')

    tk_ld = [None] * 3
    tk_use = [None] * 3

    def load_pair(p, slot):
        @B.on('sy')
        def _(e, p=p, slot=slot):
            if tk_use[slot] is not None:
                B.wait(e, 'sy', tk_use[slot])
            pass

    for p in range(8):
        for wi in range(3):
            @B.on('sy')
            def _(e, p=p, wi=wi):
                if tk_use[wi] is not None:
                    B.wait(e, 'sy', tk_use[wi])
                if p == 0 and wi == 0:
                    B.wait(e, 'sy', st.tk_p6_done)
                src = bass.AP(st.din['wqkvT'].tensor, 1024 * wi + 128 * p,
                              [[3 * DM, 128], [3 * DM * 128, 8], [1, 128]])
                tk_ld[wi] = B.inc(e.dma_start(ring[wi][:, :, :], src), 'dW', 16)

        for wi, dst in enumerate([q2, k2, v2]):
            @B.on('te')
            def _(e, p=p, wi=wi):
                B.wait(e, 'te', tk_ld[wi], tk_xn1)
                B.wait(e, 'te', ('ve', B.c['ve']))
                ins = None
                for f in range(8):
                    for n in range(4):
                        ins = e.matmul(psS[:, 512 * n:512 * (n + 1)], ring[wi][:, f, :],
                                       xn1[:, f, 512 * n:512 * (n + 1)],
                                       start=(f == 0), stop=(f == 7))
                B.inc(ins, 'te')
            tk_use[wi] = B.tok('te')
            tk_mm = B.tok('te')

            @B.on('ve')
            def _(e, dst=dst, tk_mm=tk_mm):
                B.wait(e, 've', tk_mm)
                B.inc(e.tensor_copy(dst, psS[:, :]), 've')

        for g in range(2):
            @B.on('te')
            def _(e, g=g):
                B.wait(e, 'te', ('ve', B.c['ve']))
                for kt in range(8):
                    ins = e.transpose(psT[:, 128 * kt:128 * (kt + 1)],
                                      v2[:, 128 * (8 * g + kt):128 * (8 * g + kt + 1)],
                                      cst['identb'][:, :])
                B.inc(ins, 'te')
            tkv = B.tok('te')

            @B.on('ve')
            def _(e, g=g, tkv=tkv):
                B.wait(e, 've', tkv)
                for kt in range(8):
                    ins = e.tensor_copy(vtm[:, 8 * g + kt, :], psT[:, 128 * kt:128 * (kt + 1)])
                B.inc(ins, 've')

        tk_qkv_p = ('ve', B.c['ve'])
        plan_at = B.se_plan()
        for hl in range(2):
            hb = 64 * hl
            for c in range(8):
                for qt in range(2):
                    qg = 256 * c + 128 * qt
                    if c == 0:
                        k0 = 0
                        nk = 128 * (qt + 1)
                        moff = 384 - nk
                    else:
                        k0 = 256 * (c - 1) + 128 * qt
                        nk = 384
                        moff = 0
                    if not hasattr(st, '_l1cnt'):
                        st._l1cnt = 0
                        st._l1exp = [None, None]
                    sb = [psA, psC][st._l1cnt % 2]
                    sbi = st._l1cnt % 2
                    st._l1cnt += 1
                    @B.on('te')
                    def _(e, hb=hb, qg=qg, k0=k0, nk=nk, sb=sb, sbi=sbi):
                        if st._l1exp[sbi] is not None:
                            B.wait(e, 'te', st._l1exp[sbi])
                        B.wait(e, 'te', tk_qkv_p)
                        B.inc(e.matmul(sb[:, 0:nk], q2[hb:hb + 64, qg:qg + 128],
                                       k2[hb:hb + 64, k0:k0 + nk], start=True, stop=True),
                              'te')
                    tk_sc = B.tok('te')

                    @B.on('ve')
                    def _(e, nk=nk, moff=moff, sb=sb, tk_sc=tk_sc):
                        B.wait(e, 've', tk_sc)
                        B.inc(e.tensor_tensor(sb[:, 0:nk], sb[:, 0:nk],
                                              cst['maskL1'][:, moff:moff + nk], Op.add), 've')
                    tk_msk = B.tok('ve')

                    tk_exp = plan_at.add(
                        (lambda nk, sb, tk_msk: lambda e: (
                            B.wait(e, 'se', tk_msk),
                            e.activation(aqe[:, 0:nk], sb[:, 0:nk], AF.Exp,
                                         bias=st.cv['neg8'], scale=0.125,
                                         accum_out=sm[:, 128:129]))[-1]
                         )(nk, sb, tk_msk))
                    st._l1exp[sbi] = tk_exp

                    @B.on('ve')
                    def _(e, nk=nk, tk_exp=tk_exp):
                        B.wait(e, 've', tk_exp)
                        e.reciprocal(sm[:, 129:130], sm[:, 128:129])
                        B.inc(e.tensor_scalar(aqe[:, 0:nk], aqe[:, 0:nk],
                                              sm[:, 129:130], None, Op.mult), 've')
                    tk_nrm = B.tok('ve')

                    nkt = nk // 128
                    @B.on('te')
                    def _(e, nkt=nkt, tk_nrm=tk_nrm):
                        B.wait(e, 'te', tk_nrm)
                        B.wait(e, 'te', ('ve', B.c['ve']))
                        for kt in range(nkt):
                            ins = e.transpose(psT[:, 128 * kt:128 * (kt + 1)],
                                              aqe[:, 128 * kt:128 * (kt + 1)],
                                              cst['identb'][:, :])
                        B.inc(ins, 'te')
                    tk_tp = B.tok('te')

                    @B.on('ve')
                    def _(e, nkt=nkt, tk_tp=tk_tp):
                        B.wait(e, 've', tk_tp)
                        for kt in range(nkt):
                            ins = e.tensor_copy(attnT[:, kt, :], psT[:, 128 * kt:128 * (kt + 1)])
                        B.inc(ins, 've')
                    tk_aT = B.tok('ve')

                    @B.on('te')
                    def _(e, hb=hb, c=c, qt=qt, k0=k0, nkt=nkt, tk_aT=tk_aT):
                        B.wait(e, 'te', tk_aT)
                        if qt == 0:
                            B.wait(e, 'te', ('ve', B.c['ve']))
                        for kt in range(nkt):
                            vi = (k0 // 128) + kt
                            ins = e.matmul(psB[hb:hb + 64, 128 * qt:128 * (qt + 1)],
                                           vtm[:, vi, hb:hb + 64], attnT[:, kt, :],
                                           start=(kt == 0), stop=(kt == nkt - 1),
                                           tile_position=(0, hb))
                        B.inc(ins, 'te')
                    tk_av = B.tok('te')

                @B.on('ve')
                def _(e, p=p, hb=hb, c=c, tk_av=tk_av):
                    B.wait(e, 've', tk_av)
                    B.inc(e.tensor_copy(attn_full[hb:hb + 64, p, 256 * c:256 * (c + 1)],
                                        psB[hb:hb + 64, 0:256]), 've')
        plan_at.flush()
    tk_attn = B.tok('ve')

    st._ugsig = ('se', B.c['se'])
    plan_ug = B.se_plan()
    for j in range(9):
        wi = j % 3
        @B.on('sy')
        def _(e, j=j, wi=wi):
            B.wait(e, 'sy', tk_use[wi])
            src = bass.AP(st.din['wugT'].tensor, 128 * j,
                          [[1152, 128], [1152 * 128, 8], [1, 128]])
            tk_ld[wi] = B.inc(e.dma_start(ring[wi][:, :, :], src), 'dW', 16)

        @B.on('te')
        def _(e, j=j, wi=wi):
            B.wait(e, 'te', tk_ld[wi])
            B.wait(e, 'te', ('ve', B.c['ve']))
            B.wait(e, 'te', st._ugsig)
            ins = None
            for f in range(8):
                for n in range(4):
                    ins = e.matmul(psS[:, 512 * n:512 * (n + 1)], ring[wi][:, f, :],
                                   xn1[:, f, 512 * n:512 * (n + 1)],
                                   start=(f == 0), stop=(f == 7))
            B.inc(ins, 'te')
        tk_use[wi] = B.tok('te')
        tk_mm = B.tok('te')

        if j == 0:
            @B.on('ve')
            def _(e, tk_mm=tk_mm):
                B.wait(e, 've', tk_mm)
                B.inc(e.tensor_scalar(ubuf, psS[:, :], cst['bug'][:, 0:1], None, Op.add), 've')
        else:
            st._ugsig = plan_ug.add(
                (lambda j, tk_mm: lambda e: (
                    B.wait(e, 'se', tk_mm),
                    e.activation(gsig[:, j - 1, :], psS[:, :], AF.Sigmoid,
                                 bias=cst['bug'][:, j:j + 1], scale=1.0))[-1]
                 )(j, tk_mm))
    plan_ug.flush()
    tk_ug = B.tok('se')

    @B.on('sy')
    def _(e):
        B.wait(e, 'sy', tk_use[0])
        src = bass.AP(st.din['bwT'].tensor, 0, [[SD, 128], [SD * 128, 8], [1, 128]])
        tk_ld[0] = B.inc(e.dma_start(ring[0][:, :, :], src), 'dW', 16)

    @B.on('te')
    def _(e):
        B.wait(e, 'te', tk_ld[0])
        B.wait(e, 'te', ('ve', B.c['ve']))
        B.wait(e, 'te', st._ugsig)
        ins = None
        for f in range(8):
            for n in range(4):
                ins = e.matmul(psS[:, 512 * n:512 * (n + 1)], ring[0][:, f, :],
                               xn1[:, f, 512 * n:512 * (n + 1)],
                               start=(f == 0), stop=(f == 7))
        B.inc(ins, 'te')
    tk_use[0] = B.tok('te')
    tk_bw = B.tok('te')

    @B.on('ve')
    def _(e):
        B.wait(e, 've', tk_bw)
        B.inc(e.tensor_tensor(drive, psS[:, :], ubuf, Op.add), 've')
        B.bar(e, 've')
        B.inc(e.tensor_tensor_scan(states, cst['Avec'][:, 0:1].to_broadcast([128, T]),
                                   drive, 0.0, Op.mult, Op.add), 've')
    tk_scan = B.tok('ve')

    st.attn_full1 = attn_full
    st.gsig = gsig
    st.states_ap = states
    st.drive_ap = drive
    if 'combine' in DIS:
        st.tk_p7_done = B.tok('ve')
        return
    st1 = R(O_B + 12288, 2048)
    st2 = R(O_B + 8192, 2048)
    xnst = [R(O_B + 10240, 2048), R(O_B + 14336, 2048)]
    tk_xnld = [None, None]
    tk_cmb = [None, None]
    for d in range(8):
        @B.on('sy')
        def _(e, d=d):
            B.wait(e, 'sy', tk_attn, tk_scan)
            if tk_cmb[d % 2] is not None:
                B.wait(e, 'sy', tk_cmb[d % 2])
            tk_xnld[d % 2] = B.inc(e.dma_start(xnst[d % 2], st.scr['xn32'][:, d, :]), 'dW', 16)

        wi = d % 3
        @B.on('sy')
        def _(e, d=d, wi=wi):
            B.wait(e, 'sy', tk_use[wi])
            src = bass.AP(st.din['l1_woT'].tensor, 128 * d, [[DM, 128], [DM * 128, 8], [1, 128]])
            tk_ld[wi] = B.inc(e.dma_start(ring[wi][:, :, :], src), 'dW', 16)

        @B.on('te')
        def _(e, d=d, wi=wi):
            B.wait(e, 'te', tk_ld[wi], tk_attn, tk_scan)
            B.wait(e, 'te', ('ve', B.c['ve']))
            ins = None
            for f in range(8):
                for n in range(4):
                    ins = e.matmul(psS[:, 512 * n:512 * (n + 1)], ring[wi][:, f, :],
                                   attn_full[:, f, 512 * n:512 * (n + 1)],
                                   start=(f == 0), stop=(f == 7))
            ybank = [psA, psB, psC, psTf]
            for n in range(4):
                ins = e.matmul(ybank[n][:, 0:512], cw_res[:, 128 * d:128 * (d + 1)],
                               states[:, 512 * n:512 * (n + 1)], start=True, stop=True)
            B.inc(ins, 'te')
        tk_use[wi] = B.tok('te')
        tk_mm = B.tok('te')

        @B.on('ve')
        def _(e, d=d, tk_mm=tk_mm):
            B.wait(e, 've', tk_mm, tk_ug, tk_xnld[d % 2])
            ybank = [psA, psB, psC, psTf]
            e.tensor_scalar(st1, psS[:, :], cst['bO'][:, d:d + 1], None, Op.add)
            for n in range(4):
                e.tensor_tensor(st2[:, 512 * n:512 * (n + 1)], st1[:, 512 * n:512 * (n + 1)],
                                ybank[n][:, 0:512], Op.subtract)
            e.tensor_tensor(st2, st2, gsig[:, d, :], Op.mult)
            for n in range(4):
                e.tensor_tensor(st2[:, 512 * n:512 * (n + 1)], st2[:, 512 * n:512 * (n + 1)],
                                ybank[n][:, 0:512], Op.add)
            tk_cmb[d % 2] = B.inc(e.tensor_tensor(xmid[:, d, :], st2, xnst[d % 2], Op.add), 've')
    tk_xmid = B.tok('ve')
    if 'mlp' in DIS:
        st.tk_p7_done = tk_xmid
        return

    x_tm = R(O_B, 16384, c=16)
    for c in range(8):
        @B.on('te')
        def _(e, c=c):
            B.wait(e, 'te', tk_xmid)
            B.wait(e, 'te', ('ve', B.c['ve']))
            for tt in range(16):
                ins = e.transpose(psS[:, 128 * tt:128 * (tt + 1)],
                                  xmid[:, c, 128 * tt:128 * (tt + 1)], cst['identf'][:, :])
            B.inc(ins, 'te')
        tkt = B.tok('te')

        @B.on('ve')
        def _(e, c=c, tkt=tkt):
            B.wait(e, 've', tkt)
            for tt in range(16):
                ins = e.tensor_copy(x_tm[:, tt, 128 * c:128 * (c + 1)],
                                    psS[:, 128 * tt:128 * (tt + 1)])
            B.inc(ins, 've')
    tk_tm = B.tok('ve')

    @B.on('ve')
    def _(e):
        B.wait(e, 've', tk_tm)
        for tt in range(16):
            ins = e.tensor_reduce(sm[:, tt:tt + 1], x_tm[:, tt, :], mybir.AxisListType.X, Op.add)
        B.inc(ins, 've')
        B.bar(e, 've')
        B.inc(e.tensor_scalar(sm[:, 16:32], sm[:, 0:16], -1.0 / DM, None, Op.mult), 've')
    tk_mu = B.tok('ve')

    @B.on('se')
    def _(e):
        B.wait(e, 'se', tk_mu)
        for tt in range(16):
            B.inc(e.activation(pool[:, O_D:O_D + 1024], x_tm[:, tt, :],
                               AF.Square, accum_out=sm[:, 32 + tt:33 + tt]), 'se')
    tk_sq = B.tok('se')

    @B.on('ve')
    def _(e):
        B.wait(e, 've', tk_sq)
        e.tensor_tensor(sm[:, 48:64], sm[:, 16:32], sm[:, 16:32], Op.mult)
        e.tensor_scalar(sm[:, 112:128], sm[:, 32:48], 1.0 / DM, None, Op.mult)
        B.inc(e.tensor_tensor(sm[:, 48:64], sm[:, 112:128], sm[:, 48:64], Op.subtract), 've')
    tk_var = B.tok('ve')

    @B.on('se')
    def _(e):
        B.wait(e, 'se', tk_var)
        B.inc(e.activation(sm[:, 48:64], sm[:, 48:64], AF.Sqrt, bias=st.cv['eps'], scale=1.0), 'se')
    tk_sqv = B.tok('se')

    @B.on('ve')
    def _(e):
        B.wait(e, 've', tk_sqv)
        B.inc(e.reciprocal(sm[:, 48:64], sm[:, 48:64]), 've')
        B.bar(e, 've')
        for tt in range(16):
            B.inc(e.tensor_scalar(x_tm[:, tt, :], x_tm[:, tt, :],
                                  sm[:, 16 + tt:17 + tt], sm[:, 48 + tt:49 + tt],
                                  Op.add, Op.mult), 've')
    tk_h2n = B.tok('ve')

    h2stg = [pool[:, O_D + i * 1024:O_D + i * 1024 + 1024].bitcast(BF16) for i in range(2)]
    tk_h2sp = [None, None]
    for c in range(8):
        @B.on('te')
        def _(e, c=c):
            B.wait(e, 'te', tk_h2n)
            B.wait(e, 'te', ('ve', B.c['ve']))
            for tt in range(16):
                ins = e.transpose(psS[:, 128 * tt:128 * (tt + 1)],
                                  x_tm[:, tt, 128 * c:128 * (c + 1)], cst['identf'][:, :])
            B.inc(ins, 'te')
        tkt = B.tok('te')

        @B.on('ve')
        def _(e, c=c, tkt=tkt):
            B.wait(e, 've', tkt)
            if tk_h2sp[c % 2] is not None:
                B.wait(e, 've', tk_h2sp[c % 2])
            B.inc(e.tensor_scalar(h2stg[c % 2], psS[:, :], cst['ln2g'][:, c:c + 1],
                                  cst['ln2b'][:, c:c + 1], Op.mult, Op.add), 've')
        tke = B.tok('ve')

        @B.on('sy')
        def _(e, c=c, tke=tke):
            B.wait(e, 'sy', tke)
            B.inc(e.dma_start(st.scr['h2'][:, c, :], h2stg[c % 2]), 'dW', 16)
        tk_h2sp[c % 2] = B.tok('dW')
    tk_h2 = B.tok('dW')

    h3q = Rb(O_B, 16384, c=32)
    h2buf = [Rb(O_D + 0, 4096, c=8), Rb(O_D + 2048, 4096, c=8)]
    st.h3q = h3q
    st.h2buf1 = h2buf[1]
    bank8 = [psS[:, 0:512], psS[:, 512:1024], psS[:, 1024:1536], psS[:, 1536:2048],
             psA, psB, psC, psTf]
    mring = [Rb(O_B + 8192 + 512 * i, 1024, c=8) for i in range(4)]
    tk_mld = [None] * 4
    tk_muse = [None] * 4
    tk_h2ld = [None, None]
    for q in range(4):
        @B.on('sy')
        def _(e, q=q):
            B.wait(e, 'sy', tk_h2)
            if tk_h2ld[q % 2] is not None:
                B.wait(e, 'sy', ('te', B.c['te']))
            for f in range(8):
                ins = e.dma_start(h2buf[q % 2][:, f, :],
                                  st.scr['h2'][:, f, 512 * q:512 * (q + 1)])
                B.inc(ins, 'dW', 16)
        tk_h2ld[q % 2] = B.tok('dW')

        plan_w1 = B.se_plan()
        g_tok = [None] * 32
        for hc in range(32):
            u = hc % 4
            @B.on('sy')
            def _(e, q=q, hc=hc, u=u):
                if tk_muse[u] is not None:
                    B.wait(e, 'sy', tk_muse[u])
                if q == 0 and hc == 0:
                    B.wait(e, 'sy', ('te', B.c['te']))
                src = bass.AP(st.din['mw1T'].tensor, 128 * hc,
                              [[MLPH, 128], [MLPH * 128, 8], [1, 128]])
                tk_mld[u] = B.inc(e.dma_start(mring[u][:, :, :], src), 'dW', 16)

            pb = [psA, psB][hc % 2]
            @B.on('te')
            def _(e, q=q, hc=hc, u=u, pb=pb):
                B.wait(e, 'te', tk_mld[u], tk_h2ld[q % 2])
                if hc == 0:
                    B.wait(e, 'te', ('ve', B.c['ve']))
                if hc >= 2:
                    B.wait(e, 'te', g_tok[hc - 2])
                ins = None
                for f in range(8):
                    ins = e.matmul(pb[:, 0:512], mring[u][:, f, :], h2buf[q % 2][:, f, :],
                                   start=(f == 0), stop=(f == 7))
                B.inc(ins, 'te')
            tk_muse[u] = B.tok('te')
            tk_mm = B.tok('te')

            g_tok[hc] = plan_w1.add(
                (lambda q, hc, pb, tk_mm: lambda e: (
                    B.wait(e, 'se', tk_mm),
                    e.activation(h3q[:, hc, :], pb[:, 0:512], AF.Gelu,
                                 bias=cst['mb1'][:, hc:hc + 1], scale=1.0))[-1]
                 )(q, hc, pb, tk_mm))
        plan_w1.flush()
        tk_h3 = B.tok('se')

        for hc in range(32):
            u = hc % 4
            @B.on('sy')
            def _(e, q=q, hc=hc, u=u):
                B.wait(e, 'sy', tk_muse[u])
                src = bass.AP(st.din['mw2T'].tensor, (128 * hc) * DM,
                              [[DM, 128], [128, 8], [1, 128]])
                tk_mld[u] = B.inc(e.dma_start(mring[u][:, :, :], src), 'dW', 16)

            @B.on('te')
            def _(e, q=q, hc=hc, u=u):
                B.wait(e, 'te', tk_mld[u], tk_h3)
                if hc == 0:
                    B.wait(e, 'te', ('ve', B.c['ve']))
                ins = None
                for d in range(8):
                    ins = e.matmul(bank8[d][:, 0:512], mring[u][:, d, :], h3q[:, hc, :],
                                   start=(hc == 0), stop=(hc == 31))
                B.inc(ins, 'te')
            tk_muse[u] = B.tok('te')
        tk_w2 = B.tok('te')

        @B.on('ve')
        def _(e, q=q, tk_w2=tk_w2):
            B.wait(e, 've', tk_w2)
            for d in range(8):
                ins = e.scalar_tensor_tensor(xmid[:, d, 512 * q:512 * (q + 1)],
                                             bank8[d][:, 0:512], cst['mb2'][:, d:d + 1],
                                             xmid[:, d, 512 * q:512 * (q + 1)],
                                             Op.add, Op.add)
            B.inc(ins, 've')
    st.tk_p7_done = B.tok('ve')


def ph_p8_final(st):
    B = st.B
    R, Rb = st.R, st.Rb
    cst = st.cst
    psS = st.psS
    pool = st.pool
    xmid = st.xmid
    g1 = st.g1

    Xst = [R(O_B + 2048 * i, 2048) for i in range(2)]
    tst = R(O_B + 4096, 2048)
    ost = [R(O_B + 6144 + 1024 * i, 1024) for i in range(2)]
    tk_x = [None, None]
    tk_c = [None, None]
    for d in range(8):
        @B.on('sy')
        def _(e, d=d):
            B.wait(e, 'sy', st.tk_p7_done)
            if tk_c[d % 2] is not None:
                B.wait(e, 'sy', tk_c[d % 2])
            tk_x[d % 2] = B.inc(e.dma_start(Xst[d % 2], st.scr['X'][:, d, :]), 'dW', 16)

        @B.on('ve')
        def _(e, d=d):
            B.wait(e, 've', st.tk_p7_done, tk_x[d % 2], st.tk_p6_done)
            e.tensor_tensor(tst, xmid[:, d, :], g1[:, d, :], Op.mult)
            tk_c[d % 2] = B.inc(e.tensor_tensor(xmid[:, d, :], tst, Xst[d % 2], Op.add), 've')
    tk_x2 = B.tok('ve')

    tk_out = [None, None]
    for tt in range(16):
        @B.on('te')
        def _(e, tt=tt):
            B.wait(e, 'te', tk_x2)
            B.wait(e, 'te', ('ve', B.c['ve']))
            for d in range(8):
                ins = e.transpose(psS[:, 128 * d:128 * (d + 1)],
                                  xmid[:, d, 128 * tt:128 * (tt + 1)], cst['identf'][:, :])
            B.inc(ins, 'te')
        tkt = B.tok('te')

        @B.on('ve')
        def _(e, tt=tt, tkt=tkt):
            B.wait(e, 've', tkt)
            if tk_out[tt % 2] is not None:
                B.wait(e, 've', tk_out[tt % 2])
            B.inc(e.tensor_copy(ost[tt % 2], psS[:, 0:1024]), 've')
        tke = B.tok('ve')

        @B.on('sy')
        def _(e, tt=tt, tke=tke):
            B.wait(e, 'sy', tke)
            B.inc(e.dma_start(st.dout[128 * tt:128 * (tt + 1), :], ost[tt % 2]), 'dO', 16)
        tk_out[tt % 2] = B.tok('dO')

    @B.on('sy')
    def _(e):
        e.wait_ge(B.S['dO'], B.c['dO'])



_BUILD_CACHE = {}


def kernel(**inputs):
    x = np.ascontiguousarray(np.asarray(inputs['L0_attn_x'], np.float32))
    shared = host_prepare({k: v for k, v in inputs.items() if k != 'L0_attn_x'})
    B = build_program(upto=99)
    in_maps = []
    for c in range(NCORE):
        m = {'x': np.ascontiguousarray(x[c])}
        m.update(shared)
        in_maps.append(m)
    res = run_bass_kernel_spmd(B.nc, in_maps, core_ids=list(range(NCORE)))
    out = np.stack([np.asarray(r['out'], np.float32) for r in res.results], axis=0)
    return out
